# version 10
# baseline (speedup 1.0000x reference)
import numpy as np
import ml_dtypes
from collections import deque
from contextlib import ExitStack

import concourse.bass as bass
import concourse.mybir as mybir
from concourse.bass_utils import run_bass_kernel_spmd

F32 = mybir.dt.float32
BF16 = mybir.dt.bfloat16
AF = mybir.ActivationFunctionType
OP = mybir.AluOpType
BF = ml_dtypes.bfloat16

D = 2048
NCI = 16
TT = 512
NTILE = 8
C = 128
NCH = TT // C
HV = 16
HK = 8
CONVD = 4096
INTER = 5632
N_QKV = 32
N_Z = 16
N_WO = 16
N_GU = 44
N_DN = 16
MW = 516
MD = 4
LNEPS = 1e-5
RMSEPS = 1e-6
L2EPS = 1e-6

ENGINES = ("pe", "dve", "act", "pool", "sync")

_DBG = ()
_SCAN_DEPTH = 5
_YS = set(range(40))
_MAXTILE = NTILE
_PARANOID = True
_DBG_TILE = 0


class Em:

    def __init__(self):
        self.streams = {e: [] for e in ENGINES}
        self.cnt = {}
        self.war = {}

    def sem(self, *names):
        for n in names:
            self.cnt[n] = 0

    def op(self, eng, fn, waits=(), inc=None, inc_n=1, reads=(), writes=()):
        ws = list(waits)
        for key in writes:
            rec = self.war.get(key)
            if rec:
                ws.extend(rec["r"])
                if rec["w"] is not None:
                    ws.append(rec["w"])
        for key in reads:
            rec = self.war.get(key)
            if rec and rec["w"] is not None:
                ws.append(rec["w"])
        ws2 = []
        for w in ws:
            if w is None:
                continue
            s, v = w
            assert v <= self.cnt[s], f"wait on future token {s}={v} (planned {self.cnt[s]})"
            ws2.append((s, v))
        tok = None
        if inc is not None:
            self.cnt[inc] += inc_n
            tok = (inc, self.cnt[inc])
        assert tok is not None or (not reads and not writes), "hazard tracking needs a token"
        for key in writes:
            self.war[key] = {"w": tok, "r": []}
        for key in reads:
            self.war.setdefault(key, {"w": None, "r": []})["r"].append(tok)
        self.streams[eng].append((tuple(ws2), fn, inc, inc_n))
        return tok

    def rd(self, key, tok):
        self.war.setdefault(key, {"w": None, "r": []})["r"].append(tok)

    def emit(self, block, handles):
        sections = {"pe": block.tensor, "dve": block.vector, "act": block.scalar,
                    "pool": block.gpsimd, "sync": block.sync}
        ownsem = {"pe": "spe", "dve": "sdve", "act": "sact", "pool": "spool"}
        for ename in ENGINES:
            stream = self.streams[ename]
            own = ownsem.get(ename)

            def make(stream, own):
                def body(eng):
                    guard = {}
                    own_cnt = 0
                    for ws, fn, inc, inc_n in stream:
                        for s, v in ws:
                            if s == own and v <= own_cnt and not _PARANOID:
                                continue
                            if guard.get(s, 0) < v:
                                eng.wait_ge(handles[s], v)
                                guard[s] = v
                        if fn is not None:
                            ins = fn(eng)
                            if inc is not None:
                                if inc_n == 1 and ename in ("dve", "act", "pool"):
                                    eng.drain().then_inc(handles[inc], inc_n)
                                else:
                                    ins.then_inc(handles[inc], inc_n)
                        elif inc is not None:
                            eng.sem_inc(handles[inc], inc_n)
                        if inc == own:
                            own_cnt += inc_n
                return body
            sections[ename](make(stream, own))


class Banks:
    def __init__(self, aps):
        self.recs = [{"ap": ap, "rtok": None} for ap in aps]
        self.i = 0

    def get(self):
        r = self.recs[self.i % len(self.recs)]
        self.i += 1
        return r


DBG_SHAPES = {
    "ln1": ([128, NCI * TT], "f32"),
    "mixed": ([128, N_QKV * TT], "f32"),
    "qn": ([128, NCI * TT], "f32"),
    "gclb": ([16, 3 * TT], "f32"),
    "cs0": ([128, 16], "f32"),
    "core": ([128, NCI * TT], "f32"),
    "attn": ([128, NCI * TT], "f32"),
    "x2": ([128, NCI * TT], "f32"),
    "ln3": ([128, NCI * TT], "f32"),
    "hbuf": ([128, 8 * TT], "f32"),
    "zgw": ([128, NCI * TT], "bf16"),
    "sEs": ([128, 16 * 128], "bf16"),
    "sEi": ([128, 16 * 128], "bf16"),
    "sG": ([128, 16 * 128], "bf16"),
    "sY0": ([128, 16 * 128], "bf16"),
    "sU": ([128, 16 * 128], "bf16"),
    "sO": ([128, 16 * 128], "f32"),
    "svar": ([128, 4], "f32"),
    "sS1": ([128, 16 * 128], "f32"),
    "sKtm": ([128, 16 * 128], "bf16"),
    "sVtm": ([128, 16 * 128], "bf16"),
    "sct": ([128, 8 * 16], "f32"),
}


def build_program(dbg=()):
    nc = bass.Bass("TRN2", target_bir_lowering=False)

    xin = [nc.dram_tensor(f"x{b}", [D, 2048], F32, kind="ExternalInput") for b in range(2)]
    yout = [nc.dram_tensor(f"y{b}", [D, 2048], F32, kind="ExternalOutput") for b in range(2)]

    wqkvT = nc.dram_tensor("wqkvT", [D, CONVD], F32, kind="ExternalInput")
    wzT = nc.dram_tensor("wzT", [D, D], F32, kind="ExternalInput")
    wabT = nc.dram_tensor("wabT", [D, 32], F32, kind="ExternalInput")
    woutT = nc.dram_tensor("woutT", [D, D], F32, kind="ExternalInput")
    wgT = nc.dram_tensor("wgT", [D, INTER], F32, kind="ExternalInput")
    wuT = nc.dram_tensor("wuT", [D, INTER], F32, kind="ExternalInput")
    wdT = nc.dram_tensor("wdT", [INTER, D], F32, kind="ExternalInput")

    smalls_in = {}
    for nm, shp in [("bqkv_s", [128, 32]), ("bz_s", [128, 16]), ("bout_s", [128, 16]),
                    ("convw_s", [128, 128]), ("convb_s", [128, 32]),
                    ("gs_s", [128, 44]), ("us_s", [128, 44]), ("ds_s", [128, 16]),
                    ("normw_s", [128, 1]), ("ba_s", [16, 1]), ("bb_s", [16, 1]), ("dtb_s", [16, 1]),
                    ("alog_s", [16, 1])]:
        smalls_in[nm] = nc.dram_tensor(nm, shp, F32, kind="ExternalInput")
    consts_f = {}
    for nm, shp in [("identf", [128, 128]), ("maskUf", [128, 128]),
                    ("ones16f", [16, 128]), ("ident16f", [16, 16]),
                    ("onesrow_f", [1, 128]), ("onescol_f", [128, 1]),
                    ("cvec", [128, 8])]:
        consts_f[nm] = nc.dram_tensor(nm, shp, F32, kind="ExternalInput")
    consts_b = {}
    for nm, shp in [("identb", [128, 128]), ("maskSnb", [128, 128]),
                    ("maskIb", [128, 128]), ("onescol_b", [128, 1])]:
        consts_b[nm] = nc.dram_tensor(nm, shp, BF16, kind="ExternalInput")

    sc_qkv = nc.dram_tensor("sc_qkv", [N_QKV * 128, D], BF16)
    sc_z = nc.dram_tensor("sc_z", [N_Z * 128, D], BF16)
    sc_wo = nc.dram_tensor("sc_wo", [N_WO * 128, D], BF16)
    sc_g = nc.dram_tensor("sc_g", [N_GU * 128, D], BF16)
    sc_u = nc.dram_tensor("sc_u", [N_GU * 128, D], BF16)
    sc_d = nc.dram_tensor("sc_d", [N_DN * 128, INTER], BF16)

    build_program._outs = {"y0": yout[0], "y1": yout[1]}
    dbg_t = {nm: nc.dram_tensor("dbg_" + nm, DBG_SHAPES[nm][0],
                              F32 if DBG_SHAPES[nm][1] == "f32" else BF16,
                              kind="ExternalOutput")
             for nm in dbg}
    build_program._dbg = dbg_t

    em = Em()
    em.sem("ssm", "spl0", "spl1", "spd0", "spd1", "swd0", "swd1",
           "sxd", "sod", "sdbg", "spe", "sdve", "sact", "spool")

    with ExitStack() as stk:
        e = stk.enter_context
        xbuf = e(nc.sbuf_tensor("xbuf", [128, NCI * TT], F32))
        lnbuf = e(nc.sbuf_tensor("lnbuf", [128, NCI * TT], BF16))
        mixbuf = e(nc.sbuf_tensor("mixbuf", [128, N_QKV * MW], BF16))
        qkvbuf = e(nc.sbuf_tensor("qkvbuf", [128, N_QKV * TT], BF16))
        zgwbuf = e(nc.sbuf_tensor("zgwbuf", [128, NCI * TT], BF16))
        state = e(nc.sbuf_tensor("state", [128, HV * 128], F32))
        statebf = e(nc.sbuf_tensor("statebf", [128, HV * 128], BF16))
        WHALF = 4096
        wstream = e(nc.sbuf_tensor("wstream", [128, 2 * WHALF], BF16))
        sqtmp = e(nc.sbuf_tensor("sqtmp", [128, 2 * TT], BF16))
        tmpA = e(nc.sbuf_tensor("tmpA", [128, TT], F32))
        convacc = e(nc.sbuf_tensor("convacc", [128, TT], F32))
        rows = e(nc.sbuf_tensor("rows", [1, 1536], F32))
        tailbuf = e(nc.sbuf_tensor("tailbuf", [128, N_QKV * 3], BF16))
        zero128 = e(nc.sbuf_tensor("zero128", [128, 128], BF16))
        gfm = e(nc.sbuf_tensor("gfm", [16, TT], F32))
        lbfm = e(nc.sbuf_tensor("lbfm", [16, TT], F32))
        befm = e(nc.sbuf_tensor("befm", [16, TT], F32))
        abw = e(nc.sbuf_tensor("abw", [128, NCI * 32], BF16))
        NIP = 3
        scnames = ["G", "P2t", "P4t", "P8t", "RW0",
                   "RW1", "Y0", "Y1", "Khat", "Mb"]
        scn = {nm: e(nc.sbuf_tensor("sc_" + nm, [128, NIP * 256], BF16)) for nm in scnames}
        scn["E"] = scn["Y1"]
        scn["Vtm"] = scn["RW0"]
        scf = {nm: e(nc.sbuf_tensor("sf_" + nm, [128, NIP * 256], F32))
               for nm in ["TO", "O"]}
        scf["CN"] = scf["TO"]
        scf["B0"] = e(nc.sbuf_tensor("sf_B0", [128, NIP * 128], F32))
        scp = {nm: e(nc.sbuf_tensor("sp_" + nm, [128, NIP * 128],
                                    F32 if nm != "Ktm" else BF16))
               for nm in ["SkkMf", "SkqMf", "Ktm"]}

        def sfsl2(nm, ipp):
            return scp[nm][:, ipp * 128:(ipp + 1) * 128]
        sfsl2b = sfsl2
        ctile = e(nc.sbuf_tensor("ctile", [128, 2 * 8 * 16], F32))
        cs127 = e(nc.sbuf_tensor("cs127", [16, 2], F32))
        bc16 = e(nc.sbuf_tensor("bc16", [16, 2], F32))
        diagbc = e(nc.sbuf_tensor("diagbc", [16, 2 * 16], F32))
        varcol = e(nc.sbuf_tensor("varcol", [128, 2 * NIP], F32))
        bcb = e(nc.sbuf_tensor("bcb", [128, 2 * NIP], F32))
        rstdcol = e(nc.sbuf_tensor("rstdcol", [128, 2 * NIP], F32))
        small16 = e(nc.sbuf_tensor("small16", [16, 4], F32))
        smalls = {nm: e(nc.sbuf_tensor("sb_" + nm, list(t.shape), F32))
                  for nm, t in smalls_in.items()}
        cf = {nm: e(nc.sbuf_tensor("sb_" + nm, list(t.shape), F32))
              for nm, t in consts_f.items()}
        cb = {nm: e(nc.sbuf_tensor("sb_" + nm, list(t.shape), BF16))
              for nm, t in consts_b.items()}

        pb = [e(nc.psum_tensor(f"pb{i}", [128, 512], F32)) for i in range(6)]
        pxb = [e(nc.psum_tensor(f"pxb{i}", [128, 1024], BF16)) for i in range(2)]

        sems = {n: e(nc.semaphore(n)) for n in
                ["ssm", "spl0", "spl1", "spd0", "spd1", "swd0", "swd1",
                 "sxd", "sod", "sdbg", "spe", "sdve", "sact", "spool"]}
        block = e(nc.Block())

        fb = Banks([t[:, 0:512] for t in pb])
        bb = Banks([t[:, 0:256] for t in pxb])

        def mm(out, lhsT, rhs, start=True, stop=True):
            return lambda t: t.matmul(out, lhsT, rhs, start=start, stop=stop)

        def trans(out, in_, ident):
            return lambda t: t.transpose(out, in_, ident)

        def act(out, in_, func, bias=None, scale=None, accum=None):
            kw = {}
            if bias is not None:
                kw["bias"] = bias
            if scale is not None:
                kw["scale"] = scale
            if accum is not None:
                kw["accum_out"] = accum
            return lambda s: s.activation(out, in_, func, **kw)

        def tt(out, a, b, op):
            return lambda v: v.tensor_tensor(out, a, b, op=op)

        def ts(out, a, s1, op0, s2=None, op1=None):
            if op1 is None:
                return lambda v: v.tensor_scalar(out, a, scalar1=s1, scalar2=None, op0=op0)
            return lambda v: v.tensor_scalar(out, a, scalar1=s1, scalar2=s2, op0=op0, op1=op1)

        def stt(out, in0, s, in1, op0, op1):
            return lambda v: v.scalar_tensor_tensor(out, in0, s, in1, op0=op0, op1=op1)

        def cp(out, in_):
            return lambda v: v.tensor_copy(out, in_)

        def dma(out, in_):
            return lambda q: q.dma_start(out, in_)

        def xsl(ci):
            return xbuf[:, ci * TT:(ci + 1) * TT]

        def lnsl(ci, c0=0, w=TT):
            return lnbuf[:, ci * TT + c0:ci * TT + c0 + w]

        def msl(co, off, w):
            return mixbuf[:, co * MW + off:co * MW + off + w]

        def mdata(co, c0=0, w=TT):
            return msl(co, MD + c0, w)

        def qsl(co, c0=0, w=TT):
            return qkvbuf[:, co * TT + c0:co * TT + c0 + w]

        def zsl(ci, c0=0, w=TT):
            return zgwbuf[:, ci * TT + c0:ci * TT + c0 + w]

        def ssl(h):
            return state[:, h * 128:(h + 1) * 128]

        def sbsl(h):
            return statebf[:, h * 128:(h + 1) * 128]

        def scsl(nm, par):
            return scn[nm][:, par * 128:(par + 1) * 128]

        def sfsl(nm, par):
            return scf[nm][:, par * 128:(par + 1) * 128]

        def ct(par, slot):
            base = par * 8 * 16 + slot * 16
            return ctile[:, base:base + 16]

        K_ = lambda *a: tuple(a)

        def pxsl(bk, s):
            return bk["ap"][:, s * 128:(s + 1) * 128]

        for nm, t in smalls_in.items():
            em.op("sync", dma(smalls[nm][:], t.ap()), inc="ssm", inc_n=16)
        for nm, t in consts_f.items():
            em.op("sync", dma(cf[nm][:], t.ap()), inc="ssm", inc_n=16)
        for nm, t in consts_b.items():
            em.op("sync", dma(cb[nm][:], t.ap()), inc="ssm", inc_n=16)
        tok_smalls = ("ssm", em.cnt["ssm"])

        em.op("dve", lambda v: v.memset(zero128[:], 0.0), inc="sdve")
        t_e = em.op("act", act(small16[:, 0:1], smalls["alog_s"][:, 0:1], AF.Exp),
                    waits=[tok_smalls], inc="sact")
        em.op("dve", ts(small16[:, 0:1], small16[:, 0:1], -1.0, OP.mult),
              waits=[t_e], inc="sdve")
        em.op("dve", tt(small16[:, 1:2], smalls["ba_s"][:, 0:1],
                        smalls["dtb_s"][:, 0:1], OP.add),
              waits=[tok_smalls], inc="sdve")
        em.op("dve", ts(small16[:, 2:3], smalls["bb_s"][:, 0:1], -1.0, OP.mult),
              waits=[tok_smalls], inc="sdve")
        tok_small_done = em.op("dve", lambda v: v.memset(tailbuf[:], 0.0), inc="sdve")
        negea = small16[:, 0:1]
        dtbeff = small16[:, 1:2]
        negbb = small16[:, 2:3]

        pre_tok = {}
        pre_cast = [tok_small_done, tok_small_done]
        pre_store = [None, None]

        def prelude_weight(name, wt, sc, n_co, IN, kctr=[0]):
            pieces = [(i * 1024, min(1024, IN - i * 1024)) for i in range((IN + 1023) // 1024)]
            for co in range(n_co):
                for (r0, rl) in pieces:
                    par = kctr[0] % 2
                    kctr[0] += 1
                    pin = state[:, par * 1024: par * 1024 + rl]
                    pout = statebf[:, par * 1024: par * 1024 + rl]
                    src = wt.ap()[r0:r0 + rl, co * 128:(co + 1) * 128] \
                        .rearrange("(c p) q -> p c q", p=128)
                    t_ld = em.op("sync", dma(pin.rearrange("p (c q) -> p c q", q=128), src),
                                 waits=[pre_cast[par]], inc="spl%d" % par, inc_n=16)
                    t_c = em.op("act", act(pout, pin, AF.Copy),
                                waits=[t_ld, pre_store[par]], inc="sact")
                    pre_cast[par] = t_c
                    t_cd = em.op("act", lambda s: s.drain(), inc="sact")
                    t_st = em.op("pool", dma(sc.ap()[co * 128:(co + 1) * 128, r0:r0 + rl], pout),
                                 waits=[t_cd], inc="spd%d" % par, inc_n=16)
                    pre_store[par] = t_st
            pre_tok[name] = em.op("pool", None,
                                  waits=[("spd0", em.cnt["spd0"]),
                                         ("spd1", em.cnt["spd1"])],
                                  inc="spool")


        whalf_tok = [None, None]
        wload_tok = [None, None]
        wload_i = [0]
        tok_state_cast = [tok_small_done] * HV
        tok_state_new = [tok_small_done] * HV

        def wload(srcs, waits_extra=()):
            par = wload_i[0] % 2
            wload_i[0] += 1
            w = [whalf_tok[par]] + list(waits_extra)
            tok = None
            for (sc_ap, off, width) in srcs:
                tok = em.op("sync",
                            dma(wstream[:, par * WHALF + off: par * WHALF + off + width], sc_ap),
                            waits=w, inc="swd%d" % par, inc_n=16)
                w = []
            wload_tok[par] = tok
            return par

        def wsl(par, off, w):
            return wstream[:, par * WHALF + off: par * WHALF + off + w]

        def pe_drained(fn, waits=(), reads=()):
            em.op("pe", fn, waits=waits)
            tok = em.op("pe", lambda t: t.drain(), inc="spe")
            for k in reads:
                em.rd(k, tok)
            return tok

        def emit_ln(kind, get, key_of, tile_toks, out_writer):
            bS = fb.get()
            bQ = fb.get()
            onesc = cf["onescol_f"][:] if kind == "f32" else cb["onescol_b"][:]
            grp_base = None
            for ci in range(NCI):
                w0 = [bS["rtok"], tile_toks[ci]] if ci == 0 else [tile_toks[ci]]
                em.op("pe", mm(bS["ap"][0:1, :], onesc, get(ci),
                               start=(ci == 0), stop=(ci == NCI - 1)),
                      waits=w0, inc="spe" if ci == NCI - 1 else None)
            tokS = ("spe", em.cnt["spe"])
            for ci in range(NCI):
                em.rd(key_of(ci), tokS)
            for ci in range(NCI):
                par = ci % 2
                sqv = sqtmp[:, par * TT:(par + 1) * TT]
                t2 = em.op("act", act(sqv, get(ci), AF.Square),
                           waits=[tile_toks[ci]], inc="sact",
                           reads=[key_of(ci)], writes=[K_("sq", par)])
                w1 = [bQ["rtok"], t2] if ci == 0 else [t2]
                t3 = em.op("pe", mm(bQ["ap"][0:1, :], cb["onescol_b"][:], sqv,
                                    start=(ci == 0), stop=(ci == NCI - 1)),
                           waits=w1, inc="spe", reads=[K_("sq", par)])
            tokQ = ("spe", em.cnt["spe"])
            m_r, v_r, w_r = rows[0:1, 0:512], rows[0:1, 512:1024], rows[0:1, 1024:1536]
            t4 = em.op("dve", ts(m_r, bS["ap"][0:1, :], 1.0 / D, OP.mult),
                       waits=[tokS], inc="sdve", writes=[K_("rows", 0)])
            bS["rtok"] = t4
            t5 = em.op("dve", ts(v_r, bQ["ap"][0:1, :], 1.0 / D, OP.mult),
                       waits=[tokQ], inc="sdve", writes=[K_("rows", 1)])
            bQ["rtok"] = t5
            t6 = em.op("dve", tt(w_r, m_r, m_r, OP.mult), inc="sdve",
                       reads=[K_("rows", 0)], writes=[K_("rows", 2)])
            t7 = em.op("dve", tt(v_r, v_r, w_r, OP.subtract), inc="sdve",
                       reads=[K_("rows", 2)], writes=[K_("rows", 1)])
            t8 = em.op("act", act(w_r, v_r, AF.Ln, bias=cf["cvec"][0:1, 0:1]), waits=[t7], inc="sact",
                       reads=[K_("rows", 1)], writes=[K_("rows", 2)])
            t9 = em.op("act", act(v_r, w_r, AF.Exp, scale=-0.5), inc="sact",
                       reads=[K_("rows", 2)], writes=[K_("rows", 1)])
            t10 = em.op("dve", tt(w_r, m_r, v_r, OP.mult), waits=[t9], inc="sdve",
                        reads=[K_("rows", 0), K_("rows", 1)], writes=[K_("rows", 2)])
            bR = fb.get()
            bM = fb.get()
            tR = em.op("pe", mm(bR["ap"], cf["onesrow_f"][:], v_r),
                       waits=[bR["rtok"], t9], inc="spe", reads=[K_("rows", 1)])
            tM = em.op("pe", mm(bM["ap"], cf["onesrow_f"][:], w_r),
                       waits=[bM["rtok"], t10], inc="spe", reads=[K_("rows", 2)])
            out_toks = [out_writer(ci, bR["ap"], bM["ap"], tR, tM) for ci in range(NCI)]
            bR["rtok"] = out_toks[-1]
            bM["rtok"] = out_toks[-1]
            return out_toks

        def dump(name, src_ap, waits, keys=(), dst=None):
            if name in dbg and tile_now[0] == _DBG_TILE:
                t_d1 = em.op("dve", lambda v: v.drain(), waits=waits, inc="sdve")
                t_d2 = em.op("act", lambda s: s.drain(), waits=waits, inc="sact")
                em.op("pool", dma(dst if dst is not None else dbg_t[name].ap(), src_ap),
                      waits=[t_d1, t_d2], inc="sdbg", inc_n=16)
                t_f = em.op("pool", None, waits=[("sdbg", em.cnt["sdbg"])], inc="spool")
                for k in keys:
                    em.rd(k, t_f)

        tile_now = [0]

        for ci in range(NCI):
            src0 = xin[0].ap()[ci * 128:(ci + 1) * 128, 0:TT]
            em.op("act", dma(xsl(ci), src0), inc="sxd", inc_n=16,
                  writes=[K_("x", ci)])
        t_xf0 = em.op("act", None, waits=[("sxd", em.cnt["sxd"]), tok_smalls],
                      inc="sact")
        for ci in range(NCI):
            em.war[K_("x", ci)]["w"] = t_xf0
        tok_xload0 = [t_xf0] * NCI

        def ln1_writer0(ci, apR, apM, tR, tM, _toks=tok_xload0):
            em.op("dve", tt(tmpA[:], xsl(ci), apR, OP.mult),
                  waits=[tR, _toks[ci]], inc="sdve",
                  reads=[K_("x", ci)], writes=[K_("tmpA",)])
            return em.op("dve", tt(lnsl(ci), tmpA[:], apM, OP.subtract),
                         waits=[tM], inc="sdve",
                         reads=[K_("tmpA",)], writes=[K_("ln", ci)])
        tok_ln1_t0 = emit_ln("f32", xsl, lambda ci: K_("x", ci), tok_xload0,
                             ln1_writer0)

        prelude_weight("qkv", wqkvT, sc_qkv, N_QKV, D)
        prelude_weight("z", wzT, sc_z, N_Z, D)
        t_ld = em.op("sync", dma(state[:, 0:512].rearrange("p (c q) -> p c q", q=32),
                                 wabT.ap().rearrange("(c p) q -> p c q", p=128)),
                     waits=[pre_cast[0]], inc="spl0", inc_n=16)
        t_ab = em.op("act", act(abw[:], state[:, 0:512], AF.Copy),
                     waits=[t_ld, pre_store[0]], inc="sact")
        pre_cast[0] = t_ab
        pre_tok["ab"] = t_ab
        tok_prelude_head = em.op("pool", None,
                                 waits=[("spd0", em.cnt["spd0"]),
                                        ("spd1", em.cnt["spd1"])],
                                 inc="spool")

        def prelude_tail_gen():
            pins = [(tmpA[:, 0:512], sqtmp[:, 0:TT], K_("tmpA",), K_("sq", 0)),
                    (convacc[:, 0:512], sqtmp[:, TT:2 * TT], K_("ca0",),
                     K_("sq", 1))]
            lstore = [None, None]
            plan = []
            done_after = {}
            for name, wt, sc, n_co, IN in [("wo", woutT, sc_wo, N_WO, D),
                                           ("g", wgT, sc_g, N_GU, D),
                                           ("u", wuT, sc_u, N_GU, D),
                                           ("d", wdT, sc_d, N_DN, INTER)]:
                for co in range(n_co):
                    for r0 in range(0, IN, 512):
                        plan.append((wt, sc, co, r0))
                done_after[len(plan) - 1] = name
            for i, (wt, sc, co, r0) in enumerate(plan):
                par = i % 2
                pin, bout, kpin, kbout = pins[par]
                src = wt.ap()[r0:r0 + 512, co * 128:(co + 1) * 128] \
                    .rearrange("(c p) q -> p c q", p=128)
                t_ld2 = em.op("sync", dma(pin.rearrange("p (c q) -> p c q", q=128), src),
                              inc="spl%d" % par, inc_n=16, writes=[kpin])
                if par == 0:
                    t_c2 = em.op("act", act(bout, pin, AF.Copy),
                                 waits=[t_ld2, lstore[par]], inc="sact",
                                 reads=[kpin], writes=[kbout])
                    st_eng = "act"
                else:
                    t_c2 = em.op("dve", cp(bout, pin),
                                 waits=[t_ld2, lstore[par]], inc="sdve",
                                 reads=[kpin], writes=[kbout])
                    st_eng = "sync"
                t_st2 = em.op(st_eng, dma(sc.ap()[co * 128:(co + 1) * 128,
                                                  r0:r0 + 512], bout),
                              waits=[t_c2], inc="spd%d" % par, inc_n=16)
                em.rd(kbout, t_st2)
                lstore[par] = t_st2
                if i in done_after:
                    pre_tok[done_after[i]] = em.op(
                        "pool", None,
                        waits=[("spd0", em.cnt["spd0"]),
                               ("spd1", em.cnt["spd1"])],
                        inc="spool")
                if i % 2 == 1:
                    yield

        tok_g = tok_lb = tok_be = tok_small_done

        for tile in range(min(NTILE, _MAXTILE)):
            tile_now[0] = tile
            b = tile // 4
            sc4 = tile % 4
            tok0 = sc4 * TT
            seq_start = (sc4 == 0)

            if tile == 0:
                tok_ln1 = tok_ln1_t0
            else:
                for ci in range(NCI):
                    src = xin[b].ap()[ci * 128:(ci + 1) * 128, tok0:tok0 + TT]
                    em.op("act", dma(xsl(ci), src), inc="sxd", inc_n=16,
                          writes=[K_("x", ci)])
                t_xf = em.op("act", None, waits=[("sxd", em.cnt["sxd"])], inc="sact")
                for ci in range(NCI):
                    em.war[K_("x", ci)]["w"] = t_xf
                tok_xload = [t_xf] * NCI

                def ln1_writer(ci, apR, apM, tR, tM, _toks=tok_xload):
                    em.op("dve", tt(tmpA[:], xsl(ci), apR, OP.mult),
                          waits=[tR, _toks[ci]], inc="sdve",
                          reads=[K_("x", ci)], writes=[K_("tmpA",)])
                    return em.op("dve", tt(lnsl(ci), tmpA[:], apM, OP.subtract),
                                 waits=[tM], inc="sdve",
                                 reads=[K_("tmpA",)], writes=[K_("ln", ci)])
                tok_ln1 = emit_ln("f32", xsl, lambda ci: K_("x", ci), tok_xload,
                                  ln1_writer)
            dump("ln1", lnbuf[:], [tok_ln1[-1]], keys=[K_("ln", i) for i in range(NCI)])

            tok_conv = []
            for co in range(N_QKV):
                par = wload([(sc_qkv.ap()[co * 128:(co + 1) * 128, :], 0, D)],
                            waits_extra=[pre_tok["qkv"]])
                bk = fb.get()
                for ci in range(NCI):
                    w = [bk["rtok"], wload_tok[par], tok_ln1[ci]] if ci == 0 \
                        else [tok_ln1[ci]]
                    em.op("pe", mm(bk["ap"], wsl(par, ci * 128, 128), lnsl(ci),
                                   start=(ci == 0), stop=(ci == NCI - 1)),
                          waits=w, inc="spe" if ci == NCI - 1 else None)
                tok_mm = ("spe", em.cnt["spe"])
                whalf_tok[par] = tok_mm
                for ci in range(NCI):
                    em.rd(K_("ln", ci), tok_mm)
                t_ev = em.op("act", act(mdata(co), bk["ap"], AF.Identity,
                                        bias=smalls["bqkv_s"][:, co:co + 1]),
                             waits=[tok_mm], inc="sact", writes=[K_("mixd", co)])
                bk["rtok"] = t_ev
                if seq_start:
                    t_h = em.op("pool", lambda v, _co=co: v.memset(msl(_co, 1, 3), 0.0),
                                inc="spool", writes=[K_("mixh", co)])
                else:
                    t_h = em.op("pool", cp(msl(co, 1, 3), tailbuf[:, co * 3:co * 3 + 3]),
                                inc="spool", writes=[K_("mixh", co)], reads=[K_("tail", co)])
                em.op("pool", cp(tailbuf[:, co * 3:co * 3 + 3], msl(co, 513, 3)),
                      waits=[t_ev], inc="spool", reads=[K_("mixd", co)],
                      writes=[K_("tail", co)])
                a0 = convacc[:, 0:TT]
                a1 = tmpA[:, 0:TT]
                em.op("dve", ts(a0, msl(co, 1, TT),
                                smalls["convw_s"][:, co * 4:co * 4 + 1], OP.mult),
                      waits=[t_ev, t_h], inc="sdve",
                      reads=[K_("mixd", co), K_("mixh", co)], writes=[K_("ca0",)])
                em.op("dve", stt(a1, msl(co, 2, TT), smalls["convw_s"][:, co * 4 + 1:co * 4 + 2],
                                 a0, OP.mult, OP.add), inc="sdve",
                      reads=[K_("mixd", co), K_("mixh", co), K_("ca0",)], writes=[K_("tmpA",)])
                em.op("dve", stt(a0, msl(co, 3, TT), smalls["convw_s"][:, co * 4 + 2:co * 4 + 3],
                                 a1, OP.mult, OP.add), inc="sdve",
                      reads=[K_("mixd", co), K_("tmpA",)], writes=[K_("ca0",)])
                t_c4 = em.op("dve", stt(a1, msl(co, 4, TT),
                                        smalls["convw_s"][:, co * 4 + 3:co * 4 + 4],
                                        a0, OP.mult, OP.add), inc="sdve",
                             reads=[K_("mixd", co), K_("ca0",)], writes=[K_("tmpA",)])
                t_s = em.op("act", act(qsl(co), a1, AF.Silu,
                                       bias=smalls["convb_s"][:, co:co + 1]),
                            waits=[t_c4], inc="sact",
                            reads=[K_("tmpA",)], writes=[K_("qkv", co)])
                tok_conv.append(t_s)
            dump("mixed", qkvbuf[:], [tok_conv[-1]], keys=[K_("qkv", i) for i in range(N_QKV)])

            tok_qn = []
            for j in range(16):
                is_q = j < 8
                par = j % 2
                sqv = sqtmp[:, par * TT:(par + 1) * TT]
                t_sq = em.op("act", act(sqv, qsl(j), AF.Square),
                             waits=[tok_conv[j]], inc="sact",
                             reads=[K_("qkv", j)], writes=[K_("sq", par)])
                bk = fb.get()
                t_mm = em.op("pe", mm(bk["ap"][0:1, :], cb["onescol_b"][:], sqv),
                             waits=[bk["rtok"], t_sq], inc="spe", reads=[K_("sq", par)])
                t_l = em.op("act", act(rows[0:1, 0:512], bk["ap"][0:1, :], AF.Ln,
                                       bias=cf["cvec"][0:1, 1:2]),
                            waits=[t_mm], inc="sact", writes=[K_("rows", 0)])
                bk["rtok"] = t_l
                bias = cf["cvec"][0:1, 3:4] if is_q else None
                t_r = em.op("act", act(rows[0:1, 512:1024], rows[0:1, 0:512], AF.Exp,
                                       scale=-0.5, bias=bias), inc="sact",
                            reads=[K_("rows", 0)], writes=[K_("rows", 1)])
                bk2 = fb.get()
                t_b = em.op("pe", mm(bk2["ap"], cf["onesrow_f"][:], rows[0:1, 512:1024]),
                            waits=[bk2["rtok"], t_r], inc="spe", reads=[K_("rows", 1)])
                t_ap = em.op("dve", tt(mdata(j), qsl(j), bk2["ap"], OP.mult),
                             waits=[t_b], inc="sdve",
                             reads=[K_("qkv", j)], writes=[K_("mixd", j)])
                bk2["rtok"] = t_ap
                tok_qn.append(t_ap)
            if "qn" in dbg and tile == _DBG_TILE:
                for j in range(16):
                    dump("qn", mdata(j), [tok_qn[j]], keys=[K_("mixd", j)],
                         dst=dbg_t["qn"].ap()[:, j * TT:(j + 1) * TT])

            tok_zgw = []
            for co in range(N_Z):
                par = wload([(sc_z.ap()[co * 128:(co + 1) * 128, :], 0, D)],
                            waits_extra=[pre_tok["z"]])
                bk = fb.get()
                for ci in range(NCI):
                    w = [bk["rtok"], wload_tok[par], tok_ln1[ci]] if ci == 0 else [tok_ln1[ci]]
                    em.op("pe", mm(bk["ap"], wsl(par, ci * 128, 128), lnsl(ci),
                                   start=(ci == 0), stop=(ci == NCI - 1)),
                          waits=w, inc="spe" if ci == NCI - 1 else None)
                tok_mm = ("spe", em.cnt["spe"])
                whalf_tok[par] = tok_mm
                for ci in range(NCI):
                    em.rd(K_("ln", ci), tok_mm)
                t_s = em.op("act", act(tmpA[:], bk["ap"], AF.Silu,
                                       bias=smalls["bz_s"][:, co:co + 1]),
                            waits=[tok_mm], inc="sact", writes=[K_("tmpA",)])
                bk["rtok"] = t_s
                t_g_ = em.op("dve", ts(zsl(co), tmpA[:], smalls["normw_s"][:, 0:1], OP.mult),
                             waits=[t_s], inc="sdve",
                             reads=[K_("tmpA",)], writes=[K_("zg", co)])
                tok_zgw.append(t_g_)

            if "zgw" in dbg and tile == _DBG_TILE:
                dump("zgw", zgwbuf[:], [tok_zgw[-1]], keys=[K_("zg", i) for i in range(NCI)])
            bkA2 = fb.get()
            for ci in range(NCI):
                w = [bkA2["rtok"], pre_tok["ab"], tok_ln1[ci]] if ci == 0 else [tok_ln1[ci]]
                em.op("pe", mm(bkA2["ap"][0:16, :], abw[:, ci * 32:ci * 32 + 16], lnsl(ci),
                               start=(ci == 0), stop=(ci == NCI - 1)),
                      waits=w, inc="spe" if ci == NCI - 1 else None)
            tok_a2 = ("spe", em.cnt["spe"])
            bkB2 = fb.get()
            for ci in range(NCI):
                w = [bkB2["rtok"]] if ci == 0 else []
                em.op("pe", mm(bkB2["ap"][0:16, :], abw[:, ci * 32 + 16:ci * 32 + 32], lnsl(ci),
                               start=(ci == 0), stop=(ci == NCI - 1)),
                      waits=w, inc="spe" if ci == NCI - 1 else None)
            tok_ab = ("spe", em.cnt["spe"])
            for ci in range(NCI):
                em.rd(K_("ln", ci), tok_ab)
            psa = bkA2["ap"][0:16, :]
            psb = bkB2["ap"][0:16, :]
            sA = tmpA[0:16, :]
            sB = convacc[0:16, 0:TT]
            em.op("act", act(sA, psa, AF.Abs, bias=dtbeff),
                  waits=[tok_a2], inc="sact", writes=[K_("tmpA",)])
            em.op("act", act(sB, sA, AF.Exp, scale=-1.0), inc="sact",
                  reads=[K_("tmpA",)], writes=[K_("ca0",)])
            em.op("act", act(sA, sB, AF.Ln, bias=cf["cvec"][0:16, 2:3]), inc="sact",
                  reads=[K_("ca0",)], writes=[K_("tmpA",)])
            t_g1 = em.op("act", act(sB, psa, AF.Relu, bias=dtbeff), inc="sact",
                         writes=[K_("ca0",)])
            t_g2 = em.op("dve", tt(gfm[:], sA, sB, OP.add),
                         waits=[t_g1], inc="sdve",
                         reads=[K_("tmpA",), K_("ca0",)], writes=[K_("gfm",)])
            tok_g = em.op("dve", ts(gfm[:], gfm[:], negea, OP.mult), inc="sdve",
                          writes=[K_("gfm",)])
            bkA2["rtok"] = t_g1
            em.op("act", act(sA, psb, AF.Abs, bias=smalls["bb_s"][:, 0:1]),
                  waits=[tok_ab, t_g2], inc="sact", writes=[K_("tmpA",)])
            em.op("act", act(sB, sA, AF.Exp, scale=-1.0), inc="sact",
                  reads=[K_("tmpA",)], writes=[K_("ca0",)])
            em.op("act", act(sA, sB, AF.Ln, bias=cf["cvec"][0:16, 2:3]), inc="sact",
                  reads=[K_("ca0",)], writes=[K_("tmpA",)])
            t_b1 = em.op("act", act(sB, psb, AF.Relu, scale=-1.0, bias=negbb),
                         inc="sact", writes=[K_("ca0",)])
            t_b2 = em.op("dve", tt(lbfm[:], sA, sB, OP.add),
                         waits=[t_b1], inc="sdve",
                         reads=[K_("tmpA",), K_("ca0",)], writes=[K_("lbfm",)])
            tok_lb = em.op("dve", ts(lbfm[:], lbfm[:], -1.0, OP.mult), inc="sdve",
                           writes=[K_("lbfm",)])
            tok_be = em.op("act", act(befm[:], psb, AF.Sigmoid,
                                      bias=smalls["bb_s"][:, 0:1]),
                           waits=[tok_ab], inc="sact", writes=[K_("befm",)])
            bkB2["rtok"] = tok_be
            if "gclb" in dbg and tile == _DBG_TILE:
                dump("gclb", gfm[:], [tok_g], keys=[K_("gfm",)], dst=dbg_t["gclb"].ap()[:, 0:TT])
                dump("gclb", lbfm[:], [tok_lb], keys=[K_("lbfm",)], dst=dbg_t["gclb"].ap()[:, TT:2 * TT])
                dump("gclb", befm[:], [tok_be], keys=[K_("befm",)], dst=dbg_t["gclb"].ap()[:, 2 * TT:3 * TT])

            if seq_start:
                w_pre = [tok_prelude_head, pre_cast[0], pre_cast[1]] \
                    if tile == 0 else []
                t_ms1 = em.op("dve", lambda v: v.memset(state[:], 0.0),
                              waits=w_pre + [tok_state_cast[HV - 1]] + tok_state_new,
                              inc="sdve", writes=[K_("state",)])
                t_ms2 = em.op("act", act(statebf[:], state[:], AF.Copy),
                              waits=[t_ms1], inc="sact",
                              reads=[K_("state",)], writes=[K_("statebf",)])
                tok_state_cast = [t_ms2] * HV
                tok_state_new = [t_ms1] * HV

            prepres = {}
            pairtok = {}

            def prep_gen(c):
                par = c % 2
                c0 = c * 128
                bkp = fb.get()
                t_tr = pe_drained(trans(bkp["ap"][:, 0:16], gfm[:, c0:c0 + 128],
                                        cf["identf"][0:16, 0:16]),
                                  waits=[bkp["rtok"], tok_g], reads=[K_("gfm",)])
                t_ev = em.op("dve", cp(ct(par, 0), bkp["ap"][:, 0:16]), waits=[t_tr],
                             inc="sdve", writes=[K_("ct", par, 0)])
                bkp["rtok"] = t_ev
                yield
                bkc = fb.get()
                t_cs = pe_drained(mm(bkc["ap"][:, 0:16], cf["maskUf"][:], ct(par, 0)),
                                  waits=[bkc["rtok"], t_ev], reads=[K_("ct", par, 0)])
                t_cse = em.op("dve", cp(ct(par, 1), bkc["ap"][:, 0:16]), waits=[t_cs],
                              inc="sdve", writes=[K_("ct", par, 1)])
                bkc["rtok"] = t_cse
                yield
                bkl = fb.get()
                t_tl = pe_drained(trans(bkl["ap"][:, 0:16], lbfm[:, c0:c0 + 128],
                                        cf["identf"][0:16, 0:16]),
                                  waits=[bkl["rtok"], tok_lb], reads=[K_("lbfm",)])
                t_le = em.op("dve", cp(ct(par, 2), bkl["ap"][:, 0:16]), waits=[t_tl],
                             inc="sdve", writes=[K_("ct", par, 2)])
                bkl["rtok"] = t_le
                t_phi = em.op("dve", tt(ct(par, 3), ct(par, 1), ct(par, 2), OP.add),
                              waits=[t_cse, t_le], inc="sdve",
                              reads=[K_("ct", par, 1), K_("ct", par, 2)],
                              writes=[K_("ct", par, 3)])
                yield
                bkb = fb.get()
                t_tb = pe_drained(trans(bkb["ap"][:, 0:16], befm[:, c0:c0 + 128],
                                        cf["identf"][0:16, 0:16]),
                                  waits=[bkb["rtok"], tok_be], reads=[K_("befm",)])
                t_bev = em.op("dve", cp(ct(par, 4), bkb["ap"][:, 0:16]), waits=[t_tb],
                              inc="sdve", writes=[K_("ct", par, 4)])
                bkb["rtok"] = t_bev
                t_bc = em.op("act", act(ct(par, 5), ct(par, 1), AF.Exp), waits=[t_cse],
                             inc="sact", reads=[K_("ct", par, 1)], writes=[K_("ct", par, 5)])
                t_nb = em.op("dve", ts(ct(par, 6), ct(par, 5), -1.0, OP.mult),
                             waits=[t_bc], inc="sdve",
                             reads=[K_("ct", par, 5)], writes=[K_("ct", par, 6)])
                yield
                bkt = fb.get()
                t_ctr0 = pe_drained(trans(bkt["ap"][0:16, 0:128], ct(par, 1), cf["identf"][:]),
                                    waits=[bkt["rtok"], t_cse], reads=[K_("ct", par, 1)])
                t_c127 = em.op("dve", cp(cs127[:, par:par + 1], bkt["ap"][0:16, 127:128]),
                               waits=[t_ctr0], inc="sdve", writes=[K_("cs127", par)])
                bkt["rtok"] = t_c127
                t_bc16 = em.op("act", act(bc16[:, par:par + 1], cs127[:, par:par + 1], AF.Exp),
                               waits=[t_c127], inc="sact",
                               reads=[K_("cs127", par)], writes=[K_("bc16", par)])
                t_diag = em.op("dve", ts(diagbc[:, par * 16:(par + 1) * 16], cf["ident16f"][:],
                                         bc16[:, par:par + 1], OP.mult),
                               waits=[t_bc16], inc="sdve",
                               reads=[K_("bc16", par)], writes=[K_("diag", par)])
                yield
                bkr = fb.get()
                t_br = pe_drained(mm(bkr["ap"][:, 0:16], cf["ones16f"][:],
                                     diagbc[:, par * 16:(par + 1) * 16]),
                                  waits=[bkr["rtok"], t_diag], reads=[K_("diag", par)])
                t_bre = em.op("dve", cp(ct(par, 7), bkr["ap"][:, 0:16]), waits=[t_br],
                              inc="sdve", writes=[K_("ct", par, 7)])
                bkr["rtok"] = t_bre
                if "sct" in dbg and tile == _DBG_TILE and c == 0:
                    dump("sct", ctile[:, 0:128], [t_bre, t_phi, t_nb, t_bev],
                         keys=[K_("ct", 0, s) for s in range(8)])
                if "cs0" in dbg and tile == _DBG_TILE and c == 0:
                    dump("cs0", ct(par, 1), [t_cse], keys=[K_("ct", par, 1)])
                prepres[c] = dict(par=par, t_phi=t_phi, t_cse=t_cse, t_nb=t_nb,
                                  t_bev=t_bev, t_bc=t_bc, t_bre=t_bre)

            def pair_gen(c, kp):
                while c not in prepres:
                    yield
                gp = c * HK + kp
                while gp >= NIP and (gp - NIP) not in inst_done:
                    yield
                P = prepres[c]
                par = P["par"]
                c0 = c * 128
                t_phi, t_cse, t_nb = P["t_phi"], P["t_cse"], P["t_nb"]
                t_bev, t_bc, t_bre = P["t_bev"], P["t_bc"], P["t_bre"]
                h0 = 2 * kp
                ip = gp % NIP
                hs = [h0, h0 + 1]

                def ccol(slot, hh, _p=par):
                    return ct(_p, slot)[:, hh:hh + 1]

                def pr(nm, sub=None, w=128):
                    base = ip * 256 + (0 if sub is None else sub * 128)
                    return scn[nm][:, base:base + (256 if sub is None else w)]

                def prf(nm, sub=None, w=128):
                    base = ip * 256 + (0 if sub is None else sub * 128)
                    return scf[nm][:, base:base + (256 if sub is None else w)]

                KALIAS = {"E": "Y1", "Vtm": "RW0", "CN": "TO"}
                KP = lambda nm: K_(KALIAS.get(nm, nm), ip)
                Qf = mdata(kp, c0, 128)
                Kf = mdata(8 + kp, c0, 128)
                bk1 = fb.get()
                t1 = em.op("pe", mm(bk1["ap"][:, 0:128], Kf, Kf),
                           waits=[bk1["rtok"], tok_qn[8 + kp]], inc="spe",
                           reads=[K_("mixd", 8 + kp)])
                tS1 = em.op("dve", tt(sfsl2("SkkMf", ip), bk1["ap"][:, 0:128],
                                      cb["maskSnb"][:], OP.mult),
                            waits=[t1], inc="sdve", writes=[K_("SkkM", ip)])
                bk1["rtok"] = tS1
                yield
                bk2 = fb.get()
                t2 = em.op("pe", mm(bk2["ap"][:, 0:128], Kf, Qf),
                           waits=[bk2["rtok"], tok_qn[kp], tok_qn[8 + kp]], inc="spe",
                           reads=[K_("mixd", kp), K_("mixd", 8 + kp)])
                tS2 = em.op("dve", tt(sfsl2("SkqMf", ip), bk2["ap"][:, 0:128],
                                      cb["maskIb"][:], OP.mult),
                            waits=[t2], inc="sdve", writes=[K_("SkqM", ip)])
                bk2["rtok"] = tS2
                yield
                bkT = bb.get()
                t3 = em.op("pe", trans(pxsl(bkT, 0), Kf, cb["identb"][:]),
                           waits=[bkT["rtok"], tok_qn[8 + kp]], inc="spe",
                           reads=[K_("mixd", 8 + kp)])
                tKtm = em.op("act", act(sfsl2b("Ktm", ip), pxsl(bkT, 0), AF.Copy),
                             waits=[t3], inc="sact", writes=[K_("Ktm", ip)])
                bkT["rtok"] = tKtm
                yield
                tG = []
                tMb = []
                tKh = []
                for (cslot, Ename, tcol) in [(3, "Es", t_phi), (1, "Ei", t_cse)]:
                    bkB = fb.get()
                    tb0a = em.op("pool", ts(scf["B0"][:, ip * 128:(ip + 1) * 128],
                                            cf["identf"][:], 0.0, OP.mult,
                                            ccol(cslot, hs[0]), OP.add),
                                 waits=[tcol], inc="spool",
                                 reads=[K_("ct", par, cslot)], writes=[K_("B0", ip)])
                    tt1 = em.op("pe", trans(bkB["ap"][:, 0:128],
                                            scf["B0"][:, ip * 128:(ip + 1) * 128],
                                            cf["identf"][:]),
                                waits=[bkB["rtok"], tb0a], inc="spe", reads=[K_("B0", ip)])
                    tb0b = em.op("pool", ts(scf["B0"][:, ip * 128:(ip + 1) * 128],
                                            cf["identf"][:], 0.0, OP.mult,
                                            ccol(cslot, hs[1]), OP.add),
                                 waits=[tcol], inc="spool",
                                 reads=[K_("ct", par, cslot)], writes=[K_("B0", ip)])
                    tt2 = em.op("pe", trans(bkB["ap"][:, 128:256],
                                            scf["B0"][:, ip * 128:(ip + 1) * 128],
                                            cf["identf"][:]),
                                waits=[tb0b], inc="spe", reads=[K_("B0", ip)])
                    te1 = em.op("dve", ts(prf("TO", 0), bkB["ap"][:, 0:128],
                                          ccol(1, hs[0]), OP.subtract, 0.0, OP.min),
                                waits=[tt2, t_cse], inc="sdve",
                                reads=[K_("ct", par, 1)], writes=[KP("TO")])
                    te2 = em.op("dve", ts(prf("TO", 1), bkB["ap"][:, 128:256],
                                          ccol(1, hs[1]), OP.subtract, 0.0, OP.min),
                                waits=[tt2], inc="sdve",
                                reads=[K_("ct", par, 1)], writes=[KP("TO")])
                    bkB["rtok"] = te2
                    tE = em.op("act", act(pr("E"), prf("TO"), AF.Exp),
                               waits=[te1, te2], inc="sact",
                               reads=[KP("TO")], writes=[KP("E")])
                    yield
                    if Ename == "Es":
                        for s in (0, 1):
                            tG.append(em.op("pool", tt(pr("G", s), sfsl2("SkkMf", ip),
                                                       pr("E", s), OP.mult),
                                            waits=[tS1, tE], inc="spool",
                                            reads=[K_("SkkM", ip), KP("E")],
                                            writes=[KP("G")]))
                    else:
                        tEi = tE
                        for s in (0, 1):
                            tMb.append(em.op("pool", tt(pr("Mb", s), sfsl2("SkqMf", ip),
                                                        pr("E", s), OP.mult),
                                             waits=[tS2, tE], inc="spool",
                                             reads=[K_("SkqM", ip), KP("E")],
                                             writes=[KP("Mb")]))
                            tb = em.op("dve", cp(bcb[:, 2 * ip + s:2 * ip + s + 1],
                                                 pr("E", s)[:, 127:128]),
                                       waits=[tE], inc="sdve",
                                       reads=[KP("E")], writes=[K_("bcb", ip, s)])
                            tKh.append(em.op("pool", ts(pr("Khat", s), sfsl2b("Ktm", ip),
                                                        bcb[:, 2 * ip + s:2 * ip + s + 1],
                                                        OP.mult),
                                             waits=[tKtm, tb], inc="spool",
                                             reads=[K_("Ktm", ip), K_("bcb", ip, s)],
                                             writes=[KP("Khat")]))
                yield
                bkV = bb.get()
                tv1 = em.op("pe", trans(pxsl(bkV, 0), qsl(16 + hs[0], c0, 128),
                                        cb["identb"][:]),
                            waits=[bkV["rtok"], tok_conv[16 + hs[0]]], inc="spe",
                            reads=[K_("qkv", 16 + hs[0])])
                tv2 = em.op("pe", trans(pxsl(bkV, 1), qsl(16 + hs[1], c0, 128),
                                        cb["identb"][:]),
                            waits=[tok_conv[16 + hs[1]]], inc="spe",
                            reads=[K_("qkv", 16 + hs[1])])
                tVe = em.op("act", act(pr("Vtm"), bkV["ap"][:, 0:256], AF.Copy),
                            waits=[tv2], inc="sact", writes=[KP("Vtm")])
                bkV["rtok"] = tVe
                yield
                bkK = fb.get()
                tk1 = em.op("pe", mm(bkK["ap"][:, 0:128], Kf, sbsl(hs[0])),
                            waits=[bkK["rtok"], tok_state_cast[hs[0]], tok_qn[8 + kp]],
                            inc="spe", reads=[K_("mixd", 8 + kp), K_("statebf", hs[0])])
                tk2 = em.op("pe", mm(bkK["ap"][:, 128:256], Kf, sbsl(hs[1])),
                            waits=[tok_state_cast[hs[1]]], inc="spe",
                            reads=[K_("statebf", hs[1])])
                tr1 = em.op("dve", stt(prf("TO", 0), bkK["ap"][:, 0:128], ccol(6, hs[0]),
                                       pr("Vtm", 0), OP.mult, OP.add),
                            waits=[tk2, t_nb, tVe], inc="sdve",
                            reads=[K_("ct", par, 6), KP("Vtm")], writes=[KP("TO")])
                tr2 = em.op("dve", stt(prf("TO", 1), bkK["ap"][:, 128:256], ccol(6, hs[1]),
                                       pr("Vtm", 1), OP.mult, OP.add),
                            inc="sdve", reads=[K_("ct", par, 6), KP("Vtm")],
                            writes=[KP("TO")])
                bkK["rtok"] = tr2
                ty1 = em.op("dve", ts(pr("Y0", 0), prf("TO", 0), ccol(4, hs[0]), OP.mult),
                            waits=[t_bev], inc="sdve",
                            reads=[KP("TO"), K_("ct", par, 4)], writes=[KP("Y0")])
                t_y = em.op("dve", ts(pr("Y0", 1), prf("TO", 1), ccol(4, hs[1]), OP.mult),
                            inc="sdve", reads=[KP("TO"), K_("ct", par, 4)],
                            writes=[KP("Y0")])
                yield
                bkq = fb.get()
                tq1 = em.op("pe", mm(bkq["ap"][:, 0:128], Qf, sbsl(hs[0])),
                            waits=[bkq["rtok"], tok_state_cast[hs[0]], tok_qn[kp]],
                            inc="spe", reads=[K_("mixd", kp), K_("statebf", hs[0])])
                tq2 = em.op("pe", mm(bkq["ap"][:, 128:256], Qf, sbsl(hs[1])),
                            waits=[tok_state_cast[hs[1]]], inc="spe",
                            reads=[K_("statebf", hs[1])])
                tt1_ = em.op("dve", ts(prf("TO", 0), bkq["ap"][:, 0:128], ccol(5, hs[0]),
                                       OP.mult),
                             waits=[tq2, t_bc], inc="sdve",
                             reads=[K_("ct", par, 5)], writes=[KP("TO")])
                tt2_ = em.op("dve", ts(prf("TO", 1), bkq["ap"][:, 128:256], ccol(5, hs[1]),
                                       OP.mult),
                             inc="sdve", reads=[K_("ct", par, 5)], writes=[KP("TO")])
                bkq["rtok"] = tt2_
                yield
                bkA = bb.get()
                ta1 = em.op("pe", trans(pxsl(bkA, 0), pr("G", 0),
                                        cb["identb"][:]),
                            waits=[bkA["rtok"], tG[0]], inc="spe", reads=[KP("G")])
                ta2 = em.op("pe", trans(pxsl(bkA, 1), pr("G", 1), cb["identb"][:]),
                            waits=[tG[1]], inc="spe", reads=[KP("G")])
                tAe = em.op("act", act(pr("RW1"), bkA["ap"][:, 0:256], AF.Copy),
                            waits=[ta2], inc="sact", writes=[KP("RW1")])
                bkA["rtok"] = tAe
                yield
                ycur, ynext = "Y0", "Y1"
                apT0, apT1, apTok, apKey = pr("G", 0), pr("G", 1), tG[1], KP("G")
                rwT0, rwT1, rwTok, rwKey = pr("RW1", 0), pr("RW1", 1), tAe, KP("RW1")
                rot = [("P2t", "RW0"), ("P4t", "P8t")]
                for lv in range(7):
                    bks = fb.get()
                    em.op("pe", mm(bks["ap"][:, 0:128], apT0, pr(ycur, 0),
                                   start=True, stop=False),
                          waits=[bks["rtok"], apTok, t_y], inc="spe",
                          reads=[apKey, KP(ycur)])
                    em.op("pe", mm(bks["ap"][:, 0:128], cb["identb"][:], pr(ycur, 0),
                                   start=False, stop=True), inc="spe", reads=[KP(ycur)])
                    em.op("pe", mm(bks["ap"][:, 128:256], apT1, pr(ycur, 1),
                                   start=True, stop=False), inc="spe",
                          reads=[apKey, KP(ycur)])
                    ts2 = em.op("pe", mm(bks["ap"][:, 128:256], cb["identb"][:],
                                         pr(ycur, 1), start=False, stop=True),
                                inc="spe", reads=[KP(ycur)])
                    eng_y = "dve" if lv % 2 == 0 else "act"
                    t_y = em.op(eng_y,
                                (cp(pr(ynext), bks["ap"][:, 0:256]) if eng_y == "dve"
                                 else act(pr(ynext), bks["ap"][:, 0:256], AF.Copy)),
                                waits=[ts2], inc="s" + eng_y, writes=[KP(ynext)])
                    bks["rtok"] = t_y
                    ycur, ynext = ynext, ycur
                    if lv < 6:
                        ptn, prn = rot[lv % 2]
                        bkp2 = fb.get()
                        em.op("pe", mm(bkp2["ap"][:, 0:128], rwT0, apT0),
                              waits=[bkp2["rtok"], rwTok, apTok], inc="spe",
                              reads=[rwKey, apKey])
                        tb2 = em.op("pe", mm(bkp2["ap"][:, 128:256], rwT1, apT1),
                                    inc="spe", reads=[rwKey, apKey])
                        eng_t = "act" if lv % 2 == 0 else "dve"
                        tev1 = em.op(eng_t,
                                     (act(pr(ptn), bkp2["ap"][:, 0:256], AF.Copy)
                                      if eng_t == "act"
                                      else cp(pr(ptn), bkp2["ap"][:, 0:256])),
                                     waits=[tb2], inc="s" + eng_t, writes=[KP(ptn)])
                        bkp2["rtok"] = tev1
                        if lv < 5:
                            bkp3 = fb.get()
                            em.op("pe", mm(bkp3["ap"][:, 0:128], apT0, rwT0),
                                  waits=[bkp3["rtok"], rwTok, apTok], inc="spe",
                                  reads=[rwKey, apKey])
                            tb4 = em.op("pe", mm(bkp3["ap"][:, 128:256], apT1, rwT1),
                                        inc="spe", reads=[rwKey, apKey])
                            eng_r = "dve" if lv % 2 == 0 else "act"
                            tev2 = em.op(eng_r,
                                         (cp(pr(prn), bkp3["ap"][:, 0:256])
                                          if eng_r == "dve"
                                          else act(pr(prn), bkp3["ap"][:, 0:256],
                                                   AF.Copy)),
                                         waits=[tb4], inc="s" + eng_r,
                                         writes=[KP(prn)])
                            bkp3["rtok"] = tev2
                            rwT0, rwT1, rwTok, rwKey = (pr(prn, 0), pr(prn, 1),
                                                        tev2, KP(prn))
                        apT0, apT1, apTok, apKey = (pr(ptn, 0), pr(ptn, 1),
                                                    tev1, KP(ptn))
                    yield
                Ukey = KP(ycur)
                t_U = t_y
                bkm = fb.get()
                tm1 = em.op("pe", mm(bkm["ap"][:, 0:128], pr("Mb", 0), pr(ycur, 0)),
                            waits=[bkm["rtok"], tMb[0], t_U], inc="spe",
                            reads=[KP("Mb"), Ukey])
                tm2 = em.op("pe", mm(bkm["ap"][:, 128:256], pr("Mb", 1), pr(ycur, 1)),
                            waits=[tMb[1]], inc="spe", reads=[KP("Mb"), Ukey])
                t_O = em.op("dve", tt(prf("O"), prf("TO"), bkm["ap"][:, 0:256], OP.add),
                            waits=[tm2, tt2_], inc="sdve",
                            reads=[KP("TO")], writes=[KP("O")])
                bkm["rtok"] = t_O
                yield
                bksn = fb.get()
                tn1 = em.op("pe", mm(bksn["ap"][:, 0:128], pr("Khat", 0), pr(ycur, 0)),
                            waits=[bksn["rtok"], tKh[0], t_U], inc="spe",
                            reads=[KP("Khat"), Ukey])
                tn2 = em.op("pe", mm(bksn["ap"][:, 128:256], pr("Khat", 1), pr(ycur, 1)),
                            waits=[tKh[1]], inc="spe", reads=[KP("Khat"), Ukey])
                ts1_ = em.op("dve", stt(ssl(hs[0]), ssl(hs[0]), ccol(7, hs[0]),
                                        bksn["ap"][:, 0:128], OP.mult, OP.add),
                             waits=[tn2, t_bre, tok_state_new[hs[0]],
                                    tok_state_cast[hs[0]]],
                             inc="sdve",
                             reads=[K_("ct", par, 7)], writes=[K_("stateh", hs[0])])
                ts2_ = em.op("dve", stt(ssl(hs[1]), ssl(hs[1]), ccol(7, hs[1]),
                                        bksn["ap"][:, 128:256], OP.mult, OP.add),
                             waits=[tok_state_new[hs[1]], tok_state_cast[hs[1]]],
                             inc="sdve",
                             reads=[K_("ct", par, 7)], writes=[K_("stateh", hs[1])])
                bksn["rtok"] = ts2_
                tok_state_new[hs[0]] = ts1_
                tok_state_new[hs[1]] = ts2_
                tcast = em.op("act", act(statebf[:, h0 * 128:(h0 + 2) * 128],
                                         state[:, h0 * 128:(h0 + 2) * 128], AF.Copy),
                              waits=[ts1_, ts2_], inc="sact",
                              reads=[K_("stateh", hs[0]), K_("stateh", hs[1])],
                              writes=[K_("statebf", hs[0]), K_("statebf", hs[1])])
                tok_state_cast[hs[0]] = tcast
                tok_state_cast[hs[1]] = tcast
                yield
                t_osq0 = em.op("pool", tt(prf("TO"), prf("O"), prf("O"), OP.mult),
                               waits=[t_O, tt2_], inc="spool",
                               reads=[KP("O")], writes=[KP("TO")])
                tvr = []
                for s in (0, 1):
                    tvr.append(em.op("dve",
                                     lambda v, _o=prf("TO", s),
                                     _d=varcol[:, 2 * ip + s:2 * ip + s + 1]:
                                     v.reduce_sum(_d, _o, axis=mybir.AxisListType.X),
                                     waits=[t_osq0], inc="sdve",
                                     reads=[KP("TO")], writes=[K_("var", ip, s)]))
                em.op("act", act(rstdcol[:, 2 * ip:2 * ip + 2],
                                 varcol[:, 2 * ip:2 * ip + 2], AF.Ln,
                                 scale=1.0 / 128.0, bias=cf["cvec"][:, 1:2]),
                      waits=tvr, inc="sact",
                      reads=[K_("var", ip, 0), K_("var", ip, 1)],
                      writes=[K_("rstd", ip)])
                t_rs = em.op("act", act(rstdcol[:, 2 * ip:2 * ip + 2],
                                        rstdcol[:, 2 * ip:2 * ip + 2],
                                        AF.Exp, scale=-0.5), inc="sact",
                             writes=[K_("rstd", ip)])
                tc1 = em.op("pool", ts(prf("CN", 0), prf("O", 0),
                                       rstdcol[:, 2 * ip:2 * ip + 1], OP.mult),
                            waits=[t_rs], inc="spool",
                            reads=[KP("O"), K_("rstd", ip)], writes=[KP("CN")])
                tc2 = em.op("pool", ts(prf("CN", 1), prf("O", 1),
                                       rstdcol[:, 2 * ip + 1:2 * ip + 2], OP.mult),
                            inc="spool", reads=[KP("O"), K_("rstd", ip)],
                            writes=[KP("CN")])
                yield
                bkct = fb.get()
                tct1 = em.op("pe", trans(bkct["ap"][:, 0:128], prf("CN", 0),
                                         cf["identf"][:]),
                             waits=[bkct["rtok"], tc1, tc2], inc="spe", reads=[KP("CN")])
                tct2 = em.op("pe", trans(bkct["ap"][:, 128:256], prf("CN", 1),
                                         cf["identf"][:]), inc="spe", reads=[KP("CN")])
                tg1 = em.op("dve", tt(lnsl(hs[0], c0, 128), bkct["ap"][:, 0:128],
                                      zsl(hs[0], c0, 128), OP.mult),
                            waits=[tct2, tok_zgw[hs[0]]], inc="sdve",
                            reads=[K_("zg", hs[0])], writes=[K_("ln", hs[0])])
                tg2 = em.op("dve", tt(lnsl(hs[1], c0, 128), bkct["ap"][:, 128:256],
                                      zsl(hs[1], c0, 128), OP.mult),
                            waits=[tok_zgw[hs[1]]], inc="sdve",
                            reads=[K_("zg", hs[1])], writes=[K_("ln", hs[1])])
                bkct["rtok"] = tg2
                inst_done.add(gp)

            inst_done = set()
            gens = deque()
            if tile == 0:
                gens.append(prelude_tail_gen())
            for c in range(NCH):
                gens.append(prep_gen(c))
                for kp in range(HK):
                    gens.append(pair_gen(c, kp))
            active = deque()
            while gens or active:
                while len(active) < _SCAN_DEPTH and gens:
                    active.append(gens.popleft())
                g0 = active.popleft()
                try:
                    next(g0)
                    active.append(g0)
                except StopIteration:
                    pass
            tok_core = ("sdve", em.cnt["sdve"])
            dump("core", lnbuf[:], [tok_core], keys=[K_("ln", i) for i in range(NCI)])

            tok_attn = []
            for co in range(N_WO):
                par = wload([(sc_wo.ap()[co * 128:(co + 1) * 128, :], 0, D)],
                            waits_extra=[pre_tok["wo"]])
                bk = fb.get()
                for ci in range(NCI):
                    w = [bk["rtok"], wload_tok[par], tok_core] if ci == 0 else []
                    em.op("pe", mm(bk["ap"], wsl(par, ci * 128, 128), lnsl(ci),
                                   start=(ci == 0), stop=(ci == NCI - 1)),
                          waits=w, inc="spe" if ci == NCI - 1 else None)
                tok_mm = ("spe", em.cnt["spe"])
                whalf_tok[par] = tok_mm
                for ci in range(NCI):
                    em.rd(K_("ln", ci), tok_mm)
                t_ev = em.op("act", act(qsl(co), bk["ap"], AF.Identity,
                                        bias=smalls["bout_s"][:, co:co + 1]),
                             waits=[tok_mm], inc="sact", writes=[K_("qkv", co)])
                bk["rtok"] = t_ev
                tok_attn.append(t_ev)
            dump("attn", qkvbuf[:, 0:NCI * TT], [tok_attn[-1]], keys=[K_("qkv", i) for i in range(NCI)])

            def ln2_writer(ci, apR, apM, tR, tM, _ta=tok_attn):
                em.op("dve", tt(tmpA[:], qsl(ci), apR, OP.mult),
                      waits=[tR, _ta[ci]], inc="sdve",
                      reads=[K_("qkv", ci)], writes=[K_("tmpA",)])
                em.op("dve", tt(tmpA[:], tmpA[:], apM, OP.subtract), waits=[tM], inc="sdve",
                      writes=[K_("tmpA",)])
                return em.op("dve", tt(xsl(ci), xsl(ci), tmpA[:], OP.add), inc="sdve",
                             reads=[K_("tmpA",)], writes=[K_("x", ci)])
            tok_x2 = emit_ln("bf16", qsl, lambda ci: K_("qkv", ci), tok_attn, ln2_writer)
            dump("x2", xbuf[:], [tok_x2[-1]], keys=[K_("x", i) for i in range(NCI)])

            def ln3_writer(ci, apR, apM, tR, tM, _tx=tok_x2):
                em.op("dve", tt(tmpA[:], xsl(ci), apR, OP.mult),
                      waits=[tR, _tx[ci]], inc="sdve",
                      reads=[K_("x", ci)], writes=[K_("tmpA",)])
                return em.op("dve", tt(lnsl(ci), tmpA[:], apM, OP.subtract),
                             waits=[tM], inc="sdve",
                             reads=[K_("tmpA",)], writes=[K_("ln", ci)])
            tok_ln3 = emit_ln("f32", xsl, lambda ci: K_("x", ci), tok_x2, ln3_writer)
            dump("ln3", lnbuf[:], [tok_ln3[-1]], keys=[K_("ln", i) for i in range(NCI)])

            def hkey(j):
                return K_("qkv", j) if j < 32 else K_("mixd", j - 32)

            def hslot(j, c0=0, w=TT):
                return qsl(j, c0, w) if j < 32 else mdata(j - 32, c0, w)

            tok_h = []
            for co in range(N_GU):
                par = wload([(sc_g.ap()[co * 128:(co + 1) * 128, :], 0, D),
                             (sc_u.ap()[co * 128:(co + 1) * 128, :], D, D)],
                            waits_extra=[pre_tok["g"], pre_tok["u"]])
                ldtok = wload_tok[par]
                bkG = fb.get()
                for ci in range(NCI):
                    w = [bkG["rtok"], ldtok, tok_ln3[ci]] if ci == 0 else [tok_ln3[ci]]
                    em.op("pe", mm(bkG["ap"], wsl(par, ci * 128, 128), lnsl(ci),
                                   start=(ci == 0), stop=(ci == NCI - 1)),
                          waits=w, inc="spe" if ci == NCI - 1 else None)
                tok_mmg = ("spe", em.cnt["spe"])
                bkU = fb.get()
                for ci in range(NCI):
                    w = [bkU["rtok"]] if ci == 0 else []
                    em.op("pe", mm(bkU["ap"], wsl(par, D + ci * 128, 128), lnsl(ci),
                                   start=(ci == 0), stop=(ci == NCI - 1)),
                          waits=w, inc="spe" if ci == NCI - 1 else None)
                tok_mmu = ("spe", em.cnt["spe"])
                whalf_tok[par] = tok_mmu
                for ci in range(NCI):
                    em.rd(K_("ln", ci), tok_mmu)
                t_sg = em.op("act", act(tmpA[:], bkG["ap"], AF.Silu,
                                        scale=smalls["gs_s"][:, co:co + 1]),
                             waits=[tok_mmg], inc="sact", writes=[K_("tmpA",)])
                bkG["rtok"] = t_sg
                spar = co % 2
                t_su_ = em.op("act", act(sqtmp[:, spar * TT:(spar + 1) * TT], bkU["ap"],
                                         AF.Identity, scale=smalls["us_s"][:, co:co + 1]),
                              waits=[tok_mmu], inc="sact", writes=[K_("sq", spar)])
                bkU["rtok"] = t_su_
                t_h = em.op("dve", tt(hslot(co), tmpA[:],
                                      sqtmp[:, spar * TT:(spar + 1) * TT], OP.mult),
                            waits=[t_sg, t_su_], inc="sdve",
                            reads=[K_("tmpA",), K_("sq", spar)], writes=[hkey(co)])
                tok_h.append(t_h)
            if "hbuf" in dbg and tile == _DBG_TILE:
                dump("hbuf", qkvbuf[:, 0:8 * TT], [tok_h[7]],
                     keys=[hkey(j) for j in range(8)])

            tok_ffn = []
            for co in range(N_DN):
                par = wload([(sc_d.ap()[co * 128:(co + 1) * 128, 0:32 * 128], 0, 32 * 128)],
                            waits_extra=[pre_tok["d"]])
                ld1 = wload_tok[par]
                ld2holder = []
                for ci in range(N_GU):
                    w = ([bkd["rtok"] if False else None] if False else [])
                    if ci == 0:
                        bk = fb.get()
                        w = [bk["rtok"], ld1] + tok_h
                    elif ci == 32:
                        w = [ld2holder[0]]
                    em.op("pe", mm(bk["ap"], wsl(par, (ci if ci < 32 else ci - 32) * 128, 128),
                                   hslot(ci),
                                   start=(ci == 0), stop=(ci == N_GU - 1)),
                          waits=w, inc="spe" if ci in (11, N_GU - 1) else None)
                    if ci == 11:
                        t_mm11 = ("spe", em.cnt["spe"])
                        ld2 = em.op("sync",
                                    dma(wstream[:, par * WHALF: par * WHALF + 12 * 128],
                                        sc_d.ap()[co * 128:(co + 1) * 128, 32 * 128:44 * 128]),
                                    waits=[t_mm11], inc="swd%d" % par, inc_n=16)
                        ld2holder.append(ld2)
                tok_mm = ("spe", em.cnt["spe"])
                whalf_tok[par] = tok_mm
                for ci in range(N_GU):
                    em.rd(hkey(ci), tok_mm)
                t_ev = em.op("act", act(mdata(12 + co), bk["ap"], AF.Identity,
                                        scale=smalls["ds_s"][:, co:co + 1]),
                             waits=[tok_mm], inc="sact", writes=[K_("mixd", 12 + co)])
                bk["rtok"] = t_ev
                tok_ffn.append(t_ev)

            def ln4_writer(ci, apR, apM, tR, tM, _tf=tok_ffn):
                em.op("dve", tt(tmpA[:], mdata(12 + ci), apR, OP.mult),
                      waits=[tR, _tf[ci]], inc="sdve",
                      reads=[K_("mixd", 12 + ci)], writes=[K_("tmpA",)])
                em.op("dve", tt(tmpA[:], tmpA[:], apM, OP.subtract), waits=[tM],
                      inc="sdve", writes=[K_("tmpA",)])
                return em.op("dve", tt(xsl(ci), xsl(ci), tmpA[:], OP.add), inc="sdve",
                             reads=[K_("tmpA",)], writes=[K_("x", ci)])

            tok_res4 = emit_ln("bf16", lambda ci: mdata(12 + ci),
                               lambda ci: K_("mixd", 12 + ci), tok_ffn, ln4_writer)
            t_flush = em.op("dve", lambda v: v.drain(), waits=[tok_res4[-1]], inc="sdve")
            for ci in range(NCI):
                em.op("act",
                      dma(yout[b].ap()[ci * 128:(ci + 1) * 128, tok0:tok0 + TT], xsl(ci)),
                      waits=[t_flush], inc="sod", inc_n=16)
                em.rd(K_("x", ci), ("sod", em.cnt["sod"]))
            t_of = em.op("act", None, waits=[("sod", em.cnt["sod"])], inc="sact")
            for ci in range(NCI):
                em.rd(K_("x", ci), t_of)

        em.op("pool", None, waits=[("sod", em.cnt["sod"]), ("sxd", em.cnt["sxd"]),
                                   ("swd0", em.cnt["swd0"]), ("swd1", em.cnt["swd1"]),
                                   ("sdbg", em.cnt["sdbg"]),
                                   ("spl0", em.cnt["spl0"]), ("spl1", em.cnt["spl1"])])
        em.emit(block, sems)

    return nc


def _host_stage(inputs):
    f32 = np.float32
    x = inputs["L0_attn_hidden_states"]
    common = {}
    common["wqkvT"] = np.ascontiguousarray(np.asarray(inputs["L0_attn_Wqkv"]).T.astype(f32))
    common["wzT"] = np.ascontiguousarray(np.asarray(inputs["L0_attn_Wz"]).T.astype(f32))
    wab = np.zeros((D, 32), f32)
    wab[:, 0:16] = np.asarray(inputs["L0_attn_Wa"]).T
    wab[:, 16:32] = np.asarray(inputs["L0_attn_Wb"]).T
    common["wabT"] = wab
    common["woutT"] = np.ascontiguousarray(np.asarray(inputs["L0_attn_Wout"]).T.astype(f32))
    common["wgT"] = np.ascontiguousarray(np.asarray(inputs["L0_ffn_gate_w"]).T.astype(f32))
    common["wuT"] = np.ascontiguousarray(np.asarray(inputs["L0_ffn_up_w"]).T.astype(f32))
    common["wdT"] = np.ascontiguousarray(np.asarray(inputs["L0_ffn_down_w"]).T.astype(f32))
    common["bqkv_s"] = np.ascontiguousarray(
        np.asarray(inputs["L0_attn_bqkv"]).reshape(32, 128).T.astype(f32))
    common["bz_s"] = np.ascontiguousarray(
        np.asarray(inputs["L0_attn_bz"]).reshape(16, 128).T.astype(f32))
    common["bout_s"] = np.ascontiguousarray(
        np.asarray(inputs["L0_attn_bout"]).reshape(16, 128).T.astype(f32))
    common["convw_s"] = np.ascontiguousarray(
        np.asarray(inputs["L0_attn_conv_w"]).reshape(32, 128, 4)
        .transpose(1, 0, 2).reshape(128, 128).astype(f32))
    common["convb_s"] = np.ascontiguousarray(
        np.asarray(inputs["L0_attn_conv_b"]).reshape(32, 128).T.astype(f32))
    common["gs_s"] = np.ascontiguousarray(
        np.asarray(inputs["L0_ffn_gate_s"]).reshape(44, 128).T.astype(f32))
    common["us_s"] = np.ascontiguousarray(
        np.asarray(inputs["L0_ffn_up_s"]).reshape(44, 128).T.astype(f32))
    common["ds_s"] = np.ascontiguousarray(
        np.asarray(inputs["L0_ffn_down_s"]).reshape(16, 128).T.astype(f32))
    common["normw_s"] = np.ascontiguousarray(np.asarray(inputs["L0_attn_norm_w"]).astype(f32)[:, None])
    common["ba_s"] = np.asarray(inputs["L0_attn_ba"]).astype(f32)[:, None]
    common["bb_s"] = np.asarray(inputs["L0_attn_bb"]).astype(f32)[:, None]
    common["dtb_s"] = np.ascontiguousarray(np.asarray(inputs["L0_attn_dt_bias"]).astype(f32)[:, None])
    common["alog_s"] = np.ascontiguousarray(np.asarray(inputs["L0_attn_A_log"]).astype(f32)[:, None])
    common["identf"] = np.eye(128, dtype=f32)
    common["maskUf"] = np.triu(np.ones((128, 128), f32))
    common["ones16f"] = np.ones((16, 128), f32)
    common["ident16f"] = np.eye(16, dtype=f32)
    common["onesrow_f"] = np.ones((1, 128), f32)
    common["onescol_f"] = np.ones((128, 1), f32)
    cvec = np.zeros((128, 8), f32)
    cvec[:, 0] = LNEPS
    cvec[:, 1] = 1e-6
    cvec[:, 2] = 1.0
    cvec[:, 3] = -0.5 * np.log(128.0)
    common["cvec"] = cvec
    common["identb"] = np.eye(128).astype(BF)
    common["maskSnb"] = (-np.triu(np.ones((128, 128), f32), 1)).astype(BF)
    common["maskIb"] = np.triu(np.ones((128, 128), f32)).astype(BF)
    common["onescol_b"] = np.ones((128, 1), f32).astype(BF)
    in_maps = []
    for core in range(8):
        m = dict(common)
        for bb_ in range(2):
            m[f"x{bb_}"] = np.ascontiguousarray(np.asarray(x[core * 2 + bb_]).T.astype(f32))
        in_maps.append(m)
    return in_maps


def kernel(**inputs):
    in_maps = _host_stage(inputs)
    nc = build_program(dbg=_DBG)
    res = run_bass_kernel_spmd(nc, in_maps, core_ids=list(range(8)))
    out = np.empty((16, 2048, D), np.float32)
    for core in range(8):
        for bb_ in range(2):
            out[core * 2 + bb_] = res.results[core][f"y{bb_}"].T
    return out


# version 11
# speedup vs baseline: 1.0406x; 1.0406x over previous
import numpy as np
import ml_dtypes
from collections import deque
from contextlib import ExitStack

import concourse.bass as bass
import concourse.mybir as mybir
from concourse.bass_utils import run_bass_kernel_spmd

F32 = mybir.dt.float32
BF16 = mybir.dt.bfloat16
AF = mybir.ActivationFunctionType
OP = mybir.AluOpType
BF = ml_dtypes.bfloat16

D = 2048
NCI = 16
TT = 512
NTILE = 8
C = 128
NCH = TT // C
HV = 16
HK = 8
CONVD = 4096
INTER = 5632
N_QKV = 32
N_Z = 16
N_WO = 16
N_GU = 44
N_DN = 16
MW = 516
MD = 4
LNEPS = 1e-5
RMSEPS = 1e-6
L2EPS = 1e-6

ENGINES = ("pe", "dve", "act", "pool", "sync")

_DBG = ()
_SCAN_DEPTH = 5
_YS = set(range(40))
_MAXTILE = NTILE
_PARANOID = True
_DBG_TILE = 0


class Em:

    def __init__(self):
        self.streams = {e: [] for e in ENGINES}
        self.cnt = {}
        self.war = {}

    def sem(self, *names):
        for n in names:
            self.cnt[n] = 0

    def op(self, eng, fn, waits=(), inc=None, inc_n=1, reads=(), writes=()):
        ws = list(waits)
        for key in writes:
            rec = self.war.get(key)
            if rec:
                ws.extend(rec["r"])
                if rec["w"] is not None:
                    ws.append(rec["w"])
        for key in reads:
            rec = self.war.get(key)
            if rec and rec["w"] is not None:
                ws.append(rec["w"])
        ws2 = []
        for w in ws:
            if w is None:
                continue
            s, v = w
            assert v <= self.cnt[s], f"wait on future token {s}={v} (planned {self.cnt[s]})"
            ws2.append((s, v))
        tok = None
        if inc is not None:
            self.cnt[inc] += inc_n
            tok = (inc, self.cnt[inc])
        assert tok is not None or (not reads and not writes), "hazard tracking needs a token"
        for key in writes:
            self.war[key] = {"w": tok, "r": []}
        for key in reads:
            self.war.setdefault(key, {"w": None, "r": []})["r"].append(tok)
        self.streams[eng].append((tuple(ws2), fn, inc, inc_n))
        return tok

    def rd(self, key, tok):
        self.war.setdefault(key, {"w": None, "r": []})["r"].append(tok)

    def emit(self, block, handles):
        sections = {"pe": block.tensor, "dve": block.vector, "act": block.scalar,
                    "pool": block.gpsimd, "sync": block.sync}
        ownsem = {"pe": "spe", "dve": "sdve", "act": "sact", "pool": "spool"}
        for ename in ENGINES:
            stream = self.streams[ename]
            own = ownsem.get(ename)

            def make(stream, own):
                def body(eng):
                    guard = {}
                    own_cnt = 0
                    for ws, fn, inc, inc_n in stream:
                        for s, v in ws:
                            if s == own and v <= own_cnt and not _PARANOID:
                                continue
                            if guard.get(s, 0) < v:
                                eng.wait_ge(handles[s], v)
                                guard[s] = v
                        if fn is not None:
                            ins = fn(eng)
                            if inc is not None:
                                if inc_n == 1 and ename in ("dve", "act", "pool"):
                                    eng.drain().then_inc(handles[inc], inc_n)
                                else:
                                    ins.then_inc(handles[inc], inc_n)
                        elif inc is not None:
                            eng.sem_inc(handles[inc], inc_n)
                        if inc == own:
                            own_cnt += inc_n
                return body
            sections[ename](make(stream, own))


class Banks:
    def __init__(self, aps):
        self.recs = [{"ap": ap, "rtok": None} for ap in aps]
        self.i = 0

    def get(self):
        r = self.recs[self.i % len(self.recs)]
        self.i += 1
        return r


DBG_SHAPES = {
    "ln1": ([128, NCI * TT], "f32"),
    "mixed": ([128, N_QKV * TT], "f32"),
    "qn": ([128, NCI * TT], "f32"),
    "gclb": ([16, 3 * TT], "f32"),
    "cs0": ([128, 16], "f32"),
    "core": ([128, NCI * TT], "f32"),
    "attn": ([128, NCI * TT], "f32"),
    "x2": ([128, NCI * TT], "f32"),
    "ln3": ([128, NCI * TT], "f32"),
    "hbuf": ([128, 8 * TT], "f32"),
    "zgw": ([128, NCI * TT], "bf16"),
    "sEs": ([128, 16 * 128], "bf16"),
    "sEi": ([128, 16 * 128], "bf16"),
    "sG": ([128, 16 * 128], "bf16"),
    "sY0": ([128, 16 * 128], "bf16"),
    "sU": ([128, 16 * 128], "bf16"),
    "sO": ([128, 16 * 128], "f32"),
    "svar": ([128, 4], "f32"),
    "sS1": ([128, 16 * 128], "f32"),
    "sKtm": ([128, 16 * 128], "bf16"),
    "sVtm": ([128, 16 * 128], "bf16"),
    "sct": ([128, 8 * 16], "f32"),
}


def build_program(dbg=()):
    nc = bass.Bass("TRN2", target_bir_lowering=False)

    xin = [nc.dram_tensor(f"x{b}", [D, 2048], F32, kind="ExternalInput") for b in range(2)]
    yout = [nc.dram_tensor(f"y{b}", [D, 2048], F32, kind="ExternalOutput") for b in range(2)]

    wqkvT = nc.dram_tensor("wqkvT", [D, CONVD], F32, kind="ExternalInput")
    wzT = nc.dram_tensor("wzT", [D, D], F32, kind="ExternalInput")
    wabT = nc.dram_tensor("wabT", [D, 32], F32, kind="ExternalInput")
    woutT = nc.dram_tensor("woutT", [D, D], F32, kind="ExternalInput")
    wgT = nc.dram_tensor("wgT", [D, INTER], F32, kind="ExternalInput")
    wuT = nc.dram_tensor("wuT", [D, INTER], F32, kind="ExternalInput")
    wdT = nc.dram_tensor("wdT", [INTER, D], F32, kind="ExternalInput")

    smalls_in = {}
    for nm, shp in [("bqkv_s", [128, 32]), ("bz_s", [128, 16]), ("bout_s", [128, 16]),
                    ("convw_s", [128, 128]), ("convb_s", [128, 32]),
                    ("gs_s", [128, 44]), ("us_s", [128, 44]), ("ds_s", [128, 16]),
                    ("normw_s", [128, 1]), ("ba_s", [16, 1]), ("bb_s", [16, 1]), ("dtb_s", [16, 1]),
                    ("alog_s", [16, 1])]:
        smalls_in[nm] = nc.dram_tensor(nm, shp, F32, kind="ExternalInput")
    consts_f = {}
    for nm, shp in [("identf", [128, 128]), ("maskUf", [128, 128]),
                    ("ones16f", [16, 128]), ("ident16f", [16, 16]),
                    ("onesrow_f", [1, 128]), ("onescol_f", [128, 1]),
                    ("cvec", [128, 8])]:
        consts_f[nm] = nc.dram_tensor(nm, shp, F32, kind="ExternalInput")
    consts_b = {}
    for nm, shp in [("identb", [128, 128]), ("maskSnb", [128, 128]),
                    ("maskIb", [128, 128]), ("onescol_b", [128, 1])]:
        consts_b[nm] = nc.dram_tensor(nm, shp, BF16, kind="ExternalInput")

    sc_qkv = nc.dram_tensor("sc_qkv", [N_QKV * 128, D], BF16)
    sc_z = nc.dram_tensor("sc_z", [N_Z * 128, D], BF16)
    sc_wo = nc.dram_tensor("sc_wo", [N_WO * 128, D], BF16)
    sc_g = nc.dram_tensor("sc_g", [N_GU * 128, D], BF16)
    sc_u = nc.dram_tensor("sc_u", [N_GU * 128, D], BF16)
    sc_d = nc.dram_tensor("sc_d", [N_DN * 128, INTER], BF16)

    build_program._outs = {"y0": yout[0], "y1": yout[1]}
    dbg_t = {nm: nc.dram_tensor("dbg_" + nm, DBG_SHAPES[nm][0],
                              F32 if DBG_SHAPES[nm][1] == "f32" else BF16,
                              kind="ExternalOutput")
             for nm in dbg}
    build_program._dbg = dbg_t

    em = Em()
    em.sem("ssm", "spl0", "spl1", "spd0", "spd1", "swd0", "swd1",
           "sxd", "sod", "sdbg", "spe", "sdve", "sact", "spool")

    with ExitStack() as stk:
        e = stk.enter_context
        xbuf = e(nc.sbuf_tensor("xbuf", [128, NCI * TT], F32))
        lnbuf = e(nc.sbuf_tensor("lnbuf", [128, NCI * TT], BF16))
        mixbuf = e(nc.sbuf_tensor("mixbuf", [128, N_QKV * MW], BF16))
        qkvbuf = e(nc.sbuf_tensor("qkvbuf", [128, N_QKV * TT], BF16))
        zgwbuf = e(nc.sbuf_tensor("zgwbuf", [128, NCI * TT], BF16))
        state = e(nc.sbuf_tensor("state", [128, HV * 128], F32))
        statebf = e(nc.sbuf_tensor("statebf", [128, HV * 128], BF16))
        WHALF = 4096
        wstream = e(nc.sbuf_tensor("wstream", [128, 2 * WHALF], BF16))
        sqtmp = e(nc.sbuf_tensor("sqtmp", [128, 2 * TT], BF16))
        tmpA = e(nc.sbuf_tensor("tmpA", [128, TT], F32))
        convacc = e(nc.sbuf_tensor("convacc", [128, TT], F32))
        rows = e(nc.sbuf_tensor("rows", [1, 1536], F32))
        tailbuf = e(nc.sbuf_tensor("tailbuf", [128, N_QKV * 3], BF16))
        zero128 = e(nc.sbuf_tensor("zero128", [128, 128], BF16))
        gfm = e(nc.sbuf_tensor("gfm", [16, TT], F32))
        lbfm = e(nc.sbuf_tensor("lbfm", [16, TT], F32))
        befm = e(nc.sbuf_tensor("befm", [16, TT], F32))
        abw = e(nc.sbuf_tensor("abw", [128, NCI * 32], BF16))
        NIP = 3
        scnames = ["G", "P2t", "P4t", "P8t", "RW0",
                   "RW1", "Y0", "Y1", "Khat", "Mb"]
        scn = {nm: e(nc.sbuf_tensor("sc_" + nm, [128, NIP * 256], BF16)) for nm in scnames}
        scn["E"] = scn["Y1"]
        scn["Vtm"] = scn["RW0"]
        scf = {nm: e(nc.sbuf_tensor("sf_" + nm, [128, NIP * 256], F32))
               for nm in ["TO", "O"]}
        scf["CN"] = scf["TO"]
        scf["B0"] = e(nc.sbuf_tensor("sf_B0", [128, NIP * 128], F32))
        scp = {nm: e(nc.sbuf_tensor("sp_" + nm, [128, NIP * 128],
                                    F32 if nm != "Ktm" else BF16))
               for nm in ["SkkMf", "SkqMf", "Ktm"]}

        def sfsl2(nm, ipp):
            return scp[nm][:, ipp * 128:(ipp + 1) * 128]
        sfsl2b = sfsl2
        ctile = e(nc.sbuf_tensor("ctile", [128, 2 * 8 * 16], F32))
        cs127 = e(nc.sbuf_tensor("cs127", [16, 2], F32))
        bc16 = e(nc.sbuf_tensor("bc16", [16, 2], F32))
        diagbc = e(nc.sbuf_tensor("diagbc", [16, 2 * 16], F32))
        varcol = e(nc.sbuf_tensor("varcol", [128, 2 * NIP], F32))
        bcb = e(nc.sbuf_tensor("bcb", [128, 2 * NIP], F32))
        rstdcol = e(nc.sbuf_tensor("rstdcol", [128, 2 * NIP], F32))
        small16 = e(nc.sbuf_tensor("small16", [16, 4], F32))
        smalls = {nm: e(nc.sbuf_tensor("sb_" + nm, list(t.shape), F32))
                  for nm, t in smalls_in.items()}
        cf = {nm: e(nc.sbuf_tensor("sb_" + nm, list(t.shape), F32))
              for nm, t in consts_f.items()}
        cb = {nm: e(nc.sbuf_tensor("sb_" + nm, list(t.shape), BF16))
              for nm, t in consts_b.items()}

        pb = [e(nc.psum_tensor(f"pb{i}", [128, 512], F32)) for i in range(6)]
        pxb = [e(nc.psum_tensor(f"pxb{i}", [128, 1024], BF16)) for i in range(2)]

        sems = {n: e(nc.semaphore(n)) for n in
                ["ssm", "spl0", "spl1", "spd0", "spd1", "swd0", "swd1",
                 "sxd", "sod", "sdbg", "spe", "sdve", "sact", "spool"]}
        block = e(nc.Block())

        fb = Banks([t[:, 0:512] for t in pb])
        bb = Banks([t[:, c0:c0 + 256] for t in pxb for c0 in (0, 256)])

        def mm(out, lhsT, rhs, start=True, stop=True):
            return lambda t: t.matmul(out, lhsT, rhs, start=start, stop=stop)

        def trans(out, in_, ident):
            return lambda t: t.transpose(out, in_, ident)

        def act(out, in_, func, bias=None, scale=None, accum=None):
            kw = {}
            if bias is not None:
                kw["bias"] = bias
            if scale is not None:
                kw["scale"] = scale
            if accum is not None:
                kw["accum_out"] = accum
            return lambda s: s.activation(out, in_, func, **kw)

        def tt(out, a, b, op):
            return lambda v: v.tensor_tensor(out, a, b, op=op)

        def ts(out, a, s1, op0, s2=None, op1=None):
            if op1 is None:
                return lambda v: v.tensor_scalar(out, a, scalar1=s1, scalar2=None, op0=op0)
            return lambda v: v.tensor_scalar(out, a, scalar1=s1, scalar2=s2, op0=op0, op1=op1)

        def stt(out, in0, s, in1, op0, op1):
            return lambda v: v.scalar_tensor_tensor(out, in0, s, in1, op0=op0, op1=op1)

        def cp(out, in_):
            return lambda v: v.tensor_copy(out, in_)

        def dma(out, in_):
            return lambda q: q.dma_start(out, in_)

        def xsl(ci):
            return xbuf[:, ci * TT:(ci + 1) * TT]

        def lnsl(ci, c0=0, w=TT):
            return lnbuf[:, ci * TT + c0:ci * TT + c0 + w]

        def msl(co, off, w):
            return mixbuf[:, co * MW + off:co * MW + off + w]

        def mdata(co, c0=0, w=TT):
            return msl(co, MD + c0, w)

        def qsl(co, c0=0, w=TT):
            return qkvbuf[:, co * TT + c0:co * TT + c0 + w]

        def zsl(ci, c0=0, w=TT):
            return zgwbuf[:, ci * TT + c0:ci * TT + c0 + w]

        def ssl(h):
            return state[:, h * 128:(h + 1) * 128]

        def sbsl(h):
            return statebf[:, h * 128:(h + 1) * 128]

        def scsl(nm, par):
            return scn[nm][:, par * 128:(par + 1) * 128]

        def sfsl(nm, par):
            return scf[nm][:, par * 128:(par + 1) * 128]

        def ct(par, slot):
            base = par * 8 * 16 + slot * 16
            return ctile[:, base:base + 16]

        K_ = lambda *a: tuple(a)

        def pxsl(bk, s):
            return bk["ap"][:, s * 128:(s + 1) * 128]

        for nm, t in smalls_in.items():
            em.op("sync", dma(smalls[nm][:], t.ap()), inc="ssm", inc_n=16)
        for nm, t in consts_f.items():
            em.op("sync", dma(cf[nm][:], t.ap()), inc="ssm", inc_n=16)
        for nm, t in consts_b.items():
            em.op("sync", dma(cb[nm][:], t.ap()), inc="ssm", inc_n=16)
        tok_smalls = ("ssm", em.cnt["ssm"])

        em.op("dve", lambda v: v.memset(zero128[:], 0.0), inc="sdve")
        t_e = em.op("act", act(small16[:, 0:1], smalls["alog_s"][:, 0:1], AF.Exp),
                    waits=[tok_smalls], inc="sact")
        em.op("dve", ts(small16[:, 0:1], small16[:, 0:1], -1.0, OP.mult),
              waits=[t_e], inc="sdve")
        em.op("dve", tt(small16[:, 1:2], smalls["ba_s"][:, 0:1],
                        smalls["dtb_s"][:, 0:1], OP.add),
              waits=[tok_smalls], inc="sdve")
        em.op("dve", ts(small16[:, 2:3], smalls["bb_s"][:, 0:1], -1.0, OP.mult),
              waits=[tok_smalls], inc="sdve")
        tok_small_done = em.op("dve", lambda v: v.memset(tailbuf[:], 0.0), inc="sdve")
        negea = small16[:, 0:1]
        dtbeff = small16[:, 1:2]
        negbb = small16[:, 2:3]

        pre_tok = {}
        pre_cast = [tok_small_done, tok_small_done]
        pre_store = [None, None]

        def prelude_weight(name, wt, sc, n_co, IN, kctr=[0]):
            pieces = [(i * 1024, min(1024, IN - i * 1024)) for i in range((IN + 1023) // 1024)]
            for co in range(n_co):
                for (r0, rl) in pieces:
                    par = kctr[0] % 2
                    kctr[0] += 1
                    pin = state[:, par * 1024: par * 1024 + rl]
                    pout = statebf[:, par * 1024: par * 1024 + rl]
                    src = wt.ap()[r0:r0 + rl, co * 128:(co + 1) * 128] \
                        .rearrange("(c p) q -> p c q", p=128)
                    t_ld = em.op("sync", dma(pin.rearrange("p (c q) -> p c q", q=128), src),
                                 waits=[pre_cast[par]], inc="spl%d" % par, inc_n=16)
                    t_c = em.op("act", act(pout, pin, AF.Copy),
                                waits=[t_ld, pre_store[par]], inc="sact")
                    pre_cast[par] = t_c
                    t_cd = em.op("act", lambda s: s.drain(), inc="sact")
                    t_st = em.op("pool", dma(sc.ap()[co * 128:(co + 1) * 128, r0:r0 + rl], pout),
                                 waits=[t_cd], inc="spd%d" % par, inc_n=16)
                    pre_store[par] = t_st
            pre_tok[name] = em.op("pool", None,
                                  waits=[("spd0", em.cnt["spd0"]),
                                         ("spd1", em.cnt["spd1"])],
                                  inc="spool")


        whalf_tok = [None, None]
        wload_tok = [None, None]
        wload_i = [0]
        tok_state_cast = [tok_small_done] * HV
        tok_state_new = [tok_small_done] * HV

        def wload(srcs, waits_extra=()):
            par = wload_i[0] % 2
            wload_i[0] += 1
            w = [whalf_tok[par]] + list(waits_extra)
            tok = None
            for (sc_ap, off, width) in srcs:
                tok = em.op("sync",
                            dma(wstream[:, par * WHALF + off: par * WHALF + off + width], sc_ap),
                            waits=w, inc="swd%d" % par, inc_n=16)
                w = []
            wload_tok[par] = tok
            return par

        def wsl(par, off, w):
            return wstream[:, par * WHALF + off: par * WHALF + off + w]

        def pe_drained(fn, waits=(), reads=()):
            em.op("pe", fn, waits=waits)
            tok = em.op("pe", lambda t: t.drain(), inc="spe")
            for k in reads:
                em.rd(k, tok)
            return tok

        def emit_ln(kind, get, key_of, tile_toks, out_writer):
            bS = fb.get()
            bQ = fb.get()
            onesc = cf["onescol_f"][:] if kind == "f32" else cb["onescol_b"][:]
            grp_base = None
            for ci in range(NCI):
                w0 = [bS["rtok"], tile_toks[ci]] if ci == 0 else [tile_toks[ci]]
                em.op("pe", mm(bS["ap"][0:1, :], onesc, get(ci),
                               start=(ci == 0), stop=(ci == NCI - 1)),
                      waits=w0, inc="spe" if ci == NCI - 1 else None)
            tokS = ("spe", em.cnt["spe"])
            for ci in range(NCI):
                em.rd(key_of(ci), tokS)
            for ci in range(NCI):
                par = ci % 2
                sqv = sqtmp[:, par * TT:(par + 1) * TT]
                t2 = em.op("act", act(sqv, get(ci), AF.Square),
                           waits=[tile_toks[ci]], inc="sact",
                           reads=[key_of(ci)], writes=[K_("sq", par)])
                w1 = [bQ["rtok"], t2] if ci == 0 else [t2]
                t3 = em.op("pe", mm(bQ["ap"][0:1, :], cb["onescol_b"][:], sqv,
                                    start=(ci == 0), stop=(ci == NCI - 1)),
                           waits=w1, inc="spe", reads=[K_("sq", par)])
            tokQ = ("spe", em.cnt["spe"])
            m_r, v_r, w_r = rows[0:1, 0:512], rows[0:1, 512:1024], rows[0:1, 1024:1536]
            t4 = em.op("dve", ts(m_r, bS["ap"][0:1, :], 1.0 / D, OP.mult),
                       waits=[tokS], inc="sdve", writes=[K_("rows", 0)])
            bS["rtok"] = t4
            t5 = em.op("dve", ts(v_r, bQ["ap"][0:1, :], 1.0 / D, OP.mult),
                       waits=[tokQ], inc="sdve", writes=[K_("rows", 1)])
            bQ["rtok"] = t5
            t6 = em.op("dve", tt(w_r, m_r, m_r, OP.mult), inc="sdve",
                       reads=[K_("rows", 0)], writes=[K_("rows", 2)])
            t7 = em.op("dve", tt(v_r, v_r, w_r, OP.subtract), inc="sdve",
                       reads=[K_("rows", 2)], writes=[K_("rows", 1)])
            t8 = em.op("act", act(w_r, v_r, AF.Ln, bias=cf["cvec"][0:1, 0:1]), waits=[t7], inc="sact",
                       reads=[K_("rows", 1)], writes=[K_("rows", 2)])
            t9 = em.op("act", act(v_r, w_r, AF.Exp, scale=-0.5), inc="sact",
                       reads=[K_("rows", 2)], writes=[K_("rows", 1)])
            t10 = em.op("dve", tt(w_r, m_r, v_r, OP.mult), waits=[t9], inc="sdve",
                        reads=[K_("rows", 0), K_("rows", 1)], writes=[K_("rows", 2)])
            bR = fb.get()
            bM = fb.get()
            tR = em.op("pe", mm(bR["ap"], cf["onesrow_f"][:], v_r),
                       waits=[bR["rtok"], t9], inc="spe", reads=[K_("rows", 1)])
            tM = em.op("pe", mm(bM["ap"], cf["onesrow_f"][:], w_r),
                       waits=[bM["rtok"], t10], inc="spe", reads=[K_("rows", 2)])
            out_toks = [out_writer(ci, bR["ap"], bM["ap"], tR, tM) for ci in range(NCI)]
            bR["rtok"] = out_toks[-1]
            bM["rtok"] = out_toks[-1]
            return out_toks

        def dump(name, src_ap, waits, keys=(), dst=None):
            if name in dbg and tile_now[0] == _DBG_TILE:
                t_d1 = em.op("dve", lambda v: v.drain(), waits=waits, inc="sdve")
                t_d2 = em.op("act", lambda s: s.drain(), waits=waits, inc="sact")
                em.op("pool", dma(dst if dst is not None else dbg_t[name].ap(), src_ap),
                      waits=[t_d1, t_d2], inc="sdbg", inc_n=16)
                t_f = em.op("pool", None, waits=[("sdbg", em.cnt["sdbg"])], inc="spool")
                for k in keys:
                    em.rd(k, t_f)

        tile_now = [0]

        for ci in range(NCI):
            src0 = xin[0].ap()[ci * 128:(ci + 1) * 128, 0:TT]
            em.op("act", dma(xsl(ci), src0), inc="sxd", inc_n=16,
                  writes=[K_("x", ci)])
        t_xf0 = em.op("act", None, waits=[("sxd", em.cnt["sxd"]), tok_smalls],
                      inc="sact")
        for ci in range(NCI):
            em.war[K_("x", ci)]["w"] = t_xf0
        tok_xload0 = [t_xf0] * NCI

        def ln1_writer0(ci, apR, apM, tR, tM, _toks=tok_xload0):
            em.op("dve", tt(tmpA[:], xsl(ci), apR, OP.mult),
                  waits=[tR, _toks[ci]], inc="sdve",
                  reads=[K_("x", ci)], writes=[K_("tmpA",)])
            return em.op("dve", tt(lnsl(ci), tmpA[:], apM, OP.subtract),
                         waits=[tM], inc="sdve",
                         reads=[K_("tmpA",)], writes=[K_("ln", ci)])
        tok_ln1_t0 = emit_ln("f32", xsl, lambda ci: K_("x", ci), tok_xload0,
                             ln1_writer0)

        prelude_weight("qkv", wqkvT, sc_qkv, N_QKV, D)
        prelude_weight("z", wzT, sc_z, N_Z, D)
        t_ld = em.op("sync", dma(state[:, 0:512].rearrange("p (c q) -> p c q", q=32),
                                 wabT.ap().rearrange("(c p) q -> p c q", p=128)),
                     waits=[pre_cast[0]], inc="spl0", inc_n=16)
        t_ab = em.op("act", act(abw[:], state[:, 0:512], AF.Copy),
                     waits=[t_ld, pre_store[0]], inc="sact")
        pre_cast[0] = t_ab
        pre_tok["ab"] = t_ab
        tok_prelude_head = em.op("pool", None,
                                 waits=[("spd0", em.cnt["spd0"]),
                                        ("spd1", em.cnt["spd1"])],
                                 inc="spool")

        def prelude_tail_gen():
            pins = [(tmpA[:, 0:512], sqtmp[:, 0:TT], K_("tmpA",), K_("sq", 0)),
                    (convacc[:, 0:512], sqtmp[:, TT:2 * TT], K_("ca0",),
                     K_("sq", 1))]
            lstore = [None, None]
            plan = []
            done_after = {}
            for name, wt, sc, n_co, IN in [("wo", woutT, sc_wo, N_WO, D),
                                           ("g", wgT, sc_g, N_GU, D),
                                           ("u", wuT, sc_u, N_GU, D),
                                           ("d", wdT, sc_d, N_DN, INTER)]:
                for co in range(n_co):
                    for r0 in range(0, IN, 512):
                        plan.append((wt, sc, co, r0))
                done_after[len(plan) - 1] = name
            for i, (wt, sc, co, r0) in enumerate(plan):
                par = i % 2
                pin, bout, kpin, kbout = pins[par]
                src = wt.ap()[r0:r0 + 512, co * 128:(co + 1) * 128] \
                    .rearrange("(c p) q -> p c q", p=128)
                t_ld2 = em.op("sync", dma(pin.rearrange("p (c q) -> p c q", q=128), src),
                              inc="spl%d" % par, inc_n=16, writes=[kpin])
                if par == 0:
                    t_c2 = em.op("act", act(bout, pin, AF.Copy),
                                 waits=[t_ld2, lstore[par]], inc="sact",
                                 reads=[kpin], writes=[kbout])
                    st_eng = "act"
                else:
                    t_c2 = em.op("dve", cp(bout, pin),
                                 waits=[t_ld2, lstore[par]], inc="sdve",
                                 reads=[kpin], writes=[kbout])
                    st_eng = "sync"
                t_st2 = em.op(st_eng, dma(sc.ap()[co * 128:(co + 1) * 128,
                                                  r0:r0 + 512], bout),
                              waits=[t_c2], inc="spd%d" % par, inc_n=16)
                em.rd(kbout, t_st2)
                lstore[par] = t_st2
                if i in done_after:
                    pre_tok[done_after[i]] = em.op(
                        "pool", None,
                        waits=[("spd0", em.cnt["spd0"]),
                               ("spd1", em.cnt["spd1"])],
                        inc="spool")
                if i % 2 == 1:
                    yield

        tok_g = tok_lb = tok_be = tok_small_done

        for tile in range(min(NTILE, _MAXTILE)):
            tile_now[0] = tile
            b = tile // 4
            sc4 = tile % 4
            tok0 = sc4 * TT
            seq_start = (sc4 == 0)

            if tile == 0:
                tok_ln1 = tok_ln1_t0
            else:
                for ci in range(NCI):
                    src = xin[b].ap()[ci * 128:(ci + 1) * 128, tok0:tok0 + TT]
                    em.op("act", dma(xsl(ci), src), inc="sxd", inc_n=16,
                          writes=[K_("x", ci)])
                t_xf = em.op("act", None, waits=[("sxd", em.cnt["sxd"])], inc="sact")
                for ci in range(NCI):
                    em.war[K_("x", ci)]["w"] = t_xf
                tok_xload = [t_xf] * NCI

                def ln1_writer(ci, apR, apM, tR, tM, _toks=tok_xload):
                    em.op("dve", tt(tmpA[:], xsl(ci), apR, OP.mult),
                          waits=[tR, _toks[ci]], inc="sdve",
                          reads=[K_("x", ci)], writes=[K_("tmpA",)])
                    return em.op("dve", tt(lnsl(ci), tmpA[:], apM, OP.subtract),
                                 waits=[tM], inc="sdve",
                                 reads=[K_("tmpA",)], writes=[K_("ln", ci)])
                tok_ln1 = emit_ln("f32", xsl, lambda ci: K_("x", ci), tok_xload,
                                  ln1_writer)
            dump("ln1", lnbuf[:], [tok_ln1[-1]], keys=[K_("ln", i) for i in range(NCI)])

            tok_conv = []
            for co in range(N_QKV):
                par = wload([(sc_qkv.ap()[co * 128:(co + 1) * 128, :], 0, D)],
                            waits_extra=[pre_tok["qkv"]])
                bk = fb.get()
                for ci in range(NCI):
                    w = [bk["rtok"], wload_tok[par], tok_ln1[ci]] if ci == 0 \
                        else [tok_ln1[ci]]
                    em.op("pe", mm(bk["ap"], wsl(par, ci * 128, 128), lnsl(ci),
                                   start=(ci == 0), stop=(ci == NCI - 1)),
                          waits=w, inc="spe" if ci == NCI - 1 else None)
                tok_mm = ("spe", em.cnt["spe"])
                whalf_tok[par] = tok_mm
                for ci in range(NCI):
                    em.rd(K_("ln", ci), tok_mm)
                t_ev = em.op("act", act(mdata(co), bk["ap"], AF.Identity,
                                        bias=smalls["bqkv_s"][:, co:co + 1]),
                             waits=[tok_mm], inc="sact", writes=[K_("mixd", co)])
                bk["rtok"] = t_ev
                if seq_start:
                    t_h = em.op("pool", lambda v, _co=co: v.memset(msl(_co, 1, 3), 0.0),
                                inc="spool", writes=[K_("mixh", co)])
                else:
                    t_h = em.op("pool", cp(msl(co, 1, 3), tailbuf[:, co * 3:co * 3 + 3]),
                                inc="spool", writes=[K_("mixh", co)], reads=[K_("tail", co)])
                em.op("pool", cp(tailbuf[:, co * 3:co * 3 + 3], msl(co, 513, 3)),
                      waits=[t_ev], inc="spool", reads=[K_("mixd", co)],
                      writes=[K_("tail", co)])
                a0 = convacc[:, 0:TT]
                a1 = tmpA[:, 0:TT]
                em.op("dve", ts(a0, msl(co, 1, TT),
                                smalls["convw_s"][:, co * 4:co * 4 + 1], OP.mult),
                      waits=[t_ev, t_h], inc="sdve",
                      reads=[K_("mixd", co), K_("mixh", co)], writes=[K_("ca0",)])
                em.op("dve", stt(a1, msl(co, 2, TT), smalls["convw_s"][:, co * 4 + 1:co * 4 + 2],
                                 a0, OP.mult, OP.add), inc="sdve",
                      reads=[K_("mixd", co), K_("mixh", co), K_("ca0",)], writes=[K_("tmpA",)])
                em.op("dve", stt(a0, msl(co, 3, TT), smalls["convw_s"][:, co * 4 + 2:co * 4 + 3],
                                 a1, OP.mult, OP.add), inc="sdve",
                      reads=[K_("mixd", co), K_("tmpA",)], writes=[K_("ca0",)])
                t_c4 = em.op("dve", stt(a1, msl(co, 4, TT),
                                        smalls["convw_s"][:, co * 4 + 3:co * 4 + 4],
                                        a0, OP.mult, OP.add), inc="sdve",
                             reads=[K_("mixd", co), K_("ca0",)], writes=[K_("tmpA",)])
                t_s = em.op("act", act(qsl(co), a1, AF.Silu,
                                       bias=smalls["convb_s"][:, co:co + 1]),
                            waits=[t_c4], inc="sact",
                            reads=[K_("tmpA",)], writes=[K_("qkv", co)])
                tok_conv.append(t_s)
            dump("mixed", qkvbuf[:], [tok_conv[-1]], keys=[K_("qkv", i) for i in range(N_QKV)])

            tok_qn = []
            for j in range(16):
                is_q = j < 8
                par = j % 2
                sqv = sqtmp[:, par * TT:(par + 1) * TT]
                t_sq = em.op("act", act(sqv, qsl(j), AF.Square),
                             waits=[tok_conv[j]], inc="sact",
                             reads=[K_("qkv", j)], writes=[K_("sq", par)])
                bk = fb.get()
                t_mm = em.op("pe", mm(bk["ap"][0:1, :], cb["onescol_b"][:], sqv),
                             waits=[bk["rtok"], t_sq], inc="spe", reads=[K_("sq", par)])
                t_l = em.op("act", act(rows[0:1, 0:512], bk["ap"][0:1, :], AF.Ln,
                                       bias=cf["cvec"][0:1, 1:2]),
                            waits=[t_mm], inc="sact", writes=[K_("rows", 0)])
                bk["rtok"] = t_l
                bias = cf["cvec"][0:1, 3:4] if is_q else None
                t_r = em.op("act", act(rows[0:1, 512:1024], rows[0:1, 0:512], AF.Exp,
                                       scale=-0.5, bias=bias), inc="sact",
                            reads=[K_("rows", 0)], writes=[K_("rows", 1)])
                bk2 = fb.get()
                t_b = em.op("pe", mm(bk2["ap"], cf["onesrow_f"][:], rows[0:1, 512:1024]),
                            waits=[bk2["rtok"], t_r], inc="spe", reads=[K_("rows", 1)])
                t_ap = em.op("dve", tt(mdata(j), qsl(j), bk2["ap"], OP.mult),
                             waits=[t_b], inc="sdve",
                             reads=[K_("qkv", j)], writes=[K_("mixd", j)])
                bk2["rtok"] = t_ap
                tok_qn.append(t_ap)
            if "qn" in dbg and tile == _DBG_TILE:
                for j in range(16):
                    dump("qn", mdata(j), [tok_qn[j]], keys=[K_("mixd", j)],
                         dst=dbg_t["qn"].ap()[:, j * TT:(j + 1) * TT])

            tok_zgw = []
            for co in range(N_Z):
                par = wload([(sc_z.ap()[co * 128:(co + 1) * 128, :], 0, D)],
                            waits_extra=[pre_tok["z"]])
                bk = fb.get()
                for ci in range(NCI):
                    w = [bk["rtok"], wload_tok[par], tok_ln1[ci]] if ci == 0 else [tok_ln1[ci]]
                    em.op("pe", mm(bk["ap"], wsl(par, ci * 128, 128), lnsl(ci),
                                   start=(ci == 0), stop=(ci == NCI - 1)),
                          waits=w, inc="spe" if ci == NCI - 1 else None)
                tok_mm = ("spe", em.cnt["spe"])
                whalf_tok[par] = tok_mm
                for ci in range(NCI):
                    em.rd(K_("ln", ci), tok_mm)
                t_s = em.op("act", act(tmpA[:], bk["ap"], AF.Silu,
                                       bias=smalls["bz_s"][:, co:co + 1]),
                            waits=[tok_mm], inc="sact", writes=[K_("tmpA",)])
                bk["rtok"] = t_s
                t_g_ = em.op("dve", ts(zsl(co), tmpA[:], smalls["normw_s"][:, 0:1], OP.mult),
                             waits=[t_s], inc="sdve",
                             reads=[K_("tmpA",)], writes=[K_("zg", co)])
                tok_zgw.append(t_g_)

            if "zgw" in dbg and tile == _DBG_TILE:
                dump("zgw", zgwbuf[:], [tok_zgw[-1]], keys=[K_("zg", i) for i in range(NCI)])
            bkA2 = fb.get()
            for ci in range(NCI):
                w = [bkA2["rtok"], pre_tok["ab"], tok_ln1[ci]] if ci == 0 else [tok_ln1[ci]]
                em.op("pe", mm(bkA2["ap"][0:16, :], abw[:, ci * 32:ci * 32 + 16], lnsl(ci),
                               start=(ci == 0), stop=(ci == NCI - 1)),
                      waits=w, inc="spe" if ci == NCI - 1 else None)
            tok_a2 = ("spe", em.cnt["spe"])
            bkB2 = fb.get()
            for ci in range(NCI):
                w = [bkB2["rtok"]] if ci == 0 else []
                em.op("pe", mm(bkB2["ap"][0:16, :], abw[:, ci * 32 + 16:ci * 32 + 32], lnsl(ci),
                               start=(ci == 0), stop=(ci == NCI - 1)),
                      waits=w, inc="spe" if ci == NCI - 1 else None)
            tok_ab = ("spe", em.cnt["spe"])
            for ci in range(NCI):
                em.rd(K_("ln", ci), tok_ab)
            psa = bkA2["ap"][0:16, :]
            psb = bkB2["ap"][0:16, :]
            sA = tmpA[0:16, :]
            sB = convacc[0:16, 0:TT]
            em.op("act", act(sA, psa, AF.Abs, bias=dtbeff),
                  waits=[tok_a2], inc="sact", writes=[K_("tmpA",)])
            em.op("act", act(sB, sA, AF.Exp, scale=-1.0), inc="sact",
                  reads=[K_("tmpA",)], writes=[K_("ca0",)])
            em.op("act", act(sA, sB, AF.Ln, bias=cf["cvec"][0:16, 2:3]), inc="sact",
                  reads=[K_("ca0",)], writes=[K_("tmpA",)])
            t_g1 = em.op("act", act(sB, psa, AF.Relu, bias=dtbeff), inc="sact",
                         writes=[K_("ca0",)])
            t_g2 = em.op("dve", tt(gfm[:], sA, sB, OP.add),
                         waits=[t_g1], inc="sdve",
                         reads=[K_("tmpA",), K_("ca0",)], writes=[K_("gfm",)])
            tok_g = em.op("dve", ts(gfm[:], gfm[:], negea, OP.mult), inc="sdve",
                          writes=[K_("gfm",)])
            bkA2["rtok"] = t_g1
            em.op("act", act(sA, psb, AF.Abs, bias=smalls["bb_s"][:, 0:1]),
                  waits=[tok_ab, t_g2], inc="sact", writes=[K_("tmpA",)])
            em.op("act", act(sB, sA, AF.Exp, scale=-1.0), inc="sact",
                  reads=[K_("tmpA",)], writes=[K_("ca0",)])
            em.op("act", act(sA, sB, AF.Ln, bias=cf["cvec"][0:16, 2:3]), inc="sact",
                  reads=[K_("ca0",)], writes=[K_("tmpA",)])
            t_b1 = em.op("act", act(sB, psb, AF.Relu, scale=-1.0, bias=negbb),
                         inc="sact", writes=[K_("ca0",)])
            t_b2 = em.op("dve", tt(lbfm[:], sA, sB, OP.add),
                         waits=[t_b1], inc="sdve",
                         reads=[K_("tmpA",), K_("ca0",)], writes=[K_("lbfm",)])
            tok_lb = em.op("dve", ts(lbfm[:], lbfm[:], -1.0, OP.mult), inc="sdve",
                           writes=[K_("lbfm",)])
            tok_be = em.op("act", act(befm[:], psb, AF.Sigmoid,
                                      bias=smalls["bb_s"][:, 0:1]),
                           waits=[tok_ab], inc="sact", writes=[K_("befm",)])
            bkB2["rtok"] = tok_be
            if "gclb" in dbg and tile == _DBG_TILE:
                dump("gclb", gfm[:], [tok_g], keys=[K_("gfm",)], dst=dbg_t["gclb"].ap()[:, 0:TT])
                dump("gclb", lbfm[:], [tok_lb], keys=[K_("lbfm",)], dst=dbg_t["gclb"].ap()[:, TT:2 * TT])
                dump("gclb", befm[:], [tok_be], keys=[K_("befm",)], dst=dbg_t["gclb"].ap()[:, 2 * TT:3 * TT])

            if seq_start:
                w_pre = [tok_prelude_head, pre_cast[0], pre_cast[1]] \
                    if tile == 0 else []
                t_ms1 = em.op("dve", lambda v: v.memset(state[:], 0.0),
                              waits=w_pre + [tok_state_cast[HV - 1]] + tok_state_new,
                              inc="sdve", writes=[K_("state",)])
                t_ms2 = em.op("act", act(statebf[:], state[:], AF.Copy),
                              waits=[t_ms1], inc="sact",
                              reads=[K_("state",)], writes=[K_("statebf",)])
                tok_state_cast = [t_ms2] * HV
                tok_state_new = [t_ms1] * HV

            prepres = {}
            pairtok = {}

            def prep_gen(c):
                par = c % 2
                c0 = c * 128
                bkp = fb.get()
                t_tr = pe_drained(trans(bkp["ap"][:, 0:16], gfm[:, c0:c0 + 128],
                                        cf["identf"][0:16, 0:16]),
                                  waits=[bkp["rtok"], tok_g], reads=[K_("gfm",)])
                t_ev = em.op("dve", cp(ct(par, 0), bkp["ap"][:, 0:16]), waits=[t_tr],
                             inc="sdve", writes=[K_("ct", par, 0)])
                bkp["rtok"] = t_ev
                yield
                bkc = fb.get()
                t_cs = pe_drained(mm(bkc["ap"][:, 0:16], cf["maskUf"][:], ct(par, 0)),
                                  waits=[bkc["rtok"], t_ev], reads=[K_("ct", par, 0)])
                t_cse = em.op("dve", cp(ct(par, 1), bkc["ap"][:, 0:16]), waits=[t_cs],
                              inc="sdve", writes=[K_("ct", par, 1)])
                bkc["rtok"] = t_cse
                yield
                bkl = fb.get()
                t_tl = pe_drained(trans(bkl["ap"][:, 0:16], lbfm[:, c0:c0 + 128],
                                        cf["identf"][0:16, 0:16]),
                                  waits=[bkl["rtok"], tok_lb], reads=[K_("lbfm",)])
                t_le = em.op("dve", cp(ct(par, 2), bkl["ap"][:, 0:16]), waits=[t_tl],
                             inc="sdve", writes=[K_("ct", par, 2)])
                bkl["rtok"] = t_le
                t_phi = em.op("dve", tt(ct(par, 3), ct(par, 1), ct(par, 2), OP.add),
                              waits=[t_cse, t_le], inc="sdve",
                              reads=[K_("ct", par, 1), K_("ct", par, 2)],
                              writes=[K_("ct", par, 3)])
                yield
                bkb = fb.get()
                t_tb = pe_drained(trans(bkb["ap"][:, 0:16], befm[:, c0:c0 + 128],
                                        cf["identf"][0:16, 0:16]),
                                  waits=[bkb["rtok"], tok_be], reads=[K_("befm",)])
                t_bev = em.op("dve", cp(ct(par, 4), bkb["ap"][:, 0:16]), waits=[t_tb],
                              inc="sdve", writes=[K_("ct", par, 4)])
                bkb["rtok"] = t_bev
                t_bc = em.op("act", act(ct(par, 5), ct(par, 1), AF.Exp), waits=[t_cse],
                             inc="sact", reads=[K_("ct", par, 1)], writes=[K_("ct", par, 5)])
                t_nb = em.op("dve", ts(ct(par, 6), ct(par, 5), -1.0, OP.mult),
                             waits=[t_bc], inc="sdve",
                             reads=[K_("ct", par, 5)], writes=[K_("ct", par, 6)])
                yield
                bkt = fb.get()
                t_ctr0 = pe_drained(trans(bkt["ap"][0:16, 0:128], ct(par, 1), cf["identf"][:]),
                                    waits=[bkt["rtok"], t_cse], reads=[K_("ct", par, 1)])
                t_c127 = em.op("dve", cp(cs127[:, par:par + 1], bkt["ap"][0:16, 127:128]),
                               waits=[t_ctr0], inc="sdve", writes=[K_("cs127", par)])
                bkt["rtok"] = t_c127
                t_bc16 = em.op("act", act(bc16[:, par:par + 1], cs127[:, par:par + 1], AF.Exp),
                               waits=[t_c127], inc="sact",
                               reads=[K_("cs127", par)], writes=[K_("bc16", par)])
                t_diag = em.op("dve", ts(diagbc[:, par * 16:(par + 1) * 16], cf["ident16f"][:],
                                         bc16[:, par:par + 1], OP.mult),
                               waits=[t_bc16], inc="sdve",
                               reads=[K_("bc16", par)], writes=[K_("diag", par)])
                yield
                bkr = fb.get()
                t_br = pe_drained(mm(bkr["ap"][:, 0:16], cf["ones16f"][:],
                                     diagbc[:, par * 16:(par + 1) * 16]),
                                  waits=[bkr["rtok"], t_diag], reads=[K_("diag", par)])
                t_bre = em.op("dve", cp(ct(par, 7), bkr["ap"][:, 0:16]), waits=[t_br],
                              inc="sdve", writes=[K_("ct", par, 7)])
                bkr["rtok"] = t_bre
                if "sct" in dbg and tile == _DBG_TILE and c == 0:
                    dump("sct", ctile[:, 0:128], [t_bre, t_phi, t_nb, t_bev],
                         keys=[K_("ct", 0, s) for s in range(8)])
                if "cs0" in dbg and tile == _DBG_TILE and c == 0:
                    dump("cs0", ct(par, 1), [t_cse], keys=[K_("ct", par, 1)])
                prepres[c] = dict(par=par, t_phi=t_phi, t_cse=t_cse, t_nb=t_nb,
                                  t_bev=t_bev, t_bc=t_bc, t_bre=t_bre)

            def pair_gen(c, kp):
                while c not in prepres:
                    yield
                gp = c * HK + kp
                while gp >= NIP and (gp - NIP) not in inst_done:
                    yield
                P = prepres[c]
                par = P["par"]
                c0 = c * 128
                t_phi, t_cse, t_nb = P["t_phi"], P["t_cse"], P["t_nb"]
                t_bev, t_bc, t_bre = P["t_bev"], P["t_bc"], P["t_bre"]
                h0 = 2 * kp
                ip = gp % NIP
                hs = [h0, h0 + 1]

                def ccol(slot, hh, _p=par):
                    return ct(_p, slot)[:, hh:hh + 1]

                def pr(nm, sub=None, w=128):
                    base = ip * 256 + (0 if sub is None else sub * 128)
                    return scn[nm][:, base:base + (256 if sub is None else w)]

                def prf(nm, sub=None, w=128):
                    base = ip * 256 + (0 if sub is None else sub * 128)
                    return scf[nm][:, base:base + (256 if sub is None else w)]

                KALIAS = {"E": "Y1", "Vtm": "RW0", "CN": "TO"}
                KP = lambda nm: K_(KALIAS.get(nm, nm), ip)
                Qf = mdata(kp, c0, 128)
                Kf = mdata(8 + kp, c0, 128)
                bk1 = fb.get()
                t1 = em.op("pe", mm(bk1["ap"][:, 0:128], Kf, Kf),
                           waits=[bk1["rtok"], tok_qn[8 + kp]], inc="spe",
                           reads=[K_("mixd", 8 + kp)])
                tS1 = em.op("dve", tt(sfsl2("SkkMf", ip), bk1["ap"][:, 0:128],
                                      cb["maskSnb"][:], OP.mult),
                            waits=[t1], inc="sdve", writes=[K_("SkkM", ip)])
                bk1["rtok"] = tS1
                yield
                bk2 = fb.get()
                t2 = em.op("pe", mm(bk2["ap"][:, 0:128], Kf, Qf),
                           waits=[bk2["rtok"], tok_qn[kp], tok_qn[8 + kp]], inc="spe",
                           reads=[K_("mixd", kp), K_("mixd", 8 + kp)])
                tS2 = em.op("dve", tt(sfsl2("SkqMf", ip), bk2["ap"][:, 0:128],
                                      cb["maskIb"][:], OP.mult),
                            waits=[t2], inc="sdve", writes=[K_("SkqM", ip)])
                bk2["rtok"] = tS2
                yield
                bkT = bb.get()
                t3 = em.op("pe", trans(pxsl(bkT, 0), Kf, cb["identb"][:]),
                           waits=[bkT["rtok"], tok_qn[8 + kp]], inc="spe",
                           reads=[K_("mixd", 8 + kp)])
                tKtm = em.op("act", act(sfsl2b("Ktm", ip), pxsl(bkT, 0), AF.Copy),
                             waits=[t3], inc="sact", writes=[K_("Ktm", ip)])
                bkT["rtok"] = tKtm
                yield
                tG = []
                tMb = []
                tKh = []
                for (cslot, Ename, tcol) in [(3, "Es", t_phi), (1, "Ei", t_cse)]:
                    bkB = fb.get()
                    tb0a = em.op("act", act(scf["B0"][:, ip * 128:(ip + 1) * 128],
                                            zero128[:], AF.Identity,
                                            bias=ccol(cslot, hs[0])),
                                 waits=[tcol], inc="sact",
                                 reads=[K_("ct", par, cslot)], writes=[K_("B0", ip)])
                    tt1 = em.op("pe", trans(bkB["ap"][:, 0:128],
                                            scf["B0"][:, ip * 128:(ip + 1) * 128],
                                            cf["identf"][:]),
                                waits=[bkB["rtok"], tb0a], inc="spe", reads=[K_("B0", ip)])
                    tb0b = em.op("act", act(scf["B0"][:, ip * 128:(ip + 1) * 128],
                                            zero128[:], AF.Identity,
                                            bias=ccol(cslot, hs[1])),
                                 waits=[tcol], inc="sact",
                                 reads=[K_("ct", par, cslot)], writes=[K_("B0", ip)])
                    tt2 = em.op("pe", trans(bkB["ap"][:, 128:256],
                                            scf["B0"][:, ip * 128:(ip + 1) * 128],
                                            cf["identf"][:]),
                                waits=[tb0b], inc="spe", reads=[K_("B0", ip)])
                    te1 = em.op("dve", ts(prf("TO", 0), bkB["ap"][:, 0:128],
                                          ccol(1, hs[0]), OP.subtract, 0.0, OP.min),
                                waits=[tt2, t_cse], inc="sdve",
                                reads=[K_("ct", par, 1)], writes=[KP("TO")])
                    te2 = em.op("dve", ts(prf("TO", 1), bkB["ap"][:, 128:256],
                                          ccol(1, hs[1]), OP.subtract, 0.0, OP.min),
                                waits=[tt2], inc="sdve",
                                reads=[K_("ct", par, 1)], writes=[KP("TO")])
                    bkB["rtok"] = te2
                    tE = em.op("act", act(pr("E"), prf("TO"), AF.Exp),
                               waits=[te1, te2], inc="sact",
                               reads=[KP("TO")], writes=[KP("E")])
                    yield
                    if Ename == "Es":
                        for s in (0, 1):
                            tG.append(em.op("pool", tt(pr("G", s), sfsl2("SkkMf", ip),
                                                       pr("E", s), OP.mult),
                                            waits=[tS1, tE], inc="spool",
                                            reads=[K_("SkkM", ip), KP("E")],
                                            writes=[KP("G")]))
                    else:
                        tEi = tE
                        for s in (0, 1):
                            tMb.append(em.op("pool", tt(pr("Mb", s), sfsl2("SkqMf", ip),
                                                        pr("E", s), OP.mult),
                                             waits=[tS2, tE], inc="spool",
                                             reads=[K_("SkqM", ip), KP("E")],
                                             writes=[KP("Mb")]))
                            tb = em.op("dve", cp(bcb[:, 2 * ip + s:2 * ip + s + 1],
                                                 pr("E", s)[:, 127:128]),
                                       waits=[tE], inc="sdve",
                                       reads=[KP("E")], writes=[K_("bcb", ip, s)])
                            tKh.append(em.op("pool", ts(pr("Khat", s), sfsl2b("Ktm", ip),
                                                        bcb[:, 2 * ip + s:2 * ip + s + 1],
                                                        OP.mult),
                                             waits=[tKtm, tb], inc="spool",
                                             reads=[K_("Ktm", ip), K_("bcb", ip, s)],
                                             writes=[KP("Khat")]))
                yield
                bkV = bb.get()
                tv1 = em.op("pe", trans(pxsl(bkV, 0), qsl(16 + hs[0], c0, 128),
                                        cb["identb"][:]),
                            waits=[bkV["rtok"], tok_conv[16 + hs[0]]], inc="spe",
                            reads=[K_("qkv", 16 + hs[0])])
                tv2 = em.op("pe", trans(pxsl(bkV, 1), qsl(16 + hs[1], c0, 128),
                                        cb["identb"][:]),
                            waits=[tok_conv[16 + hs[1]]], inc="spe",
                            reads=[K_("qkv", 16 + hs[1])])
                tVe = em.op("act", act(pr("Vtm"), bkV["ap"][:, 0:256], AF.Copy),
                            waits=[tv2], inc="sact", writes=[KP("Vtm")])
                bkV["rtok"] = tVe
                yield
                bkK = fb.get()
                tk1 = em.op("pe", mm(bkK["ap"][:, 0:128], Kf, sbsl(hs[0])),
                            waits=[bkK["rtok"], tok_state_cast[hs[0]], tok_qn[8 + kp]],
                            inc="spe", reads=[K_("mixd", 8 + kp), K_("statebf", hs[0])])
                tk2 = em.op("pe", mm(bkK["ap"][:, 128:256], Kf, sbsl(hs[1])),
                            waits=[tok_state_cast[hs[1]]], inc="spe",
                            reads=[K_("statebf", hs[1])])
                tr1 = em.op("dve", stt(prf("TO", 0), bkK["ap"][:, 0:128], ccol(6, hs[0]),
                                       pr("Vtm", 0), OP.mult, OP.add),
                            waits=[tk2, t_nb, tVe], inc="sdve",
                            reads=[K_("ct", par, 6), KP("Vtm")], writes=[KP("TO")])
                tr2 = em.op("dve", stt(prf("TO", 1), bkK["ap"][:, 128:256], ccol(6, hs[1]),
                                       pr("Vtm", 1), OP.mult, OP.add),
                            inc="sdve", reads=[K_("ct", par, 6), KP("Vtm")],
                            writes=[KP("TO")])
                bkK["rtok"] = tr2
                ty1 = em.op("dve", ts(pr("Y0", 0), prf("TO", 0), ccol(4, hs[0]), OP.mult),
                            waits=[t_bev], inc="sdve",
                            reads=[KP("TO"), K_("ct", par, 4)], writes=[KP("Y0")])
                t_y = em.op("dve", ts(pr("Y0", 1), prf("TO", 1), ccol(4, hs[1]), OP.mult),
                            inc="sdve", reads=[KP("TO"), K_("ct", par, 4)],
                            writes=[KP("Y0")])
                yield
                bkq = fb.get()
                tq1 = em.op("pe", mm(bkq["ap"][:, 0:128], Qf, sbsl(hs[0])),
                            waits=[bkq["rtok"], tok_state_cast[hs[0]], tok_qn[kp]],
                            inc="spe", reads=[K_("mixd", kp), K_("statebf", hs[0])])
                tq2 = em.op("pe", mm(bkq["ap"][:, 128:256], Qf, sbsl(hs[1])),
                            waits=[tok_state_cast[hs[1]]], inc="spe",
                            reads=[K_("statebf", hs[1])])
                tt1_ = em.op("dve", ts(prf("TO", 0), bkq["ap"][:, 0:128], ccol(5, hs[0]),
                                       OP.mult),
                             waits=[tq2, t_bc], inc="sdve",
                             reads=[K_("ct", par, 5)], writes=[KP("TO")])
                tt2_ = em.op("dve", ts(prf("TO", 1), bkq["ap"][:, 128:256], ccol(5, hs[1]),
                                       OP.mult),
                             inc="sdve", reads=[K_("ct", par, 5)], writes=[KP("TO")])
                bkq["rtok"] = tt2_
                yield
                bkA = bb.get()
                ta1 = em.op("pe", trans(pxsl(bkA, 0), pr("G", 0),
                                        cb["identb"][:]),
                            waits=[bkA["rtok"], tG[0]], inc="spe", reads=[KP("G")])
                ta2 = em.op("pe", trans(pxsl(bkA, 1), pr("G", 1), cb["identb"][:]),
                            waits=[tG[1]], inc="spe", reads=[KP("G")])
                tAe = em.op("act", act(pr("RW1"), bkA["ap"][:, 0:256], AF.Copy),
                            waits=[ta2], inc="sact", writes=[KP("RW1")])
                bkA["rtok"] = tAe
                yield
                ycur, ynext = "Y0", "Y1"
                apT0, apT1, apTok, apKey = pr("G", 0), pr("G", 1), tG[1], KP("G")
                rwT0, rwT1, rwTok, rwKey = pr("RW1", 0), pr("RW1", 1), tAe, KP("RW1")
                rot = [("P2t", "RW0"), ("P4t", "P8t")]
                for lv in range(7):
                    bks = fb.get()
                    em.op("pe", mm(bks["ap"][:, 0:128], apT0, pr(ycur, 0),
                                   start=True, stop=False),
                          waits=[bks["rtok"], apTok, t_y], inc="spe",
                          reads=[apKey, KP(ycur)])
                    em.op("pe", mm(bks["ap"][:, 0:128], cb["identb"][:], pr(ycur, 0),
                                   start=False, stop=True), inc="spe", reads=[KP(ycur)])
                    em.op("pe", mm(bks["ap"][:, 128:256], apT1, pr(ycur, 1),
                                   start=True, stop=False), inc="spe",
                          reads=[apKey, KP(ycur)])
                    ts2 = em.op("pe", mm(bks["ap"][:, 128:256], cb["identb"][:],
                                         pr(ycur, 1), start=False, stop=True),
                                inc="spe", reads=[KP(ycur)])
                    eng_y = "dve" if lv % 2 == 0 else "act"
                    t_y = em.op(eng_y,
                                (cp(pr(ynext), bks["ap"][:, 0:256]) if eng_y == "dve"
                                 else act(pr(ynext), bks["ap"][:, 0:256], AF.Copy)),
                                waits=[ts2], inc="s" + eng_y, writes=[KP(ynext)])
                    bks["rtok"] = t_y
                    ycur, ynext = ynext, ycur
                    if lv < 6:
                        ptn, prn = rot[lv % 2]
                        bkp2 = fb.get()
                        em.op("pe", mm(bkp2["ap"][:, 0:128], rwT0, apT0),
                              waits=[bkp2["rtok"], rwTok, apTok], inc="spe",
                              reads=[rwKey, apKey])
                        tb2 = em.op("pe", mm(bkp2["ap"][:, 128:256], rwT1, apT1),
                                    inc="spe", reads=[rwKey, apKey])
                        eng_t = "act" if lv % 2 == 0 else "dve"
                        tev1 = em.op(eng_t,
                                     (act(pr(ptn), bkp2["ap"][:, 0:256], AF.Copy)
                                      if eng_t == "act"
                                      else cp(pr(ptn), bkp2["ap"][:, 0:256])),
                                     waits=[tb2], inc="s" + eng_t, writes=[KP(ptn)])
                        bkp2["rtok"] = tev1
                        if lv < 5:
                            bkp3 = fb.get()
                            em.op("pe", mm(bkp3["ap"][:, 0:128], apT0, rwT0),
                                  waits=[bkp3["rtok"], rwTok, apTok], inc="spe",
                                  reads=[rwKey, apKey])
                            tb4 = em.op("pe", mm(bkp3["ap"][:, 128:256], apT1, rwT1),
                                        inc="spe", reads=[rwKey, apKey])
                            eng_r = "dve" if lv % 2 == 0 else "act"
                            tev2 = em.op(eng_r,
                                         (cp(pr(prn), bkp3["ap"][:, 0:256])
                                          if eng_r == "dve"
                                          else act(pr(prn), bkp3["ap"][:, 0:256],
                                                   AF.Copy)),
                                         waits=[tb4], inc="s" + eng_r,
                                         writes=[KP(prn)])
                            bkp3["rtok"] = tev2
                            rwT0, rwT1, rwTok, rwKey = (pr(prn, 0), pr(prn, 1),
                                                        tev2, KP(prn))
                        apT0, apT1, apTok, apKey = (pr(ptn, 0), pr(ptn, 1),
                                                    tev1, KP(ptn))
                    yield
                Ukey = KP(ycur)
                t_U = t_y
                bkm = fb.get()
                tm1 = em.op("pe", mm(bkm["ap"][:, 0:128], pr("Mb", 0), pr(ycur, 0)),
                            waits=[bkm["rtok"], tMb[0], t_U], inc="spe",
                            reads=[KP("Mb"), Ukey])
                tm2 = em.op("pe", mm(bkm["ap"][:, 128:256], pr("Mb", 1), pr(ycur, 1)),
                            waits=[tMb[1]], inc="spe", reads=[KP("Mb"), Ukey])
                t_O = em.op("dve", tt(prf("O"), prf("TO"), bkm["ap"][:, 0:256], OP.add),
                            waits=[tm2, tt2_], inc="sdve",
                            reads=[KP("TO")], writes=[KP("O")])
                bkm["rtok"] = t_O
                yield
                bksn = fb.get()
                tn1 = em.op("pe", mm(bksn["ap"][:, 0:128], pr("Khat", 0), pr(ycur, 0)),
                            waits=[bksn["rtok"], tKh[0], t_U], inc="spe",
                            reads=[KP("Khat"), Ukey])
                tn2 = em.op("pe", mm(bksn["ap"][:, 128:256], pr("Khat", 1), pr(ycur, 1)),
                            waits=[tKh[1]], inc="spe", reads=[KP("Khat"), Ukey])
                ts1_ = em.op("dve", stt(ssl(hs[0]), ssl(hs[0]), ccol(7, hs[0]),
                                        bksn["ap"][:, 0:128], OP.mult, OP.add),
                             waits=[tn2, t_bre, tok_state_new[hs[0]],
                                    tok_state_cast[hs[0]]],
                             inc="sdve",
                             reads=[K_("ct", par, 7)], writes=[K_("stateh", hs[0])])
                ts2_ = em.op("dve", stt(ssl(hs[1]), ssl(hs[1]), ccol(7, hs[1]),
                                        bksn["ap"][:, 128:256], OP.mult, OP.add),
                             waits=[tok_state_new[hs[1]], tok_state_cast[hs[1]]],
                             inc="sdve",
                             reads=[K_("ct", par, 7)], writes=[K_("stateh", hs[1])])
                bksn["rtok"] = ts2_
                tok_state_new[hs[0]] = ts1_
                tok_state_new[hs[1]] = ts2_
                tcast = em.op("act", act(statebf[:, h0 * 128:(h0 + 2) * 128],
                                         state[:, h0 * 128:(h0 + 2) * 128], AF.Copy),
                              waits=[ts1_, ts2_], inc="sact",
                              reads=[K_("stateh", hs[0]), K_("stateh", hs[1])],
                              writes=[K_("statebf", hs[0]), K_("statebf", hs[1])])
                tok_state_cast[hs[0]] = tcast
                tok_state_cast[hs[1]] = tcast
                yield
                t_osq0 = em.op("act", act(prf("TO"), prf("O"), AF.Square),
                               waits=[t_O, tt2_], inc="sact",
                               reads=[KP("O")], writes=[KP("TO")])
                tvr = []
                for s in (0, 1):
                    tvr.append(em.op("dve",
                                     lambda v, _o=prf("TO", s),
                                     _d=varcol[:, 2 * ip + s:2 * ip + s + 1]:
                                     v.reduce_sum(_d, _o, axis=mybir.AxisListType.X),
                                     waits=[t_osq0], inc="sdve",
                                     reads=[KP("TO")], writes=[K_("var", ip, s)]))
                em.op("act", act(rstdcol[:, 2 * ip:2 * ip + 2],
                                 varcol[:, 2 * ip:2 * ip + 2], AF.Ln,
                                 scale=1.0 / 128.0, bias=cf["cvec"][:, 1:2]),
                      waits=tvr, inc="sact",
                      reads=[K_("var", ip, 0), K_("var", ip, 1)],
                      writes=[K_("rstd", ip)])
                t_rs = em.op("act", act(rstdcol[:, 2 * ip:2 * ip + 2],
                                        rstdcol[:, 2 * ip:2 * ip + 2],
                                        AF.Exp, scale=-0.5), inc="sact",
                             writes=[K_("rstd", ip)])
                tc1 = em.op("dve", ts(prf("CN", 0), prf("O", 0),
                                      rstdcol[:, 2 * ip:2 * ip + 1], OP.mult),
                            waits=[t_rs], inc="sdve",
                            reads=[KP("O"), K_("rstd", ip)], writes=[KP("CN")])
                tc2 = em.op("dve", ts(prf("CN", 1), prf("O", 1),
                                      rstdcol[:, 2 * ip + 1:2 * ip + 2], OP.mult),
                            inc="sdve", reads=[KP("O"), K_("rstd", ip)],
                            writes=[KP("CN")])
                yield
                bkct = fb.get()
                tct1 = em.op("pe", trans(bkct["ap"][:, 0:128], prf("CN", 0),
                                         cf["identf"][:]),
                             waits=[bkct["rtok"], tc1, tc2], inc="spe", reads=[KP("CN")])
                tct2 = em.op("pe", trans(bkct["ap"][:, 128:256], prf("CN", 1),
                                         cf["identf"][:]), inc="spe", reads=[KP("CN")])
                tg1 = em.op("dve", tt(lnsl(hs[0], c0, 128), bkct["ap"][:, 0:128],
                                      zsl(hs[0], c0, 128), OP.mult),
                            waits=[tct2, tok_zgw[hs[0]]], inc="sdve",
                            reads=[K_("zg", hs[0])], writes=[K_("ln", hs[0])])
                tg2 = em.op("dve", tt(lnsl(hs[1], c0, 128), bkct["ap"][:, 128:256],
                                      zsl(hs[1], c0, 128), OP.mult),
                            waits=[tok_zgw[hs[1]]], inc="sdve",
                            reads=[K_("zg", hs[1])], writes=[K_("ln", hs[1])])
                bkct["rtok"] = tg2
                inst_done.add(gp)

            inst_done = set()
            gens = deque()
            if tile == 0:
                gens.append(prelude_tail_gen())
            for c in range(NCH):
                gens.append(prep_gen(c))
                for kp in range(HK):
                    gens.append(pair_gen(c, kp))
            active = deque()
            while gens or active:
                while len(active) < _SCAN_DEPTH and gens:
                    active.append(gens.popleft())
                g0 = active.popleft()
                try:
                    next(g0)
                    active.append(g0)
                except StopIteration:
                    pass
            tok_core = ("sdve", em.cnt["sdve"])
            dump("core", lnbuf[:], [tok_core], keys=[K_("ln", i) for i in range(NCI)])

            tok_attn = []
            for co in range(N_WO):
                par = wload([(sc_wo.ap()[co * 128:(co + 1) * 128, :], 0, D)],
                            waits_extra=[pre_tok["wo"]])
                bk = fb.get()
                for ci in range(NCI):
                    w = [bk["rtok"], wload_tok[par], tok_core] if ci == 0 else []
                    em.op("pe", mm(bk["ap"], wsl(par, ci * 128, 128), lnsl(ci),
                                   start=(ci == 0), stop=(ci == NCI - 1)),
                          waits=w, inc="spe" if ci == NCI - 1 else None)
                tok_mm = ("spe", em.cnt["spe"])
                whalf_tok[par] = tok_mm
                for ci in range(NCI):
                    em.rd(K_("ln", ci), tok_mm)
                t_ev = em.op("act", act(qsl(co), bk["ap"], AF.Identity,
                                        bias=smalls["bout_s"][:, co:co + 1]),
                             waits=[tok_mm], inc="sact", writes=[K_("qkv", co)])
                bk["rtok"] = t_ev
                tok_attn.append(t_ev)
            dump("attn", qkvbuf[:, 0:NCI * TT], [tok_attn[-1]], keys=[K_("qkv", i) for i in range(NCI)])

            def ln2_writer(ci, apR, apM, tR, tM, _ta=tok_attn):
                em.op("dve", tt(tmpA[:], qsl(ci), apR, OP.mult),
                      waits=[tR, _ta[ci]], inc="sdve",
                      reads=[K_("qkv", ci)], writes=[K_("tmpA",)])
                em.op("dve", tt(tmpA[:], tmpA[:], apM, OP.subtract), waits=[tM], inc="sdve",
                      writes=[K_("tmpA",)])
                return em.op("dve", tt(xsl(ci), xsl(ci), tmpA[:], OP.add), inc="sdve",
                             reads=[K_("tmpA",)], writes=[K_("x", ci)])
            tok_x2 = emit_ln("bf16", qsl, lambda ci: K_("qkv", ci), tok_attn, ln2_writer)
            dump("x2", xbuf[:], [tok_x2[-1]], keys=[K_("x", i) for i in range(NCI)])

            def ln3_writer(ci, apR, apM, tR, tM, _tx=tok_x2):
                em.op("dve", tt(tmpA[:], xsl(ci), apR, OP.mult),
                      waits=[tR, _tx[ci]], inc="sdve",
                      reads=[K_("x", ci)], writes=[K_("tmpA",)])
                return em.op("dve", tt(lnsl(ci), tmpA[:], apM, OP.subtract),
                             waits=[tM], inc="sdve",
                             reads=[K_("tmpA",)], writes=[K_("ln", ci)])
            tok_ln3 = emit_ln("f32", xsl, lambda ci: K_("x", ci), tok_x2, ln3_writer)
            dump("ln3", lnbuf[:], [tok_ln3[-1]], keys=[K_("ln", i) for i in range(NCI)])

            def hkey(j):
                return K_("qkv", j) if j < 32 else K_("mixd", j - 32)

            def hslot(j, c0=0, w=TT):
                return qsl(j, c0, w) if j < 32 else mdata(j - 32, c0, w)

            tok_h = []
            for co in range(N_GU):
                par = wload([(sc_g.ap()[co * 128:(co + 1) * 128, :], 0, D),
                             (sc_u.ap()[co * 128:(co + 1) * 128, :], D, D)],
                            waits_extra=[pre_tok["g"], pre_tok["u"]])
                ldtok = wload_tok[par]
                bkG = fb.get()
                for ci in range(NCI):
                    w = [bkG["rtok"], ldtok, tok_ln3[ci]] if ci == 0 else [tok_ln3[ci]]
                    em.op("pe", mm(bkG["ap"], wsl(par, ci * 128, 128), lnsl(ci),
                                   start=(ci == 0), stop=(ci == NCI - 1)),
                          waits=w, inc="spe" if ci == NCI - 1 else None)
                tok_mmg = ("spe", em.cnt["spe"])
                bkU = fb.get()
                for ci in range(NCI):
                    w = [bkU["rtok"]] if ci == 0 else []
                    em.op("pe", mm(bkU["ap"], wsl(par, D + ci * 128, 128), lnsl(ci),
                                   start=(ci == 0), stop=(ci == NCI - 1)),
                          waits=w, inc="spe" if ci == NCI - 1 else None)
                tok_mmu = ("spe", em.cnt["spe"])
                whalf_tok[par] = tok_mmu
                for ci in range(NCI):
                    em.rd(K_("ln", ci), tok_mmu)
                t_sg = em.op("act", act(tmpA[:], bkG["ap"], AF.Silu,
                                        scale=smalls["gs_s"][:, co:co + 1]),
                             waits=[tok_mmg], inc="sact", writes=[K_("tmpA",)])
                bkG["rtok"] = t_sg
                spar = co % 2
                t_su_ = em.op("act", act(sqtmp[:, spar * TT:(spar + 1) * TT], bkU["ap"],
                                         AF.Identity, scale=smalls["us_s"][:, co:co + 1]),
                              waits=[tok_mmu], inc="sact", writes=[K_("sq", spar)])
                bkU["rtok"] = t_su_
                t_h = em.op("dve", tt(hslot(co), tmpA[:],
                                      sqtmp[:, spar * TT:(spar + 1) * TT], OP.mult),
                            waits=[t_sg, t_su_], inc="sdve",
                            reads=[K_("tmpA",), K_("sq", spar)], writes=[hkey(co)])
                tok_h.append(t_h)
            if "hbuf" in dbg and tile == _DBG_TILE:
                dump("hbuf", qkvbuf[:, 0:8 * TT], [tok_h[7]],
                     keys=[hkey(j) for j in range(8)])

            tok_ffn = []
            for co in range(N_DN):
                par = wload([(sc_d.ap()[co * 128:(co + 1) * 128, 0:32 * 128], 0, 32 * 128)],
                            waits_extra=[pre_tok["d"]])
                ld1 = wload_tok[par]
                ld2holder = []
                for ci in range(N_GU):
                    w = ([bkd["rtok"] if False else None] if False else [])
                    if ci == 0:
                        bk = fb.get()
                        w = [bk["rtok"], ld1] + tok_h
                    elif ci == 32:
                        w = [ld2holder[0]]
                    em.op("pe", mm(bk["ap"], wsl(par, (ci if ci < 32 else ci - 32) * 128, 128),
                                   hslot(ci),
                                   start=(ci == 0), stop=(ci == N_GU - 1)),
                          waits=w, inc="spe" if ci in (11, N_GU - 1) else None)
                    if ci == 11:
                        t_mm11 = ("spe", em.cnt["spe"])
                        ld2 = em.op("sync",
                                    dma(wstream[:, par * WHALF: par * WHALF + 12 * 128],
                                        sc_d.ap()[co * 128:(co + 1) * 128, 32 * 128:44 * 128]),
                                    waits=[t_mm11], inc="swd%d" % par, inc_n=16)
                        ld2holder.append(ld2)
                tok_mm = ("spe", em.cnt["spe"])
                whalf_tok[par] = tok_mm
                for ci in range(N_GU):
                    em.rd(hkey(ci), tok_mm)
                t_ev = em.op("act", act(mdata(12 + co), bk["ap"], AF.Identity,
                                        scale=smalls["ds_s"][:, co:co + 1]),
                             waits=[tok_mm], inc="sact", writes=[K_("mixd", 12 + co)])
                bk["rtok"] = t_ev
                tok_ffn.append(t_ev)

            def ln4_writer(ci, apR, apM, tR, tM, _tf=tok_ffn):
                em.op("dve", tt(tmpA[:], mdata(12 + ci), apR, OP.mult),
                      waits=[tR, _tf[ci]], inc="sdve",
                      reads=[K_("mixd", 12 + ci)], writes=[K_("tmpA",)])
                em.op("dve", tt(tmpA[:], tmpA[:], apM, OP.subtract), waits=[tM],
                      inc="sdve", writes=[K_("tmpA",)])
                return em.op("dve", tt(xsl(ci), xsl(ci), tmpA[:], OP.add), inc="sdve",
                             reads=[K_("tmpA",)], writes=[K_("x", ci)])

            tok_res4 = emit_ln("bf16", lambda ci: mdata(12 + ci),
                               lambda ci: K_("mixd", 12 + ci), tok_ffn, ln4_writer)
            t_flush = em.op("dve", lambda v: v.drain(), waits=[tok_res4[-1]], inc="sdve")
            for ci in range(NCI):
                em.op("act",
                      dma(yout[b].ap()[ci * 128:(ci + 1) * 128, tok0:tok0 + TT], xsl(ci)),
                      waits=[t_flush], inc="sod", inc_n=16)
                em.rd(K_("x", ci), ("sod", em.cnt["sod"]))
            t_of = em.op("act", None, waits=[("sod", em.cnt["sod"])], inc="sact")
            for ci in range(NCI):
                em.rd(K_("x", ci), t_of)

        em.op("pool", None, waits=[("sod", em.cnt["sod"]), ("sxd", em.cnt["sxd"]),
                                   ("swd0", em.cnt["swd0"]), ("swd1", em.cnt["swd1"]),
                                   ("sdbg", em.cnt["sdbg"]),
                                   ("spl0", em.cnt["spl0"]), ("spl1", em.cnt["spl1"])])
        em.emit(block, sems)

    return nc


def _host_stage(inputs):
    f32 = np.float32
    x = inputs["L0_attn_hidden_states"]
    common = {}
    common["wqkvT"] = np.ascontiguousarray(np.asarray(inputs["L0_attn_Wqkv"]).T.astype(f32))
    common["wzT"] = np.ascontiguousarray(np.asarray(inputs["L0_attn_Wz"]).T.astype(f32))
    wab = np.zeros((D, 32), f32)
    wab[:, 0:16] = np.asarray(inputs["L0_attn_Wa"]).T
    wab[:, 16:32] = np.asarray(inputs["L0_attn_Wb"]).T
    common["wabT"] = wab
    common["woutT"] = np.ascontiguousarray(np.asarray(inputs["L0_attn_Wout"]).T.astype(f32))
    common["wgT"] = np.ascontiguousarray(np.asarray(inputs["L0_ffn_gate_w"]).T.astype(f32))
    common["wuT"] = np.ascontiguousarray(np.asarray(inputs["L0_ffn_up_w"]).T.astype(f32))
    common["wdT"] = np.ascontiguousarray(np.asarray(inputs["L0_ffn_down_w"]).T.astype(f32))
    common["bqkv_s"] = np.ascontiguousarray(
        np.asarray(inputs["L0_attn_bqkv"]).reshape(32, 128).T.astype(f32))
    common["bz_s"] = np.ascontiguousarray(
        np.asarray(inputs["L0_attn_bz"]).reshape(16, 128).T.astype(f32))
    common["bout_s"] = np.ascontiguousarray(
        np.asarray(inputs["L0_attn_bout"]).reshape(16, 128).T.astype(f32))
    common["convw_s"] = np.ascontiguousarray(
        np.asarray(inputs["L0_attn_conv_w"]).reshape(32, 128, 4)
        .transpose(1, 0, 2).reshape(128, 128).astype(f32))
    common["convb_s"] = np.ascontiguousarray(
        np.asarray(inputs["L0_attn_conv_b"]).reshape(32, 128).T.astype(f32))
    common["gs_s"] = np.ascontiguousarray(
        np.asarray(inputs["L0_ffn_gate_s"]).reshape(44, 128).T.astype(f32))
    common["us_s"] = np.ascontiguousarray(
        np.asarray(inputs["L0_ffn_up_s"]).reshape(44, 128).T.astype(f32))
    common["ds_s"] = np.ascontiguousarray(
        np.asarray(inputs["L0_ffn_down_s"]).reshape(16, 128).T.astype(f32))
    common["normw_s"] = np.ascontiguousarray(np.asarray(inputs["L0_attn_norm_w"]).astype(f32)[:, None])
    common["ba_s"] = np.asarray(inputs["L0_attn_ba"]).astype(f32)[:, None]
    common["bb_s"] = np.asarray(inputs["L0_attn_bb"]).astype(f32)[:, None]
    common["dtb_s"] = np.ascontiguousarray(np.asarray(inputs["L0_attn_dt_bias"]).astype(f32)[:, None])
    common["alog_s"] = np.ascontiguousarray(np.asarray(inputs["L0_attn_A_log"]).astype(f32)[:, None])
    common["identf"] = np.eye(128, dtype=f32)
    common["maskUf"] = np.triu(np.ones((128, 128), f32))
    common["ones16f"] = np.ones((16, 128), f32)
    common["ident16f"] = np.eye(16, dtype=f32)
    common["onesrow_f"] = np.ones((1, 128), f32)
    common["onescol_f"] = np.ones((128, 1), f32)
    cvec = np.zeros((128, 8), f32)
    cvec[:, 0] = LNEPS
    cvec[:, 1] = 1e-6
    cvec[:, 2] = 1.0
    cvec[:, 3] = -0.5 * np.log(128.0)
    common["cvec"] = cvec
    common["identb"] = np.eye(128).astype(BF)
    common["maskSnb"] = (-np.triu(np.ones((128, 128), f32), 1)).astype(BF)
    common["maskIb"] = np.triu(np.ones((128, 128), f32)).astype(BF)
    common["onescol_b"] = np.ones((128, 1), f32).astype(BF)
    in_maps = []
    for core in range(8):
        m = dict(common)
        for bb_ in range(2):
            m[f"x{bb_}"] = np.ascontiguousarray(np.asarray(x[core * 2 + bb_]).T.astype(f32))
        in_maps.append(m)
    return in_maps


def kernel(**inputs):
    in_maps = _host_stage(inputs)
    nc = build_program(dbg=_DBG)
    res = run_bass_kernel_spmd(nc, in_maps, core_ids=list(range(8)))
    out = np.empty((16, 2048, D), np.float32)
    for core in range(8):
        for bb_ in range(2):
            out[core * 2 + bb_] = res.results[core][f"y{bb_}"].T
    return out


# version 12 (final)
# speedup vs baseline: 1.0406x; 1.0000x over previous
import numpy as np
import ml_dtypes
from collections import deque
from contextlib import ExitStack

import concourse.bass as bass
import concourse.mybir as mybir
from concourse.bass_utils import run_bass_kernel_spmd

F32 = mybir.dt.float32
BF16 = mybir.dt.bfloat16
AF = mybir.ActivationFunctionType
OP = mybir.AluOpType
BF = ml_dtypes.bfloat16

D = 2048
NCI = 16
TT = 512
NTILE = 8
C = 128
NCH = TT // C
HV = 16
HK = 8
CONVD = 4096
INTER = 5632
N_QKV = 32
N_Z = 16
N_WO = 16
N_GU = 44
N_DN = 16
MW = 516
MD = 4
LNEPS = 1e-5
RMSEPS = 1e-6
L2EPS = 1e-6

ENGINES = ("pe", "dve", "act", "pool", "sync")

_DBG = ()
_SCAN_DEPTH = 5
_YS = set(range(40))
_MAXTILE = NTILE
_PARANOID = True
_DBG_TILE = 0


class Em:

    def __init__(self):
        self.streams = {e: [] for e in ENGINES}
        self.cnt = {}
        self.war = {}

    def sem(self, *names):
        for n in names:
            self.cnt[n] = 0

    def op(self, eng, fn, waits=(), inc=None, inc_n=1, reads=(), writes=()):
        ws = list(waits)
        for key in writes:
            rec = self.war.get(key)
            if rec:
                ws.extend(rec["r"])
                if rec["w"] is not None:
                    ws.append(rec["w"])
        for key in reads:
            rec = self.war.get(key)
            if rec and rec["w"] is not None:
                ws.append(rec["w"])
        ws2 = []
        for w in ws:
            if w is None:
                continue
            s, v = w
            assert v <= self.cnt[s], f"wait on future token {s}={v} (planned {self.cnt[s]})"
            ws2.append((s, v))
        tok = None
        if inc is not None:
            self.cnt[inc] += inc_n
            tok = (inc, self.cnt[inc])
        assert tok is not None or (not reads and not writes), "hazard tracking needs a token"
        for key in writes:
            self.war[key] = {"w": tok, "r": []}
        for key in reads:
            self.war.setdefault(key, {"w": None, "r": []})["r"].append(tok)
        self.streams[eng].append((tuple(ws2), fn, inc, inc_n))
        return tok

    def rd(self, key, tok):
        self.war.setdefault(key, {"w": None, "r": []})["r"].append(tok)

    def emit(self, block, handles):
        sections = {"pe": block.tensor, "dve": block.vector, "act": block.scalar,
                    "pool": block.gpsimd, "sync": block.sync}
        ownsem = {"pe": "spe", "dve": "sdve", "act": "sact", "pool": "spool"}
        for ename in ENGINES:
            stream = self.streams[ename]
            own = ownsem.get(ename)

            def make(stream, own):
                def body(eng):
                    guard = {}
                    own_cnt = 0
                    for ws, fn, inc, inc_n in stream:
                        for s, v in ws:
                            if s == own and v <= own_cnt and not _PARANOID:
                                continue
                            if guard.get(s, 0) < v:
                                eng.wait_ge(handles[s], v)
                                guard[s] = v
                        if fn is not None:
                            ins = fn(eng)
                            if inc is not None:
                                if inc_n == 1 and ename in ("dve", "act", "pool"):
                                    eng.drain().then_inc(handles[inc], inc_n)
                                else:
                                    ins.then_inc(handles[inc], inc_n)
                        elif inc is not None:
                            eng.sem_inc(handles[inc], inc_n)
                        if inc == own:
                            own_cnt += inc_n
                return body
            sections[ename](make(stream, own))


class Banks:
    def __init__(self, aps):
        self.recs = [{"ap": ap, "rtok": None} for ap in aps]
        self.i = 0

    def get(self):
        r = self.recs[self.i % len(self.recs)]
        self.i += 1
        return r


DBG_SHAPES = {
    "ln1": ([128, NCI * TT], "f32"),
    "mixed": ([128, N_QKV * TT], "f32"),
    "qn": ([128, NCI * TT], "f32"),
    "gclb": ([16, 3 * TT], "f32"),
    "cs0": ([128, 16], "f32"),
    "core": ([128, NCI * TT], "f32"),
    "attn": ([128, NCI * TT], "f32"),
    "x2": ([128, NCI * TT], "f32"),
    "ln3": ([128, NCI * TT], "f32"),
    "hbuf": ([128, 8 * TT], "f32"),
    "zgw": ([128, NCI * TT], "bf16"),
    "sEs": ([128, 16 * 128], "bf16"),
    "sEi": ([128, 16 * 128], "bf16"),
    "sG": ([128, 16 * 128], "bf16"),
    "sY0": ([128, 16 * 128], "bf16"),
    "sU": ([128, 16 * 128], "bf16"),
    "sO": ([128, 16 * 128], "f32"),
    "svar": ([128, 4], "f32"),
    "sS1": ([128, 16 * 128], "f32"),
    "sKtm": ([128, 16 * 128], "bf16"),
    "sVtm": ([128, 16 * 128], "bf16"),
    "sct": ([128, 8 * 16], "f32"),
}


def build_program(dbg=()):
    nc = bass.Bass("TRN2", target_bir_lowering=False)

    xin = [nc.dram_tensor(f"x{b}", [D, 2048], F32, kind="ExternalInput") for b in range(2)]
    yout = [nc.dram_tensor(f"y{b}", [D, 2048], F32, kind="ExternalOutput") for b in range(2)]

    wqkvT = nc.dram_tensor("wqkvT", [D, CONVD], F32, kind="ExternalInput")
    wzT = nc.dram_tensor("wzT", [D, D], F32, kind="ExternalInput")
    wabT = nc.dram_tensor("wabT", [D, 32], F32, kind="ExternalInput")
    woutT = nc.dram_tensor("woutT", [D, D], F32, kind="ExternalInput")
    wgT = nc.dram_tensor("wgT", [D, INTER], F32, kind="ExternalInput")
    wuT = nc.dram_tensor("wuT", [D, INTER], F32, kind="ExternalInput")
    wdT = nc.dram_tensor("wdT", [INTER, D], F32, kind="ExternalInput")

    smalls_in = {}
    for nm, shp in [("bqkv_s", [128, 32]), ("bz_s", [128, 16]), ("bout_s", [128, 16]),
                    ("convw_s", [128, 128]), ("convb_s", [128, 32]),
                    ("gs_s", [128, 44]), ("us_s", [128, 44]), ("ds_s", [128, 16]),
                    ("normw_s", [128, 1]), ("ba_s", [16, 1]), ("bb_s", [16, 1]), ("dtb_s", [16, 1]),
                    ("alog_s", [16, 1])]:
        smalls_in[nm] = nc.dram_tensor(nm, shp, F32, kind="ExternalInput")
    consts_f = {}
    for nm, shp in [("identf", [128, 128]), ("maskUf", [128, 128]),
                    ("ones16f", [16, 128]), ("ident16f", [16, 16]),
                    ("onesrow_f", [1, 128]), ("onescol_f", [128, 1]),
                    ("cvec", [128, 8])]:
        consts_f[nm] = nc.dram_tensor(nm, shp, F32, kind="ExternalInput")
    consts_b = {}
    for nm, shp in [("identb", [128, 128]), ("maskSnb", [128, 128]),
                    ("maskIb", [128, 128]), ("onescol_b", [128, 1])]:
        consts_b[nm] = nc.dram_tensor(nm, shp, BF16, kind="ExternalInput")

    sc_qkv = nc.dram_tensor("sc_qkv", [N_QKV * 128, D], BF16)
    sc_z = nc.dram_tensor("sc_z", [N_Z * 128, D], BF16)
    sc_wo = nc.dram_tensor("sc_wo", [N_WO * 128, D], BF16)
    sc_g = nc.dram_tensor("sc_g", [N_GU * 128, D], BF16)
    sc_u = nc.dram_tensor("sc_u", [N_GU * 128, D], BF16)
    sc_d = nc.dram_tensor("sc_d", [N_DN * 128, INTER], BF16)

    build_program._outs = {"y0": yout[0], "y1": yout[1]}
    dbg_t = {nm: nc.dram_tensor("dbg_" + nm, DBG_SHAPES[nm][0],
                              F32 if DBG_SHAPES[nm][1] == "f32" else BF16,
                              kind="ExternalOutput")
             for nm in dbg}
    build_program._dbg = dbg_t

    em = Em()
    em.sem("ssm", "spl0", "spl1", "spd0", "spd1", "swd0", "swd1",
           "sxd", "sxdb", "sod", "sodb", "sdbg", "spe", "sdve", "sact", "spool")

    with ExitStack() as stk:
        e = stk.enter_context
        xbuf = e(nc.sbuf_tensor("xbuf", [128, NCI * TT], F32))
        lnbuf = e(nc.sbuf_tensor("lnbuf", [128, NCI * TT], BF16))
        mixbuf = e(nc.sbuf_tensor("mixbuf", [128, N_QKV * MW], BF16))
        qkvbuf = e(nc.sbuf_tensor("qkvbuf", [128, N_QKV * TT], BF16))
        zgwbuf = e(nc.sbuf_tensor("zgwbuf", [128, NCI * TT], BF16))
        state = e(nc.sbuf_tensor("state", [128, HV * 128], F32))
        statebf = e(nc.sbuf_tensor("statebf", [128, HV * 128], BF16))
        WHALF = 4096
        wstream = e(nc.sbuf_tensor("wstream", [128, 2 * WHALF], BF16))
        sqtmp = e(nc.sbuf_tensor("sqtmp", [128, 2 * TT], BF16))
        tmpA = e(nc.sbuf_tensor("tmpA", [128, TT], F32))
        convacc = e(nc.sbuf_tensor("convacc", [128, TT], F32))
        rows = e(nc.sbuf_tensor("rows", [1, 1536], F32))
        tailbuf = e(nc.sbuf_tensor("tailbuf", [128, N_QKV * 3], BF16))
        zero128 = e(nc.sbuf_tensor("zero128", [128, 128], BF16))
        gfm = e(nc.sbuf_tensor("gfm", [16, TT], F32))
        lbfm = e(nc.sbuf_tensor("lbfm", [16, TT], F32))
        befm = e(nc.sbuf_tensor("befm", [16, TT], F32))
        abw = e(nc.sbuf_tensor("abw", [128, NCI * 32], BF16))
        NIP = 3
        scnames = ["G", "P2t", "P4t", "P8t", "RW0",
                   "RW1", "Y0", "Y1", "Khat", "Mb"]
        scn = {nm: e(nc.sbuf_tensor("sc_" + nm, [128, NIP * 256], BF16)) for nm in scnames}
        scn["E"] = scn["Y1"]
        scn["Vtm"] = scn["RW0"]
        scf = {nm: e(nc.sbuf_tensor("sf_" + nm, [128, NIP * 256], F32))
               for nm in ["TO", "O"]}
        scf["CN"] = scf["TO"]
        scf["B0"] = e(nc.sbuf_tensor("sf_B0", [128, NIP * 128], F32))
        scp = {nm: e(nc.sbuf_tensor("sp_" + nm, [128, NIP * 128],
                                    F32 if nm != "Ktm" else BF16))
               for nm in ["SkkMf", "SkqMf", "Ktm"]}

        def sfsl2(nm, ipp):
            return scp[nm][:, ipp * 128:(ipp + 1) * 128]
        sfsl2b = sfsl2
        ctile = e(nc.sbuf_tensor("ctile", [128, 2 * 8 * 16], F32))
        cs127 = e(nc.sbuf_tensor("cs127", [16, 2], F32))
        bc16 = e(nc.sbuf_tensor("bc16", [16, 2], F32))
        diagbc = e(nc.sbuf_tensor("diagbc", [16, 2 * 16], F32))
        varcol = e(nc.sbuf_tensor("varcol", [128, 2 * NIP], F32))
        bcb = e(nc.sbuf_tensor("bcb", [128, 2 * NIP], F32))
        rstdcol = e(nc.sbuf_tensor("rstdcol", [128, 2 * NIP], F32))
        small16 = e(nc.sbuf_tensor("small16", [16, 4], F32))
        smalls = {nm: e(nc.sbuf_tensor("sb_" + nm, list(t.shape), F32))
                  for nm, t in smalls_in.items()}
        cf = {nm: e(nc.sbuf_tensor("sb_" + nm, list(t.shape), F32))
              for nm, t in consts_f.items()}
        cb = {nm: e(nc.sbuf_tensor("sb_" + nm, list(t.shape), BF16))
              for nm, t in consts_b.items()}

        pb = [e(nc.psum_tensor(f"pb{i}", [128, 512], F32)) for i in range(6)]
        pxb = [e(nc.psum_tensor(f"pxb{i}", [128, 1024], BF16)) for i in range(2)]

        sems = {n: e(nc.semaphore(n)) for n in
                ["ssm", "spl0", "spl1", "spd0", "spd1", "swd0", "swd1",
                 "sxd", "sxdb", "sod", "sodb", "sdbg", "spe", "sdve", "sact",
                 "spool"]}
        block = e(nc.Block())

        fb = Banks([t[:, 0:512] for t in pb])
        bb = Banks([t[:, 0:256] for t in pxb])

        def mm(out, lhsT, rhs, start=True, stop=True):
            return lambda t: t.matmul(out, lhsT, rhs, start=start, stop=stop)

        def trans(out, in_, ident):
            return lambda t: t.transpose(out, in_, ident)

        def act(out, in_, func, bias=None, scale=None, accum=None):
            kw = {}
            if bias is not None:
                kw["bias"] = bias
            if scale is not None:
                kw["scale"] = scale
            if accum is not None:
                kw["accum_out"] = accum
            return lambda s: s.activation(out, in_, func, **kw)

        def tt(out, a, b, op):
            return lambda v: v.tensor_tensor(out, a, b, op=op)

        def ts(out, a, s1, op0, s2=None, op1=None):
            if op1 is None:
                return lambda v: v.tensor_scalar(out, a, scalar1=s1, scalar2=None, op0=op0)
            return lambda v: v.tensor_scalar(out, a, scalar1=s1, scalar2=s2, op0=op0, op1=op1)

        def stt(out, in0, s, in1, op0, op1):
            return lambda v: v.scalar_tensor_tensor(out, in0, s, in1, op0=op0, op1=op1)

        def cp(out, in_):
            return lambda v: v.tensor_copy(out, in_)

        def dma(out, in_):
            return lambda q: q.dma_start(out, in_)

        def xsl(ci):
            return xbuf[:, ci * TT:(ci + 1) * TT]

        def lnsl(ci, c0=0, w=TT):
            return lnbuf[:, ci * TT + c0:ci * TT + c0 + w]

        def msl(co, off, w):
            return mixbuf[:, co * MW + off:co * MW + off + w]

        def mdata(co, c0=0, w=TT):
            return msl(co, MD + c0, w)

        def qsl(co, c0=0, w=TT):
            return qkvbuf[:, co * TT + c0:co * TT + c0 + w]

        def zsl(ci, c0=0, w=TT):
            return zgwbuf[:, ci * TT + c0:ci * TT + c0 + w]

        def ssl(h):
            return state[:, h * 128:(h + 1) * 128]

        def sbsl(h):
            return statebf[:, h * 128:(h + 1) * 128]

        def scsl(nm, par):
            return scn[nm][:, par * 128:(par + 1) * 128]

        def sfsl(nm, par):
            return scf[nm][:, par * 128:(par + 1) * 128]

        def ct(par, slot):
            base = par * 8 * 16 + slot * 16
            return ctile[:, base:base + 16]

        K_ = lambda *a: tuple(a)

        def pxsl(bk, s):
            return bk["ap"][:, s * 128:(s + 1) * 128]

        for nm, t in smalls_in.items():
            em.op("sync", dma(smalls[nm][:], t.ap()), inc="ssm", inc_n=16)
        for nm, t in consts_f.items():
            em.op("sync", dma(cf[nm][:], t.ap()), inc="ssm", inc_n=16)
        for nm, t in consts_b.items():
            em.op("sync", dma(cb[nm][:], t.ap()), inc="ssm", inc_n=16)
        tok_smalls = ("ssm", em.cnt["ssm"])

        em.op("dve", lambda v: v.memset(zero128[:], 0.0), inc="sdve")
        t_e = em.op("act", act(small16[:, 0:1], smalls["alog_s"][:, 0:1], AF.Exp),
                    waits=[tok_smalls], inc="sact")
        em.op("dve", ts(small16[:, 0:1], small16[:, 0:1], -1.0, OP.mult),
              waits=[t_e], inc="sdve")
        em.op("dve", tt(small16[:, 1:2], smalls["ba_s"][:, 0:1],
                        smalls["dtb_s"][:, 0:1], OP.add),
              waits=[tok_smalls], inc="sdve")
        em.op("dve", ts(small16[:, 2:3], smalls["bb_s"][:, 0:1], -1.0, OP.mult),
              waits=[tok_smalls], inc="sdve")
        tok_small_done = em.op("dve", lambda v: v.memset(tailbuf[:], 0.0), inc="sdve")
        negea = small16[:, 0:1]
        dtbeff = small16[:, 1:2]
        negbb = small16[:, 2:3]

        pre_tok = {}
        pre_cast = [tok_small_done, tok_small_done]
        pre_store = [None, None]

        def prelude_weight(name, wt, sc, n_co, IN, kctr=[0]):
            pieces = [(i * 1024, min(1024, IN - i * 1024)) for i in range((IN + 1023) // 1024)]
            for co in range(n_co):
                for (r0, rl) in pieces:
                    par = kctr[0] % 2
                    kctr[0] += 1
                    pin = state[:, par * 1024: par * 1024 + rl]
                    pout = statebf[:, par * 1024: par * 1024 + rl]
                    src = wt.ap()[r0:r0 + rl, co * 128:(co + 1) * 128] \
                        .rearrange("(c p) q -> p c q", p=128)
                    t_ld = em.op("sync", dma(pin.rearrange("p (c q) -> p c q", q=128), src),
                                 waits=[pre_cast[par]], inc="spl%d" % par, inc_n=16)
                    t_c = em.op("act", act(pout, pin, AF.Copy),
                                waits=[t_ld, pre_store[par]], inc="sact")
                    pre_cast[par] = t_c
                    t_cd = em.op("act", lambda s: s.drain(), inc="sact")
                    t_st = em.op("pool", dma(sc.ap()[co * 128:(co + 1) * 128, r0:r0 + rl], pout),
                                 waits=[t_cd], inc="spd%d" % par, inc_n=16)
                    pre_store[par] = t_st
            pre_tok[name] = em.op("pool", None,
                                  waits=[("spd0", em.cnt["spd0"]),
                                         ("spd1", em.cnt["spd1"])],
                                  inc="spool")


        whalf_tok = [None, None]
        wload_tok = [None, None]
        wload_i = [0]
        tok_state_cast = [tok_small_done] * HV
        tok_state_new = [tok_small_done] * HV

        def wload(srcs, waits_extra=()):
            par = wload_i[0] % 2
            wload_i[0] += 1
            w = [whalf_tok[par]] + list(waits_extra)
            tok = None
            for (sc_ap, off, width) in srcs:
                tok = em.op("sync",
                            dma(wstream[:, par * WHALF + off: par * WHALF + off + width], sc_ap),
                            waits=w, inc="swd%d" % par, inc_n=16)
                w = []
            wload_tok[par] = tok
            return par

        def wsl(par, off, w):
            return wstream[:, par * WHALF + off: par * WHALF + off + w]

        def pe_drained(fn, waits=(), reads=()):
            em.op("pe", fn, waits=waits)
            tok = em.op("pe", lambda t: t.drain(), inc="spe")
            for k in reads:
                em.rd(k, tok)
            return tok

        def emit_ln(kind, get, key_of, tile_toks, out_writer):
            bS = fb.get()
            bQ = fb.get()
            onesc = cf["onescol_f"][:] if kind == "f32" else cb["onescol_b"][:]
            grp_base = None
            for ci in range(NCI):
                w0 = [bS["rtok"], tile_toks[ci]] if ci == 0 else [tile_toks[ci]]
                em.op("pe", mm(bS["ap"][0:1, :], onesc, get(ci),
                               start=(ci == 0), stop=(ci == NCI - 1)),
                      waits=w0, inc="spe" if ci == NCI - 1 else None)
            tokS = ("spe", em.cnt["spe"])
            for ci in range(NCI):
                em.rd(key_of(ci), tokS)
            for ci in range(NCI):
                par = ci % 2
                sqv = sqtmp[:, par * TT:(par + 1) * TT]
                t2 = em.op("act", act(sqv, get(ci), AF.Square),
                           waits=[tile_toks[ci]], inc="sact",
                           reads=[key_of(ci)], writes=[K_("sq", par)])
                w1 = [bQ["rtok"], t2] if ci == 0 else [t2]
                t3 = em.op("pe", mm(bQ["ap"][0:1, :], cb["onescol_b"][:], sqv,
                                    start=(ci == 0), stop=(ci == NCI - 1)),
                           waits=w1, inc="spe", reads=[K_("sq", par)])
            tokQ = ("spe", em.cnt["spe"])
            m_r, v_r, w_r = rows[0:1, 0:512], rows[0:1, 512:1024], rows[0:1, 1024:1536]
            t4 = em.op("dve", ts(m_r, bS["ap"][0:1, :], 1.0 / D, OP.mult),
                       waits=[tokS], inc="sdve", writes=[K_("rows", 0)])
            bS["rtok"] = t4
            t5 = em.op("dve", ts(v_r, bQ["ap"][0:1, :], 1.0 / D, OP.mult),
                       waits=[tokQ], inc="sdve", writes=[K_("rows", 1)])
            bQ["rtok"] = t5
            t6 = em.op("dve", tt(w_r, m_r, m_r, OP.mult), inc="sdve",
                       reads=[K_("rows", 0)], writes=[K_("rows", 2)])
            t7 = em.op("dve", tt(v_r, v_r, w_r, OP.subtract), inc="sdve",
                       reads=[K_("rows", 2)], writes=[K_("rows", 1)])
            t8 = em.op("act", act(w_r, v_r, AF.Ln, bias=cf["cvec"][0:1, 0:1]), waits=[t7], inc="sact",
                       reads=[K_("rows", 1)], writes=[K_("rows", 2)])
            t9 = em.op("act", act(v_r, w_r, AF.Exp, scale=-0.5), inc="sact",
                       reads=[K_("rows", 2)], writes=[K_("rows", 1)])
            t10 = em.op("dve", tt(w_r, m_r, v_r, OP.mult), waits=[t9], inc="sdve",
                        reads=[K_("rows", 0), K_("rows", 1)], writes=[K_("rows", 2)])
            bR = fb.get()
            bM = fb.get()
            tR = em.op("pe", mm(bR["ap"], cf["onesrow_f"][:], v_r),
                       waits=[bR["rtok"], t9], inc="spe", reads=[K_("rows", 1)])
            tM = em.op("pe", mm(bM["ap"], cf["onesrow_f"][:], w_r),
                       waits=[bM["rtok"], t10], inc="spe", reads=[K_("rows", 2)])
            out_toks = [out_writer(ci, bR["ap"], bM["ap"], tR, tM) for ci in range(NCI)]
            bR["rtok"] = out_toks[-1]
            bM["rtok"] = out_toks[-1]
            return out_toks

        def dump(name, src_ap, waits, keys=(), dst=None):
            if name in dbg and tile_now[0] == _DBG_TILE:
                t_d1 = em.op("dve", lambda v: v.drain(), waits=waits, inc="sdve")
                t_d2 = em.op("act", lambda s: s.drain(), waits=waits, inc="sact")
                em.op("pool", dma(dst if dst is not None else dbg_t[name].ap(), src_ap),
                      waits=[t_d1, t_d2], inc="sdbg", inc_n=16)
                t_f = em.op("pool", None, waits=[("sdbg", em.cnt["sdbg"])], inc="spool")
                for k in keys:
                    em.rd(k, t_f)

        tile_now = [0]

        for ci in range(NCI):
            src0 = xin[0].ap()[ci * 128:(ci + 1) * 128, 0:TT]
            em.op("act", dma(xsl(ci), src0), inc="sxd", inc_n=16,
                  writes=[K_("x", ci)])
        t_xf0 = em.op("act", None, waits=[("sxd", em.cnt["sxd"]), tok_smalls],
                      inc="sact")
        for ci in range(NCI):
            em.war[K_("x", ci)]["w"] = t_xf0
        tok_xload0 = [t_xf0] * NCI

        def ln1_writer0(ci, apR, apM, tR, tM, _toks=tok_xload0):
            em.op("dve", tt(tmpA[:], xsl(ci), apR, OP.mult),
                  waits=[tR, _toks[ci]], inc="sdve",
                  reads=[K_("x", ci)], writes=[K_("tmpA",)])
            return em.op("dve", tt(lnsl(ci), tmpA[:], apM, OP.subtract),
                         waits=[tM], inc="sdve",
                         reads=[K_("tmpA",)], writes=[K_("ln", ci)])
        tok_ln1_t0 = emit_ln("f32", xsl, lambda ci: K_("x", ci), tok_xload0,
                             ln1_writer0)

        prelude_weight("qkv", wqkvT, sc_qkv, N_QKV, D)
        prelude_weight("z", wzT, sc_z, N_Z, D)
        t_ld = em.op("sync", dma(state[:, 0:512].rearrange("p (c q) -> p c q", q=32),
                                 wabT.ap().rearrange("(c p) q -> p c q", p=128)),
                     waits=[pre_cast[0]], inc="spl0", inc_n=16)
        t_ab = em.op("act", act(abw[:], state[:, 0:512], AF.Copy),
                     waits=[t_ld, pre_store[0]], inc="sact")
        pre_cast[0] = t_ab
        pre_tok["ab"] = t_ab
        tok_prelude_head = em.op("pool", None,
                                 waits=[("spd0", em.cnt["spd0"]),
                                        ("spd1", em.cnt["spd1"])],
                                 inc="spool")

        def prelude_tail_gen():
            pins = [(tmpA[:, 0:512], sqtmp[:, 0:TT], K_("tmpA",), K_("sq", 0)),
                    (convacc[:, 0:512], sqtmp[:, TT:2 * TT], K_("ca0",),
                     K_("sq", 1))]
            lstore = [None, None]
            plan = []
            done_after = {}
            for name, wt, sc, n_co, IN in [("wo", woutT, sc_wo, N_WO, D),
                                           ("g", wgT, sc_g, N_GU, D),
                                           ("u", wuT, sc_u, N_GU, D),
                                           ("d", wdT, sc_d, N_DN, INTER)]:
                for co in range(n_co):
                    for r0 in range(0, IN, 512):
                        plan.append((wt, sc, co, r0))
                done_after[len(plan) - 1] = name
            for i, (wt, sc, co, r0) in enumerate(plan):
                par = i % 2
                pin, bout, kpin, kbout = pins[par]
                src = wt.ap()[r0:r0 + 512, co * 128:(co + 1) * 128] \
                    .rearrange("(c p) q -> p c q", p=128)
                t_ld2 = em.op("sync", dma(pin.rearrange("p (c q) -> p c q", q=128), src),
                              inc="spl%d" % par, inc_n=16, writes=[kpin])
                if par == 0:
                    t_c2 = em.op("act", act(bout, pin, AF.Copy),
                                 waits=[t_ld2, lstore[par]], inc="sact",
                                 reads=[kpin], writes=[kbout])
                    st_eng = "act"
                else:
                    t_c2 = em.op("dve", cp(bout, pin),
                                 waits=[t_ld2, lstore[par]], inc="sdve",
                                 reads=[kpin], writes=[kbout])
                    st_eng = "sync"
                t_st2 = em.op(st_eng, dma(sc.ap()[co * 128:(co + 1) * 128,
                                                  r0:r0 + 512], bout),
                              waits=[t_c2], inc="spd%d" % par, inc_n=16)
                em.rd(kbout, t_st2)
                lstore[par] = t_st2
                if i in done_after:
                    pre_tok[done_after[i]] = em.op(
                        "pool", None,
                        waits=[("spd0", em.cnt["spd0"]),
                               ("spd1", em.cnt["spd1"])],
                        inc="spool")
                if i % 2 == 1:
                    yield

        tok_g = tok_lb = tok_be = tok_small_done
        pending_x = None

        for tile in range(min(NTILE, _MAXTILE)):
            tile_now[0] = tile
            b = tile // 4
            sc4 = tile % 4
            tok0 = sc4 * TT
            seq_start = (sc4 == 0)

            if tile == 0:
                tok_ln1 = tok_ln1_t0
            else:
                tok_xload = pending_x

                def ln1_writer(ci, apR, apM, tR, tM, _toks=tok_xload):
                    em.op("dve", tt(tmpA[:], xsl(ci), apR, OP.mult),
                          waits=[tR, _toks[ci]], inc="sdve",
                          reads=[K_("x", ci)], writes=[K_("tmpA",)])
                    return em.op("dve", tt(lnsl(ci), tmpA[:], apM, OP.subtract),
                                 waits=[tM], inc="sdve",
                                 reads=[K_("tmpA",)], writes=[K_("ln", ci)])
                tok_ln1 = emit_ln("f32", xsl, lambda ci: K_("x", ci), tok_xload,
                                  ln1_writer)
            dump("ln1", lnbuf[:], [tok_ln1[-1]], keys=[K_("ln", i) for i in range(NCI)])

            tok_conv = []
            for co in range(N_QKV):
                par = wload([(sc_qkv.ap()[co * 128:(co + 1) * 128, :], 0, D)],
                            waits_extra=[pre_tok["qkv"]])
                bk = fb.get()
                for ci in range(NCI):
                    w = [bk["rtok"], wload_tok[par], tok_ln1[ci]] if ci == 0 \
                        else [tok_ln1[ci]]
                    em.op("pe", mm(bk["ap"], wsl(par, ci * 128, 128), lnsl(ci),
                                   start=(ci == 0), stop=(ci == NCI - 1)),
                          waits=w, inc="spe" if ci == NCI - 1 else None)
                tok_mm = ("spe", em.cnt["spe"])
                whalf_tok[par] = tok_mm
                for ci in range(NCI):
                    em.rd(K_("ln", ci), tok_mm)
                t_ev = em.op("act", act(mdata(co), bk["ap"], AF.Identity,
                                        bias=smalls["bqkv_s"][:, co:co + 1]),
                             waits=[tok_mm], inc="sact", writes=[K_("mixd", co)])
                bk["rtok"] = t_ev
                if seq_start:
                    t_h = em.op("pool", lambda v, _co=co: v.memset(msl(_co, 1, 3), 0.0),
                                inc="spool", writes=[K_("mixh", co)])
                else:
                    t_h = em.op("pool", cp(msl(co, 1, 3), tailbuf[:, co * 3:co * 3 + 3]),
                                inc="spool", writes=[K_("mixh", co)], reads=[K_("tail", co)])
                em.op("pool", cp(tailbuf[:, co * 3:co * 3 + 3], msl(co, 513, 3)),
                      waits=[t_ev], inc="spool", reads=[K_("mixd", co)],
                      writes=[K_("tail", co)])
                a0 = convacc[:, 0:TT]
                a1 = tmpA[:, 0:TT]
                em.op("dve", ts(a0, msl(co, 1, TT),
                                smalls["convw_s"][:, co * 4:co * 4 + 1], OP.mult),
                      waits=[t_ev, t_h], inc="sdve",
                      reads=[K_("mixd", co), K_("mixh", co)], writes=[K_("ca0",)])
                em.op("dve", stt(a1, msl(co, 2, TT), smalls["convw_s"][:, co * 4 + 1:co * 4 + 2],
                                 a0, OP.mult, OP.add), inc="sdve",
                      reads=[K_("mixd", co), K_("mixh", co), K_("ca0",)], writes=[K_("tmpA",)])
                em.op("dve", stt(a0, msl(co, 3, TT), smalls["convw_s"][:, co * 4 + 2:co * 4 + 3],
                                 a1, OP.mult, OP.add), inc="sdve",
                      reads=[K_("mixd", co), K_("tmpA",)], writes=[K_("ca0",)])
                t_c4 = em.op("dve", stt(a1, msl(co, 4, TT),
                                        smalls["convw_s"][:, co * 4 + 3:co * 4 + 4],
                                        a0, OP.mult, OP.add), inc="sdve",
                             reads=[K_("mixd", co), K_("ca0",)], writes=[K_("tmpA",)])
                t_s = em.op("act", act(qsl(co), a1, AF.Silu,
                                       bias=smalls["convb_s"][:, co:co + 1]),
                            waits=[t_c4], inc="sact",
                            reads=[K_("tmpA",)], writes=[K_("qkv", co)])
                tok_conv.append(t_s)
            dump("mixed", qkvbuf[:], [tok_conv[-1]], keys=[K_("qkv", i) for i in range(N_QKV)])

            tok_qn = []
            for j in range(16):
                is_q = j < 8
                par = j % 2
                sqv = sqtmp[:, par * TT:(par + 1) * TT]
                t_sq = em.op("act", act(sqv, qsl(j), AF.Square),
                             waits=[tok_conv[j]], inc="sact",
                             reads=[K_("qkv", j)], writes=[K_("sq", par)])
                bk = fb.get()
                t_mm = em.op("pe", mm(bk["ap"][0:1, :], cb["onescol_b"][:], sqv),
                             waits=[bk["rtok"], t_sq], inc="spe", reads=[K_("sq", par)])
                t_l = em.op("act", act(rows[0:1, 0:512], bk["ap"][0:1, :], AF.Ln,
                                       bias=cf["cvec"][0:1, 1:2]),
                            waits=[t_mm], inc="sact", writes=[K_("rows", 0)])
                bk["rtok"] = t_l
                bias = cf["cvec"][0:1, 3:4] if is_q else None
                t_r = em.op("act", act(rows[0:1, 512:1024], rows[0:1, 0:512], AF.Exp,
                                       scale=-0.5, bias=bias), inc="sact",
                            reads=[K_("rows", 0)], writes=[K_("rows", 1)])
                bk2 = fb.get()
                t_b = em.op("pe", mm(bk2["ap"], cf["onesrow_f"][:], rows[0:1, 512:1024]),
                            waits=[bk2["rtok"], t_r], inc="spe", reads=[K_("rows", 1)])
                t_ap = em.op("dve", tt(mdata(j), qsl(j), bk2["ap"], OP.mult),
                             waits=[t_b], inc="sdve",
                             reads=[K_("qkv", j)], writes=[K_("mixd", j)])
                bk2["rtok"] = t_ap
                tok_qn.append(t_ap)
            if "qn" in dbg and tile == _DBG_TILE:
                for j in range(16):
                    dump("qn", mdata(j), [tok_qn[j]], keys=[K_("mixd", j)],
                         dst=dbg_t["qn"].ap()[:, j * TT:(j + 1) * TT])

            tok_zgw = []
            for co in range(N_Z):
                par = wload([(sc_z.ap()[co * 128:(co + 1) * 128, :], 0, D)],
                            waits_extra=[pre_tok["z"]])
                bk = fb.get()
                for ci in range(NCI):
                    w = [bk["rtok"], wload_tok[par], tok_ln1[ci]] if ci == 0 else [tok_ln1[ci]]
                    em.op("pe", mm(bk["ap"], wsl(par, ci * 128, 128), lnsl(ci),
                                   start=(ci == 0), stop=(ci == NCI - 1)),
                          waits=w, inc="spe" if ci == NCI - 1 else None)
                tok_mm = ("spe", em.cnt["spe"])
                whalf_tok[par] = tok_mm
                for ci in range(NCI):
                    em.rd(K_("ln", ci), tok_mm)
                t_s = em.op("act", act(tmpA[:], bk["ap"], AF.Silu,
                                       bias=smalls["bz_s"][:, co:co + 1]),
                            waits=[tok_mm], inc="sact", writes=[K_("tmpA",)])
                bk["rtok"] = t_s
                t_g_ = em.op("dve", ts(zsl(co), tmpA[:], smalls["normw_s"][:, 0:1], OP.mult),
                             waits=[t_s], inc="sdve",
                             reads=[K_("tmpA",)], writes=[K_("zg", co)])
                tok_zgw.append(t_g_)

            if "zgw" in dbg and tile == _DBG_TILE:
                dump("zgw", zgwbuf[:], [tok_zgw[-1]], keys=[K_("zg", i) for i in range(NCI)])
            bkA2 = fb.get()
            for ci in range(NCI):
                w = [bkA2["rtok"], pre_tok["ab"], tok_ln1[ci]] if ci == 0 else [tok_ln1[ci]]
                em.op("pe", mm(bkA2["ap"][0:16, :], abw[:, ci * 32:ci * 32 + 16], lnsl(ci),
                               start=(ci == 0), stop=(ci == NCI - 1)),
                      waits=w, inc="spe" if ci == NCI - 1 else None)
            tok_a2 = ("spe", em.cnt["spe"])
            bkB2 = fb.get()
            for ci in range(NCI):
                w = [bkB2["rtok"]] if ci == 0 else []
                em.op("pe", mm(bkB2["ap"][0:16, :], abw[:, ci * 32 + 16:ci * 32 + 32], lnsl(ci),
                               start=(ci == 0), stop=(ci == NCI - 1)),
                      waits=w, inc="spe" if ci == NCI - 1 else None)
            tok_ab = ("spe", em.cnt["spe"])
            for ci in range(NCI):
                em.rd(K_("ln", ci), tok_ab)
            psa = bkA2["ap"][0:16, :]
            psb = bkB2["ap"][0:16, :]
            sA = tmpA[0:16, :]
            sB = convacc[0:16, 0:TT]
            em.op("act", act(sA, psa, AF.Abs, bias=dtbeff),
                  waits=[tok_a2], inc="sact", writes=[K_("tmpA",)])
            em.op("act", act(sB, sA, AF.Exp, scale=-1.0), inc="sact",
                  reads=[K_("tmpA",)], writes=[K_("ca0",)])
            em.op("act", act(sA, sB, AF.Ln, bias=cf["cvec"][0:16, 2:3]), inc="sact",
                  reads=[K_("ca0",)], writes=[K_("tmpA",)])
            t_g1 = em.op("act", act(sB, psa, AF.Relu, bias=dtbeff), inc="sact",
                         writes=[K_("ca0",)])
            t_g2 = em.op("dve", tt(gfm[:], sA, sB, OP.add),
                         waits=[t_g1], inc="sdve",
                         reads=[K_("tmpA",), K_("ca0",)], writes=[K_("gfm",)])
            tok_g = em.op("dve", ts(gfm[:], gfm[:], negea, OP.mult), inc="sdve",
                          writes=[K_("gfm",)])
            bkA2["rtok"] = t_g1
            em.op("act", act(sA, psb, AF.Abs, bias=smalls["bb_s"][:, 0:1]),
                  waits=[tok_ab, t_g2], inc="sact", writes=[K_("tmpA",)])
            em.op("act", act(sB, sA, AF.Exp, scale=-1.0), inc="sact",
                  reads=[K_("tmpA",)], writes=[K_("ca0",)])
            em.op("act", act(sA, sB, AF.Ln, bias=cf["cvec"][0:16, 2:3]), inc="sact",
                  reads=[K_("ca0",)], writes=[K_("tmpA",)])
            t_b1 = em.op("act", act(sB, psb, AF.Relu, scale=-1.0, bias=negbb),
                         inc="sact", writes=[K_("ca0",)])
            t_b2 = em.op("dve", tt(lbfm[:], sA, sB, OP.add),
                         waits=[t_b1], inc="sdve",
                         reads=[K_("tmpA",), K_("ca0",)], writes=[K_("lbfm",)])
            tok_lb = em.op("dve", ts(lbfm[:], lbfm[:], -1.0, OP.mult), inc="sdve",
                           writes=[K_("lbfm",)])
            tok_be = em.op("act", act(befm[:], psb, AF.Sigmoid,
                                      bias=smalls["bb_s"][:, 0:1]),
                           waits=[tok_ab], inc="sact", writes=[K_("befm",)])
            bkB2["rtok"] = tok_be
            if "gclb" in dbg and tile == _DBG_TILE:
                dump("gclb", gfm[:], [tok_g], keys=[K_("gfm",)], dst=dbg_t["gclb"].ap()[:, 0:TT])
                dump("gclb", lbfm[:], [tok_lb], keys=[K_("lbfm",)], dst=dbg_t["gclb"].ap()[:, TT:2 * TT])
                dump("gclb", befm[:], [tok_be], keys=[K_("befm",)], dst=dbg_t["gclb"].ap()[:, 2 * TT:3 * TT])

            if seq_start:
                w_pre = [tok_prelude_head, pre_cast[0], pre_cast[1]] \
                    if tile == 0 else []
                t_ms1 = em.op("dve", lambda v: v.memset(state[:], 0.0),
                              waits=w_pre + [tok_state_cast[HV - 1]] + tok_state_new,
                              inc="sdve", writes=[K_("state",)])
                t_ms2 = em.op("act", act(statebf[:], state[:], AF.Copy),
                              waits=[t_ms1], inc="sact",
                              reads=[K_("state",)], writes=[K_("statebf",)])
                tok_state_cast = [t_ms2] * HV
                tok_state_new = [t_ms1] * HV

            prepres = {}
            pairtok = {}

            def prep_gen(c):
                par = c % 2
                c0 = c * 128
                bkp = fb.get()
                t_tr = pe_drained(trans(bkp["ap"][:, 0:16], gfm[:, c0:c0 + 128],
                                        cf["identf"][0:16, 0:16]),
                                  waits=[bkp["rtok"], tok_g], reads=[K_("gfm",)])
                t_ev = em.op("dve", cp(ct(par, 0), bkp["ap"][:, 0:16]), waits=[t_tr],
                             inc="sdve", writes=[K_("ct", par, 0)])
                bkp["rtok"] = t_ev
                yield
                bkc = fb.get()
                t_cs = pe_drained(mm(bkc["ap"][:, 0:16], cf["maskUf"][:], ct(par, 0)),
                                  waits=[bkc["rtok"], t_ev], reads=[K_("ct", par, 0)])
                t_cse = em.op("dve", cp(ct(par, 1), bkc["ap"][:, 0:16]), waits=[t_cs],
                              inc="sdve", writes=[K_("ct", par, 1)])
                bkc["rtok"] = t_cse
                yield
                bkl = fb.get()
                t_tl = pe_drained(trans(bkl["ap"][:, 0:16], lbfm[:, c0:c0 + 128],
                                        cf["identf"][0:16, 0:16]),
                                  waits=[bkl["rtok"], tok_lb], reads=[K_("lbfm",)])
                t_le = em.op("dve", cp(ct(par, 2), bkl["ap"][:, 0:16]), waits=[t_tl],
                             inc="sdve", writes=[K_("ct", par, 2)])
                bkl["rtok"] = t_le
                t_phi = em.op("dve", tt(ct(par, 3), ct(par, 1), ct(par, 2), OP.add),
                              waits=[t_cse, t_le], inc="sdve",
                              reads=[K_("ct", par, 1), K_("ct", par, 2)],
                              writes=[K_("ct", par, 3)])
                yield
                bkb = fb.get()
                t_tb = pe_drained(trans(bkb["ap"][:, 0:16], befm[:, c0:c0 + 128],
                                        cf["identf"][0:16, 0:16]),
                                  waits=[bkb["rtok"], tok_be], reads=[K_("befm",)])
                t_bev = em.op("dve", cp(ct(par, 4), bkb["ap"][:, 0:16]), waits=[t_tb],
                              inc="sdve", writes=[K_("ct", par, 4)])
                bkb["rtok"] = t_bev
                t_bc = em.op("act", act(ct(par, 5), ct(par, 1), AF.Exp), waits=[t_cse],
                             inc="sact", reads=[K_("ct", par, 1)], writes=[K_("ct", par, 5)])
                t_nb = em.op("dve", ts(ct(par, 6), ct(par, 5), -1.0, OP.mult),
                             waits=[t_bc], inc="sdve",
                             reads=[K_("ct", par, 5)], writes=[K_("ct", par, 6)])
                yield
                bkt = fb.get()
                t_ctr0 = pe_drained(trans(bkt["ap"][0:16, 0:128], ct(par, 1), cf["identf"][:]),
                                    waits=[bkt["rtok"], t_cse], reads=[K_("ct", par, 1)])
                t_c127 = em.op("dve", cp(cs127[:, par:par + 1], bkt["ap"][0:16, 127:128]),
                               waits=[t_ctr0], inc="sdve", writes=[K_("cs127", par)])
                bkt["rtok"] = t_c127
                t_bc16 = em.op("act", act(bc16[:, par:par + 1], cs127[:, par:par + 1], AF.Exp),
                               waits=[t_c127], inc="sact",
                               reads=[K_("cs127", par)], writes=[K_("bc16", par)])
                t_diag = em.op("dve", ts(diagbc[:, par * 16:(par + 1) * 16], cf["ident16f"][:],
                                         bc16[:, par:par + 1], OP.mult),
                               waits=[t_bc16], inc="sdve",
                               reads=[K_("bc16", par)], writes=[K_("diag", par)])
                yield
                bkr = fb.get()
                t_br = pe_drained(mm(bkr["ap"][:, 0:16], cf["ones16f"][:],
                                     diagbc[:, par * 16:(par + 1) * 16]),
                                  waits=[bkr["rtok"], t_diag], reads=[K_("diag", par)])
                t_bre = em.op("dve", cp(ct(par, 7), bkr["ap"][:, 0:16]), waits=[t_br],
                              inc="sdve", writes=[K_("ct", par, 7)])
                bkr["rtok"] = t_bre
                if "sct" in dbg and tile == _DBG_TILE and c == 0:
                    dump("sct", ctile[:, 0:128], [t_bre, t_phi, t_nb, t_bev],
                         keys=[K_("ct", 0, s) for s in range(8)])
                if "cs0" in dbg and tile == _DBG_TILE and c == 0:
                    dump("cs0", ct(par, 1), [t_cse], keys=[K_("ct", par, 1)])
                prepres[c] = dict(par=par, t_phi=t_phi, t_cse=t_cse, t_nb=t_nb,
                                  t_bev=t_bev, t_bc=t_bc, t_bre=t_bre)

            def pair_gen(c, kp):
                while c not in prepres:
                    yield
                gp = c * HK + kp
                while gp >= NIP and (gp - NIP) not in inst_done:
                    yield
                P = prepres[c]
                par = P["par"]
                c0 = c * 128
                t_phi, t_cse, t_nb = P["t_phi"], P["t_cse"], P["t_nb"]
                t_bev, t_bc, t_bre = P["t_bev"], P["t_bc"], P["t_bre"]
                h0 = 2 * kp
                ip = gp % NIP
                hs = [h0, h0 + 1]

                def ccol(slot, hh, _p=par):
                    return ct(_p, slot)[:, hh:hh + 1]

                def pr(nm, sub=None, w=128):
                    base = ip * 256 + (0 if sub is None else sub * 128)
                    return scn[nm][:, base:base + (256 if sub is None else w)]

                def prf(nm, sub=None, w=128):
                    base = ip * 256 + (0 if sub is None else sub * 128)
                    return scf[nm][:, base:base + (256 if sub is None else w)]

                KALIAS = {"E": "Y1", "Vtm": "RW0", "CN": "TO"}
                KP = lambda nm: K_(KALIAS.get(nm, nm), ip)
                Qf = mdata(kp, c0, 128)
                Kf = mdata(8 + kp, c0, 128)
                bk1 = fb.get()
                t1 = em.op("pe", mm(bk1["ap"][:, 0:128], Kf, Kf),
                           waits=[bk1["rtok"], tok_qn[8 + kp]], inc="spe",
                           reads=[K_("mixd", 8 + kp)])
                tS1 = em.op("dve", tt(sfsl2("SkkMf", ip), bk1["ap"][:, 0:128],
                                      cb["maskSnb"][:], OP.mult),
                            waits=[t1], inc="sdve", writes=[K_("SkkM", ip)])
                bk1["rtok"] = tS1
                yield
                bk2 = fb.get()
                t2 = em.op("pe", mm(bk2["ap"][:, 0:128], Kf, Qf),
                           waits=[bk2["rtok"], tok_qn[kp], tok_qn[8 + kp]], inc="spe",
                           reads=[K_("mixd", kp), K_("mixd", 8 + kp)])
                tS2 = em.op("dve", tt(sfsl2("SkqMf", ip), bk2["ap"][:, 0:128],
                                      cb["maskIb"][:], OP.mult),
                            waits=[t2], inc="sdve", writes=[K_("SkqM", ip)])
                bk2["rtok"] = tS2
                yield
                bkT = bb.get()
                t3 = em.op("pe", trans(pxsl(bkT, 0), Kf, cb["identb"][:]),
                           waits=[bkT["rtok"], tok_qn[8 + kp]], inc="spe",
                           reads=[K_("mixd", 8 + kp)])
                tKtm = em.op("act", act(sfsl2b("Ktm", ip), pxsl(bkT, 0), AF.Copy),
                             waits=[t3], inc="sact", writes=[K_("Ktm", ip)])
                bkT["rtok"] = tKtm
                yield
                tG = []
                tMb = []
                tKh = []
                for (cslot, Ename, tcol) in [(3, "Es", t_phi), (1, "Ei", t_cse)]:
                    bkB = fb.get()
                    tb0a = em.op("act", act(scf["B0"][:, ip * 128:(ip + 1) * 128],
                                            zero128[:], AF.Identity,
                                            bias=ccol(cslot, hs[0])),
                                 waits=[tcol], inc="sact",
                                 reads=[K_("ct", par, cslot)], writes=[K_("B0", ip)])
                    tt1 = em.op("pe", trans(bkB["ap"][:, 0:128],
                                            scf["B0"][:, ip * 128:(ip + 1) * 128],
                                            cf["identf"][:]),
                                waits=[bkB["rtok"], tb0a], inc="spe", reads=[K_("B0", ip)])
                    tb0b = em.op("act", act(scf["B0"][:, ip * 128:(ip + 1) * 128],
                                            zero128[:], AF.Identity,
                                            bias=ccol(cslot, hs[1])),
                                 waits=[tcol], inc="sact",
                                 reads=[K_("ct", par, cslot)], writes=[K_("B0", ip)])
                    tt2 = em.op("pe", trans(bkB["ap"][:, 128:256],
                                            scf["B0"][:, ip * 128:(ip + 1) * 128],
                                            cf["identf"][:]),
                                waits=[tb0b], inc="spe", reads=[K_("B0", ip)])
                    te1 = em.op("dve", ts(prf("TO", 0), bkB["ap"][:, 0:128],
                                          ccol(1, hs[0]), OP.subtract, 0.0, OP.min),
                                waits=[tt2, t_cse], inc="sdve",
                                reads=[K_("ct", par, 1)], writes=[KP("TO")])
                    te2 = em.op("dve", ts(prf("TO", 1), bkB["ap"][:, 128:256],
                                          ccol(1, hs[1]), OP.subtract, 0.0, OP.min),
                                waits=[tt2], inc="sdve",
                                reads=[K_("ct", par, 1)], writes=[KP("TO")])
                    bkB["rtok"] = te2
                    tE = em.op("act", act(pr("E"), prf("TO"), AF.Exp),
                               waits=[te1, te2], inc="sact",
                               reads=[KP("TO")], writes=[KP("E")])
                    yield
                    if Ename == "Es":
                        for s in (0, 1):
                            tG.append(em.op("pool", tt(pr("G", s), sfsl2("SkkMf", ip),
                                                       pr("E", s), OP.mult),
                                            waits=[tS1, tE], inc="spool",
                                            reads=[K_("SkkM", ip), KP("E")],
                                            writes=[KP("G")]))
                    else:
                        tEi = tE
                        for s in (0, 1):
                            tMb.append(em.op("pool", tt(pr("Mb", s), sfsl2("SkqMf", ip),
                                                        pr("E", s), OP.mult),
                                             waits=[tS2, tE], inc="spool",
                                             reads=[K_("SkqM", ip), KP("E")],
                                             writes=[KP("Mb")]))
                            tb = em.op("dve", cp(bcb[:, 2 * ip + s:2 * ip + s + 1],
                                                 pr("E", s)[:, 127:128]),
                                       waits=[tE], inc="sdve",
                                       reads=[KP("E")], writes=[K_("bcb", ip, s)])
                            tKh.append(em.op("pool", ts(pr("Khat", s), sfsl2b("Ktm", ip),
                                                        bcb[:, 2 * ip + s:2 * ip + s + 1],
                                                        OP.mult),
                                             waits=[tKtm, tb], inc="spool",
                                             reads=[K_("Ktm", ip), K_("bcb", ip, s)],
                                             writes=[KP("Khat")]))
                yield
                bkV = bb.get()
                tv1 = em.op("pe", trans(pxsl(bkV, 0), qsl(16 + hs[0], c0, 128),
                                        cb["identb"][:]),
                            waits=[bkV["rtok"], tok_conv[16 + hs[0]]], inc="spe",
                            reads=[K_("qkv", 16 + hs[0])])
                tv2 = em.op("pe", trans(pxsl(bkV, 1), qsl(16 + hs[1], c0, 128),
                                        cb["identb"][:]),
                            waits=[tok_conv[16 + hs[1]]], inc="spe",
                            reads=[K_("qkv", 16 + hs[1])])
                tVe = em.op("act", act(pr("Vtm"), bkV["ap"][:, 0:256], AF.Copy),
                            waits=[tv2], inc="sact", writes=[KP("Vtm")])
                bkV["rtok"] = tVe
                yield
                bkK = fb.get()
                tk1 = em.op("pe", mm(bkK["ap"][:, 0:128], Kf, sbsl(hs[0])),
                            waits=[bkK["rtok"], tok_state_cast[hs[0]], tok_qn[8 + kp]],
                            inc="spe", reads=[K_("mixd", 8 + kp), K_("statebf", hs[0])])
                tk2 = em.op("pe", mm(bkK["ap"][:, 128:256], Kf, sbsl(hs[1])),
                            waits=[tok_state_cast[hs[1]]], inc="spe",
                            reads=[K_("statebf", hs[1])])
                tr1 = em.op("dve", stt(prf("TO", 0), bkK["ap"][:, 0:128], ccol(6, hs[0]),
                                       pr("Vtm", 0), OP.mult, OP.add),
                            waits=[tk2, t_nb, tVe], inc="sdve",
                            reads=[K_("ct", par, 6), KP("Vtm")], writes=[KP("TO")])
                tr2 = em.op("dve", stt(prf("TO", 1), bkK["ap"][:, 128:256], ccol(6, hs[1]),
                                       pr("Vtm", 1), OP.mult, OP.add),
                            inc="sdve", reads=[K_("ct", par, 6), KP("Vtm")],
                            writes=[KP("TO")])
                bkK["rtok"] = tr2
                ty1 = em.op("dve", ts(pr("Y0", 0), prf("TO", 0), ccol(4, hs[0]), OP.mult),
                            waits=[t_bev], inc="sdve",
                            reads=[KP("TO"), K_("ct", par, 4)], writes=[KP("Y0")])
                t_y = em.op("dve", ts(pr("Y0", 1), prf("TO", 1), ccol(4, hs[1]), OP.mult),
                            inc="sdve", reads=[KP("TO"), K_("ct", par, 4)],
                            writes=[KP("Y0")])
                yield
                bkq = fb.get()
                tq1 = em.op("pe", mm(bkq["ap"][:, 0:128], Qf, sbsl(hs[0])),
                            waits=[bkq["rtok"], tok_state_cast[hs[0]], tok_qn[kp]],
                            inc="spe", reads=[K_("mixd", kp), K_("statebf", hs[0])])
                tq2 = em.op("pe", mm(bkq["ap"][:, 128:256], Qf, sbsl(hs[1])),
                            waits=[tok_state_cast[hs[1]]], inc="spe",
                            reads=[K_("statebf", hs[1])])
                tt1_ = em.op("dve", ts(prf("TO", 0), bkq["ap"][:, 0:128], ccol(5, hs[0]),
                                       OP.mult),
                             waits=[tq2, t_bc], inc="sdve",
                             reads=[K_("ct", par, 5)], writes=[KP("TO")])
                tt2_ = em.op("dve", ts(prf("TO", 1), bkq["ap"][:, 128:256], ccol(5, hs[1]),
                                       OP.mult),
                             inc="sdve", reads=[K_("ct", par, 5)], writes=[KP("TO")])
                bkq["rtok"] = tt2_
                yield
                bkA = bb.get()
                ta1 = em.op("pe", trans(pxsl(bkA, 0), pr("G", 0),
                                        cb["identb"][:]),
                            waits=[bkA["rtok"], tG[0]], inc="spe", reads=[KP("G")])
                ta2 = em.op("pe", trans(pxsl(bkA, 1), pr("G", 1), cb["identb"][:]),
                            waits=[tG[1]], inc="spe", reads=[KP("G")])
                tAe = em.op("act", act(pr("RW1"), bkA["ap"][:, 0:256], AF.Copy),
                            waits=[ta2], inc="sact", writes=[KP("RW1")])
                bkA["rtok"] = tAe
                yield
                ycur, ynext = "Y0", "Y1"
                apT0, apT1, apTok, apKey = pr("G", 0), pr("G", 1), tG[1], KP("G")
                rwT0, rwT1, rwTok, rwKey = pr("RW1", 0), pr("RW1", 1), tAe, KP("RW1")
                rot = [("P2t", "RW0"), ("P4t", "P8t")]
                for lv in range(7):
                    bks = fb.get()
                    em.op("pe", mm(bks["ap"][:, 0:128], apT0, pr(ycur, 0),
                                   start=True, stop=False),
                          waits=[bks["rtok"], apTok, t_y], inc="spe",
                          reads=[apKey, KP(ycur)])
                    em.op("pe", mm(bks["ap"][:, 0:128], cb["identb"][:], pr(ycur, 0),
                                   start=False, stop=True), inc="spe", reads=[KP(ycur)])
                    em.op("pe", mm(bks["ap"][:, 128:256], apT1, pr(ycur, 1),
                                   start=True, stop=False), inc="spe",
                          reads=[apKey, KP(ycur)])
                    ts2 = em.op("pe", mm(bks["ap"][:, 128:256], cb["identb"][:],
                                         pr(ycur, 1), start=False, stop=True),
                                inc="spe", reads=[KP(ycur)])
                    eng_y = "dve" if lv % 2 == 0 else "act"
                    t_y = em.op(eng_y,
                                (cp(pr(ynext), bks["ap"][:, 0:256]) if eng_y == "dve"
                                 else act(pr(ynext), bks["ap"][:, 0:256], AF.Copy)),
                                waits=[ts2], inc="s" + eng_y, writes=[KP(ynext)])
                    bks["rtok"] = t_y
                    ycur, ynext = ynext, ycur
                    if lv < 6:
                        ptn, prn = rot[lv % 2]
                        bkp2 = fb.get()
                        em.op("pe", mm(bkp2["ap"][:, 0:128], rwT0, apT0),
                              waits=[bkp2["rtok"], rwTok, apTok], inc="spe",
                              reads=[rwKey, apKey])
                        tb2 = em.op("pe", mm(bkp2["ap"][:, 128:256], rwT1, apT1),
                                    inc="spe", reads=[rwKey, apKey])
                        eng_t = "act" if lv % 2 == 0 else "dve"
                        tev1 = em.op(eng_t,
                                     (act(pr(ptn), bkp2["ap"][:, 0:256], AF.Copy)
                                      if eng_t == "act"
                                      else cp(pr(ptn), bkp2["ap"][:, 0:256])),
                                     waits=[tb2], inc="s" + eng_t, writes=[KP(ptn)])
                        bkp2["rtok"] = tev1
                        if lv < 5:
                            bkp3 = fb.get()
                            em.op("pe", mm(bkp3["ap"][:, 0:128], apT0, rwT0),
                                  waits=[bkp3["rtok"], rwTok, apTok], inc="spe",
                                  reads=[rwKey, apKey])
                            tb4 = em.op("pe", mm(bkp3["ap"][:, 128:256], apT1, rwT1),
                                        inc="spe", reads=[rwKey, apKey])
                            eng_r = "dve" if lv % 2 == 0 else "act"
                            tev2 = em.op(eng_r,
                                         (cp(pr(prn), bkp3["ap"][:, 0:256])
                                          if eng_r == "dve"
                                          else act(pr(prn), bkp3["ap"][:, 0:256],
                                                   AF.Copy)),
                                         waits=[tb4], inc="s" + eng_r,
                                         writes=[KP(prn)])
                            bkp3["rtok"] = tev2
                            rwT0, rwT1, rwTok, rwKey = (pr(prn, 0), pr(prn, 1),
                                                        tev2, KP(prn))
                        apT0, apT1, apTok, apKey = (pr(ptn, 0), pr(ptn, 1),
                                                    tev1, KP(ptn))
                    yield
                Ukey = KP(ycur)
                t_U = t_y
                bkm = fb.get()
                tm1 = em.op("pe", mm(bkm["ap"][:, 0:128], pr("Mb", 0), pr(ycur, 0)),
                            waits=[bkm["rtok"], tMb[0], t_U], inc="spe",
                            reads=[KP("Mb"), Ukey])
                tm2 = em.op("pe", mm(bkm["ap"][:, 128:256], pr("Mb", 1), pr(ycur, 1)),
                            waits=[tMb[1]], inc="spe", reads=[KP("Mb"), Ukey])
                t_O = em.op("dve", tt(prf("O"), prf("TO"), bkm["ap"][:, 0:256], OP.add),
                            waits=[tm2, tt2_], inc="sdve",
                            reads=[KP("TO")], writes=[KP("O")])
                bkm["rtok"] = t_O
                yield
                bksn = fb.get()
                tn1 = em.op("pe", mm(bksn["ap"][:, 0:128], pr("Khat", 0), pr(ycur, 0)),
                            waits=[bksn["rtok"], tKh[0], t_U], inc="spe",
                            reads=[KP("Khat"), Ukey])
                tn2 = em.op("pe", mm(bksn["ap"][:, 128:256], pr("Khat", 1), pr(ycur, 1)),
                            waits=[tKh[1]], inc="spe", reads=[KP("Khat"), Ukey])
                ts1_ = em.op("dve", stt(ssl(hs[0]), ssl(hs[0]), ccol(7, hs[0]),
                                        bksn["ap"][:, 0:128], OP.mult, OP.add),
                             waits=[tn2, t_bre, tok_state_new[hs[0]],
                                    tok_state_cast[hs[0]]],
                             inc="sdve",
                             reads=[K_("ct", par, 7)], writes=[K_("stateh", hs[0])])
                ts2_ = em.op("dve", stt(ssl(hs[1]), ssl(hs[1]), ccol(7, hs[1]),
                                        bksn["ap"][:, 128:256], OP.mult, OP.add),
                             waits=[tok_state_new[hs[1]], tok_state_cast[hs[1]]],
                             inc="sdve",
                             reads=[K_("ct", par, 7)], writes=[K_("stateh", hs[1])])
                bksn["rtok"] = ts2_
                tok_state_new[hs[0]] = ts1_
                tok_state_new[hs[1]] = ts2_
                tcast = em.op("act", act(statebf[:, h0 * 128:(h0 + 2) * 128],
                                         state[:, h0 * 128:(h0 + 2) * 128], AF.Copy),
                              waits=[ts1_, ts2_], inc="sact",
                              reads=[K_("stateh", hs[0]), K_("stateh", hs[1])],
                              writes=[K_("statebf", hs[0]), K_("statebf", hs[1])])
                tok_state_cast[hs[0]] = tcast
                tok_state_cast[hs[1]] = tcast
                yield
                t_osq0 = em.op("act", act(prf("TO"), prf("O"), AF.Square),
                               waits=[t_O, tt2_], inc="sact",
                               reads=[KP("O")], writes=[KP("TO")])
                tvr = []
                for s in (0, 1):
                    tvr.append(em.op("dve",
                                     lambda v, _o=prf("TO", s),
                                     _d=varcol[:, 2 * ip + s:2 * ip + s + 1]:
                                     v.reduce_sum(_d, _o, axis=mybir.AxisListType.X),
                                     waits=[t_osq0], inc="sdve",
                                     reads=[KP("TO")], writes=[K_("var", ip, s)]))
                em.op("act", act(rstdcol[:, 2 * ip:2 * ip + 2],
                                 varcol[:, 2 * ip:2 * ip + 2], AF.Ln,
                                 scale=1.0 / 128.0, bias=cf["cvec"][:, 1:2]),
                      waits=tvr, inc="sact",
                      reads=[K_("var", ip, 0), K_("var", ip, 1)],
                      writes=[K_("rstd", ip)])
                t_rs = em.op("act", act(rstdcol[:, 2 * ip:2 * ip + 2],
                                        rstdcol[:, 2 * ip:2 * ip + 2],
                                        AF.Exp, scale=-0.5), inc="sact",
                             writes=[K_("rstd", ip)])
                tc1 = em.op("dve", ts(prf("CN", 0), prf("O", 0),
                                      rstdcol[:, 2 * ip:2 * ip + 1], OP.mult),
                            waits=[t_rs], inc="sdve",
                            reads=[KP("O"), K_("rstd", ip)], writes=[KP("CN")])
                tc2 = em.op("dve", ts(prf("CN", 1), prf("O", 1),
                                      rstdcol[:, 2 * ip + 1:2 * ip + 2], OP.mult),
                            inc="sdve", reads=[KP("O"), K_("rstd", ip)],
                            writes=[KP("CN")])
                yield
                bkct = fb.get()
                tct1 = em.op("pe", trans(bkct["ap"][:, 0:128], prf("CN", 0),
                                         cf["identf"][:]),
                             waits=[bkct["rtok"], tc1, tc2], inc="spe", reads=[KP("CN")])
                tct2 = em.op("pe", trans(bkct["ap"][:, 128:256], prf("CN", 1),
                                         cf["identf"][:]), inc="spe", reads=[KP("CN")])
                tg1 = em.op("dve", tt(lnsl(hs[0], c0, 128), bkct["ap"][:, 0:128],
                                      zsl(hs[0], c0, 128), OP.mult),
                            waits=[tct2, tok_zgw[hs[0]]], inc="sdve",
                            reads=[K_("zg", hs[0])], writes=[K_("ln", hs[0])])
                tg2 = em.op("dve", tt(lnsl(hs[1], c0, 128), bkct["ap"][:, 128:256],
                                      zsl(hs[1], c0, 128), OP.mult),
                            waits=[tok_zgw[hs[1]]], inc="sdve",
                            reads=[K_("zg", hs[1])], writes=[K_("ln", hs[1])])
                bkct["rtok"] = tg2
                inst_done.add(gp)

            inst_done = set()
            gens = deque()
            if tile == 0:
                gens.append(prelude_tail_gen())
            for c in range(NCH):
                gens.append(prep_gen(c))
                for kp in range(HK):
                    gens.append(pair_gen(c, kp))
            active = deque()
            while gens or active:
                while len(active) < _SCAN_DEPTH and gens:
                    active.append(gens.popleft())
                g0 = active.popleft()
                try:
                    next(g0)
                    active.append(g0)
                except StopIteration:
                    pass
            tok_core = ("sdve", em.cnt["sdve"])
            dump("core", lnbuf[:], [tok_core], keys=[K_("ln", i) for i in range(NCI)])

            tok_attn = []
            for co in range(N_WO):
                par = wload([(sc_wo.ap()[co * 128:(co + 1) * 128, :], 0, D)],
                            waits_extra=[pre_tok["wo"]])
                bk = fb.get()
                for ci in range(NCI):
                    w = [bk["rtok"], wload_tok[par], tok_core] if ci == 0 else []
                    em.op("pe", mm(bk["ap"], wsl(par, ci * 128, 128), lnsl(ci),
                                   start=(ci == 0), stop=(ci == NCI - 1)),
                          waits=w, inc="spe" if ci == NCI - 1 else None)
                tok_mm = ("spe", em.cnt["spe"])
                whalf_tok[par] = tok_mm
                for ci in range(NCI):
                    em.rd(K_("ln", ci), tok_mm)
                t_ev = em.op("act", act(qsl(co), bk["ap"], AF.Identity,
                                        bias=smalls["bout_s"][:, co:co + 1]),
                             waits=[tok_mm], inc="sact", writes=[K_("qkv", co)])
                bk["rtok"] = t_ev
                tok_attn.append(t_ev)
            dump("attn", qkvbuf[:, 0:NCI * TT], [tok_attn[-1]], keys=[K_("qkv", i) for i in range(NCI)])

            def ln2_writer(ci, apR, apM, tR, tM, _ta=tok_attn):
                em.op("dve", tt(tmpA[:], qsl(ci), apR, OP.mult),
                      waits=[tR, _ta[ci]], inc="sdve",
                      reads=[K_("qkv", ci)], writes=[K_("tmpA",)])
                em.op("dve", tt(tmpA[:], tmpA[:], apM, OP.subtract), waits=[tM], inc="sdve",
                      writes=[K_("tmpA",)])
                return em.op("dve", tt(xsl(ci), xsl(ci), tmpA[:], OP.add), inc="sdve",
                             reads=[K_("tmpA",)], writes=[K_("x", ci)])
            tok_x2 = emit_ln("bf16", qsl, lambda ci: K_("qkv", ci), tok_attn, ln2_writer)
            dump("x2", xbuf[:], [tok_x2[-1]], keys=[K_("x", i) for i in range(NCI)])

            def ln3_writer(ci, apR, apM, tR, tM, _tx=tok_x2):
                em.op("dve", tt(tmpA[:], xsl(ci), apR, OP.mult),
                      waits=[tR, _tx[ci]], inc="sdve",
                      reads=[K_("x", ci)], writes=[K_("tmpA",)])
                return em.op("dve", tt(lnsl(ci), tmpA[:], apM, OP.subtract),
                             waits=[tM], inc="sdve",
                             reads=[K_("tmpA",)], writes=[K_("ln", ci)])
            tok_ln3 = emit_ln("f32", xsl, lambda ci: K_("x", ci), tok_x2, ln3_writer)
            dump("ln3", lnbuf[:], [tok_ln3[-1]], keys=[K_("ln", i) for i in range(NCI)])

            def hkey(j):
                return K_("qkv", j) if j < 32 else K_("mixd", j - 32)

            def hslot(j, c0=0, w=TT):
                return qsl(j, c0, w) if j < 32 else mdata(j - 32, c0, w)

            tok_h = []
            for co in range(N_GU):
                par = wload([(sc_g.ap()[co * 128:(co + 1) * 128, :], 0, D),
                             (sc_u.ap()[co * 128:(co + 1) * 128, :], D, D)],
                            waits_extra=[pre_tok["g"], pre_tok["u"]])
                ldtok = wload_tok[par]
                bkG = fb.get()
                for ci in range(NCI):
                    w = [bkG["rtok"], ldtok, tok_ln3[ci]] if ci == 0 else [tok_ln3[ci]]
                    em.op("pe", mm(bkG["ap"], wsl(par, ci * 128, 128), lnsl(ci),
                                   start=(ci == 0), stop=(ci == NCI - 1)),
                          waits=w, inc="spe" if ci == NCI - 1 else None)
                tok_mmg = ("spe", em.cnt["spe"])
                bkU = fb.get()
                for ci in range(NCI):
                    w = [bkU["rtok"]] if ci == 0 else []
                    em.op("pe", mm(bkU["ap"], wsl(par, D + ci * 128, 128), lnsl(ci),
                                   start=(ci == 0), stop=(ci == NCI - 1)),
                          waits=w, inc="spe" if ci == NCI - 1 else None)
                tok_mmu = ("spe", em.cnt["spe"])
                whalf_tok[par] = tok_mmu
                for ci in range(NCI):
                    em.rd(K_("ln", ci), tok_mmu)
                t_sg = em.op("act", act(tmpA[:], bkG["ap"], AF.Silu,
                                        scale=smalls["gs_s"][:, co:co + 1]),
                             waits=[tok_mmg], inc="sact", writes=[K_("tmpA",)])
                bkG["rtok"] = t_sg
                spar = co % 2
                t_su_ = em.op("act", act(sqtmp[:, spar * TT:(spar + 1) * TT], bkU["ap"],
                                         AF.Identity, scale=smalls["us_s"][:, co:co + 1]),
                              waits=[tok_mmu], inc="sact", writes=[K_("sq", spar)])
                bkU["rtok"] = t_su_
                t_h = em.op("dve", tt(hslot(co), tmpA[:],
                                      sqtmp[:, spar * TT:(spar + 1) * TT], OP.mult),
                            waits=[t_sg, t_su_], inc="sdve",
                            reads=[K_("tmpA",), K_("sq", spar)], writes=[hkey(co)])
                tok_h.append(t_h)
            if "hbuf" in dbg and tile == _DBG_TILE:
                dump("hbuf", qkvbuf[:, 0:8 * TT], [tok_h[7]],
                     keys=[hkey(j) for j in range(8)])

            tok_ffn = []
            for co in range(N_DN):
                par = wload([(sc_d.ap()[co * 128:(co + 1) * 128, 0:32 * 128], 0, 32 * 128)],
                            waits_extra=[pre_tok["d"]])
                ld1 = wload_tok[par]
                ld2holder = []
                for ci in range(N_GU):
                    w = ([bkd["rtok"] if False else None] if False else [])
                    if ci == 0:
                        bk = fb.get()
                        w = [bk["rtok"], ld1] + tok_h
                    elif ci == 32:
                        w = [ld2holder[0]]
                    em.op("pe", mm(bk["ap"], wsl(par, (ci if ci < 32 else ci - 32) * 128, 128),
                                   hslot(ci),
                                   start=(ci == 0), stop=(ci == N_GU - 1)),
                          waits=w, inc="spe" if ci in (11, N_GU - 1) else None)
                    if ci == 11:
                        t_mm11 = ("spe", em.cnt["spe"])
                        ld2 = em.op("sync",
                                    dma(wstream[:, par * WHALF: par * WHALF + 12 * 128],
                                        sc_d.ap()[co * 128:(co + 1) * 128, 32 * 128:44 * 128]),
                                    waits=[t_mm11], inc="swd%d" % par, inc_n=16)
                        ld2holder.append(ld2)
                tok_mm = ("spe", em.cnt["spe"])
                whalf_tok[par] = tok_mm
                for ci in range(N_GU):
                    em.rd(hkey(ci), tok_mm)
                t_ev = em.op("act", act(mdata(12 + co), bk["ap"], AF.Identity,
                                        scale=smalls["ds_s"][:, co:co + 1]),
                             waits=[tok_mm], inc="sact", writes=[K_("mixd", 12 + co)])
                bk["rtok"] = t_ev
                tok_ffn.append(t_ev)

            def ln4_writer(ci, apR, apM, tR, tM, _tf=tok_ffn):
                em.op("dve", tt(tmpA[:], mdata(12 + ci), apR, OP.mult),
                      waits=[tR, _tf[ci]], inc="sdve",
                      reads=[K_("mixd", 12 + ci)], writes=[K_("tmpA",)])
                em.op("dve", tt(tmpA[:], tmpA[:], apM, OP.subtract), waits=[tM],
                      inc="sdve", writes=[K_("tmpA",)])
                return em.op("dve", tt(xsl(ci), xsl(ci), tmpA[:], OP.add), inc="sdve",
                             reads=[K_("tmpA",)], writes=[K_("x", ci)])

            tok_res4 = emit_ln("bf16", lambda ci: mdata(12 + ci),
                               lambda ci: K_("mixd", 12 + ci), tok_ffn, ln4_writer)
            halves = [(0, "sod"), (8, "sodb")]
            half_store_tok = []
            for (c0h, s_st) in halves:
                t_fl = em.op("dve", lambda v: v.drain(),
                             waits=[tok_res4[c0h + 7]], inc="sdve")
                for ci in range(c0h, c0h + 8):
                    em.op("act",
                          dma(yout[b].ap()[ci * 128:(ci + 1) * 128,
                                           tok0:tok0 + TT], xsl(ci)),
                          waits=[t_fl], inc=s_st, inc_n=16)
                    em.rd(K_("x", ci), (s_st, em.cnt[s_st]))
                half_store_tok.append((s_st, em.cnt[s_st]))
            nxt = tile + 1
            if nxt < min(NTILE, _MAXTILE):
                bn = nxt // 4
                tok0n = (nxt % 4) * TT
                pending_x = []
                for hi, (c0h, s_ld) in enumerate([(0, "sxd"), (8, "sxdb")]):
                    for ci in range(c0h, c0h + 8):
                        w = [half_store_tok[hi]] if ci == c0h else []
                        em.op("pool",
                              dma(xsl(ci), xin[bn].ap()[ci * 128:(ci + 1) * 128,
                                                        tok0n:tok0n + TT]),
                              waits=w, inc=s_ld, inc_n=16, writes=[K_("x", ci)])
                    t_xfh = em.op("pool", None, waits=[(s_ld, em.cnt[s_ld])],
                                  inc="spool")
                    for ci in range(c0h, c0h + 8):
                        em.war[K_("x", ci)]["w"] = t_xfh
                    pending_x += [t_xfh] * 8
            else:
                t_of = em.op("act", None,
                             waits=[half_store_tok[0], half_store_tok[1]],
                             inc="sact")
                for ci in range(NCI):
                    em.rd(K_("x", ci), t_of)

        em.op("pool", None, waits=[("sod", em.cnt["sod"]), ("sodb", em.cnt["sodb"]),
                                   ("sxd", em.cnt["sxd"]), ("sxdb", em.cnt["sxdb"]),
                                   ("swd0", em.cnt["swd0"]), ("swd1", em.cnt["swd1"]),
                                   ("sdbg", em.cnt["sdbg"]),
                                   ("spd0", em.cnt["spd0"]), ("spd1", em.cnt["spd1"]),
                                   ("spl0", em.cnt["spl0"]), ("spl1", em.cnt["spl1"])])
        em.emit(block, sems)

    return nc


def _host_stage(inputs):
    f32 = np.float32
    x = inputs["L0_attn_hidden_states"]
    common = {}
    common["wqkvT"] = np.ascontiguousarray(np.asarray(inputs["L0_attn_Wqkv"]).T.astype(f32))
    common["wzT"] = np.ascontiguousarray(np.asarray(inputs["L0_attn_Wz"]).T.astype(f32))
    wab = np.zeros((D, 32), f32)
    wab[:, 0:16] = np.asarray(inputs["L0_attn_Wa"]).T
    wab[:, 16:32] = np.asarray(inputs["L0_attn_Wb"]).T
    common["wabT"] = wab
    common["woutT"] = np.ascontiguousarray(np.asarray(inputs["L0_attn_Wout"]).T.astype(f32))
    common["wgT"] = np.ascontiguousarray(np.asarray(inputs["L0_ffn_gate_w"]).T.astype(f32))
    common["wuT"] = np.ascontiguousarray(np.asarray(inputs["L0_ffn_up_w"]).T.astype(f32))
    common["wdT"] = np.ascontiguousarray(np.asarray(inputs["L0_ffn_down_w"]).T.astype(f32))
    common["bqkv_s"] = np.ascontiguousarray(
        np.asarray(inputs["L0_attn_bqkv"]).reshape(32, 128).T.astype(f32))
    common["bz_s"] = np.ascontiguousarray(
        np.asarray(inputs["L0_attn_bz"]).reshape(16, 128).T.astype(f32))
    common["bout_s"] = np.ascontiguousarray(
        np.asarray(inputs["L0_attn_bout"]).reshape(16, 128).T.astype(f32))
    common["convw_s"] = np.ascontiguousarray(
        np.asarray(inputs["L0_attn_conv_w"]).reshape(32, 128, 4)
        .transpose(1, 0, 2).reshape(128, 128).astype(f32))
    common["convb_s"] = np.ascontiguousarray(
        np.asarray(inputs["L0_attn_conv_b"]).reshape(32, 128).T.astype(f32))
    common["gs_s"] = np.ascontiguousarray(
        np.asarray(inputs["L0_ffn_gate_s"]).reshape(44, 128).T.astype(f32))
    common["us_s"] = np.ascontiguousarray(
        np.asarray(inputs["L0_ffn_up_s"]).reshape(44, 128).T.astype(f32))
    common["ds_s"] = np.ascontiguousarray(
        np.asarray(inputs["L0_ffn_down_s"]).reshape(16, 128).T.astype(f32))
    common["normw_s"] = np.ascontiguousarray(np.asarray(inputs["L0_attn_norm_w"]).astype(f32)[:, None])
    common["ba_s"] = np.asarray(inputs["L0_attn_ba"]).astype(f32)[:, None]
    common["bb_s"] = np.asarray(inputs["L0_attn_bb"]).astype(f32)[:, None]
    common["dtb_s"] = np.ascontiguousarray(np.asarray(inputs["L0_attn_dt_bias"]).astype(f32)[:, None])
    common["alog_s"] = np.ascontiguousarray(np.asarray(inputs["L0_attn_A_log"]).astype(f32)[:, None])
    common["identf"] = np.eye(128, dtype=f32)
    common["maskUf"] = np.triu(np.ones((128, 128), f32))
    common["ones16f"] = np.ones((16, 128), f32)
    common["ident16f"] = np.eye(16, dtype=f32)
    common["onesrow_f"] = np.ones((1, 128), f32)
    common["onescol_f"] = np.ones((128, 1), f32)
    cvec = np.zeros((128, 8), f32)
    cvec[:, 0] = LNEPS
    cvec[:, 1] = 1e-6
    cvec[:, 2] = 1.0
    cvec[:, 3] = -0.5 * np.log(128.0)
    common["cvec"] = cvec
    common["identb"] = np.eye(128).astype(BF)
    common["maskSnb"] = (-np.triu(np.ones((128, 128), f32), 1)).astype(BF)
    common["maskIb"] = np.triu(np.ones((128, 128), f32)).astype(BF)
    common["onescol_b"] = np.ones((128, 1), f32).astype(BF)
    in_maps = []
    for core in range(8):
        m = dict(common)
        for bb_ in range(2):
            m[f"x{bb_}"] = np.ascontiguousarray(np.asarray(x[core * 2 + bb_]).T.astype(f32))
        in_maps.append(m)
    return in_maps


def kernel(**inputs):
    in_maps = _host_stage(inputs)
    nc = build_program(dbg=_DBG)
    res = run_bass_kernel_spmd(nc, in_maps, core_ids=list(range(8)))
    out = np.empty((16, 2048, D), np.float32)
    for core in range(8):
        for bb_ in range(2):
            out[core * 2 + bb_] = res.results[core][f"y{bb_}"].T
    return out


# version 13
# speedup vs baseline: 1.0510x; 1.0101x over previous
import numpy as np
import ml_dtypes
from collections import deque
from contextlib import ExitStack

import concourse.bass as bass
import concourse.mybir as mybir
from concourse.bass_utils import run_bass_kernel_spmd

F32 = mybir.dt.float32
BF16 = mybir.dt.bfloat16
AF = mybir.ActivationFunctionType
OP = mybir.AluOpType
BF = ml_dtypes.bfloat16

D = 2048
NCI = 16
TT = 512
NTILE = 8
C = 128
NCH = TT // C
HV = 16
HK = 8
CONVD = 4096
INTER = 5632
N_QKV = 32
N_Z = 16
N_WO = 16
N_GU = 44
N_DN = 16
MW = 516
MD = 4
LNEPS = 1e-5
RMSEPS = 1e-6
L2EPS = 1e-6

ENGINES = ("pe", "dve", "act", "pool", "sync")

_DBG = ()
_SCAN_DEPTH = 5
_YS = set(range(40))
_MAXTILE = NTILE
_PARANOID = True
_DBG_TILE = 0


class Em:

    def __init__(self):
        self.streams = {e: [] for e in ENGINES}
        self.cnt = {}
        self.war = {}

    def sem(self, *names):
        for n in names:
            self.cnt[n] = 0

    def op(self, eng, fn, waits=(), inc=None, inc_n=1, reads=(), writes=()):
        ws = list(waits)
        for key in writes:
            rec = self.war.get(key)
            if rec:
                ws.extend(rec["r"])
                if rec["w"] is not None:
                    ws.append(rec["w"])
        for key in reads:
            rec = self.war.get(key)
            if rec and rec["w"] is not None:
                ws.append(rec["w"])
        ws2 = []
        for w in ws:
            if w is None:
                continue
            s, v = w
            assert v <= self.cnt[s], f"wait on future token {s}={v} (planned {self.cnt[s]})"
            ws2.append((s, v))
        tok = None
        if inc is not None:
            self.cnt[inc] += inc_n
            tok = (inc, self.cnt[inc])
        assert tok is not None or (not reads and not writes), "hazard tracking needs a token"
        for key in writes:
            self.war[key] = {"w": tok, "r": []}
        for key in reads:
            self.war.setdefault(key, {"w": None, "r": []})["r"].append(tok)
        self.streams[eng].append((tuple(ws2), fn, inc, inc_n))
        return tok

    def rd(self, key, tok):
        self.war.setdefault(key, {"w": None, "r": []})["r"].append(tok)

    def emit(self, block, handles):
        sections = {"pe": block.tensor, "dve": block.vector, "act": block.scalar,
                    "pool": block.gpsimd, "sync": block.sync}
        ownsem = {"pe": "spe", "dve": "sdve", "act": "sact", "pool": "spool"}
        for ename in ENGINES:
            stream = self.streams[ename]
            own = ownsem.get(ename)

            def make(stream, own):
                def body(eng):
                    guard = {}
                    own_cnt = 0
                    for ws, fn, inc, inc_n in stream:
                        for s, v in ws:
                            if s == own and v <= own_cnt and not _PARANOID:
                                continue
                            if guard.get(s, 0) < v:
                                eng.wait_ge(handles[s], v)
                                guard[s] = v
                        if fn is not None:
                            ins = fn(eng)
                            if inc is not None:
                                if inc_n == 1 and ename in ("dve", "act", "pool"):
                                    eng.drain().then_inc(handles[inc], inc_n)
                                else:
                                    ins.then_inc(handles[inc], inc_n)
                        elif inc is not None:
                            eng.sem_inc(handles[inc], inc_n)
                        if inc == own:
                            own_cnt += inc_n
                return body
            sections[ename](make(stream, own))


class Banks:
    def __init__(self, aps):
        self.recs = [{"ap": ap, "rtok": None} for ap in aps]
        self.i = 0

    def get(self):
        r = self.recs[self.i % len(self.recs)]
        self.i += 1
        return r


DBG_SHAPES = {
    "ln1": ([128, NCI * TT], "f32"),
    "mixed": ([128, N_QKV * TT], "f32"),
    "qn": ([128, NCI * TT], "f32"),
    "gclb": ([16, 3 * TT], "f32"),
    "cs0": ([128, 16], "f32"),
    "core": ([128, NCI * TT], "f32"),
    "attn": ([128, NCI * TT], "f32"),
    "x2": ([128, NCI * TT], "f32"),
    "ln3": ([128, NCI * TT], "f32"),
    "hbuf": ([128, 8 * TT], "f32"),
    "zgw": ([128, NCI * TT], "bf16"),
    "sEs": ([128, 16 * 128], "bf16"),
    "sEi": ([128, 16 * 128], "bf16"),
    "sG": ([128, 16 * 128], "bf16"),
    "sY0": ([128, 16 * 128], "bf16"),
    "sU": ([128, 16 * 128], "bf16"),
    "sO": ([128, 16 * 128], "f32"),
    "svar": ([128, 4], "f32"),
    "sS1": ([128, 16 * 128], "f32"),
    "sKtm": ([128, 16 * 128], "bf16"),
    "sVtm": ([128, 16 * 128], "bf16"),
    "sct": ([128, 8 * 16], "f32"),
}


def build_program(dbg=()):
    nc = bass.Bass("TRN2", target_bir_lowering=False)

    xin = [nc.dram_tensor(f"x{b}", [D, 2048], F32, kind="ExternalInput") for b in range(2)]
    yout = [nc.dram_tensor(f"y{b}", [D, 2048], F32, kind="ExternalOutput") for b in range(2)]

    wqkvT = nc.dram_tensor("wqkvT", [D, CONVD], F32, kind="ExternalInput")
    wzT = nc.dram_tensor("wzT", [D, D], F32, kind="ExternalInput")
    wabT = nc.dram_tensor("wabT", [D, 32], F32, kind="ExternalInput")
    woutT = nc.dram_tensor("woutT", [D, D], F32, kind="ExternalInput")
    wgT = nc.dram_tensor("wgT", [D, INTER], F32, kind="ExternalInput")
    wuT = nc.dram_tensor("wuT", [D, INTER], F32, kind="ExternalInput")
    wdT = nc.dram_tensor("wdT", [INTER, D], F32, kind="ExternalInput")

    smalls_in = {}
    for nm, shp in [("bqkv_s", [128, 32]), ("bz_s", [128, 16]), ("bout_s", [128, 16]),
                    ("convw_s", [128, 128]), ("convb_s", [128, 32]),
                    ("gs_s", [128, 44]), ("us_s", [128, 44]), ("ds_s", [128, 16]),
                    ("normw_s", [128, 1]), ("ba_s", [16, 1]), ("bb_s", [16, 1]), ("dtb_s", [16, 1]),
                    ("alog_s", [16, 1])]:
        smalls_in[nm] = nc.dram_tensor(nm, shp, F32, kind="ExternalInput")
    consts_f = {}
    for nm, shp in [("identf", [128, 128]), ("maskUf", [128, 128]),
                    ("ones16f", [16, 128]), ("ident16f", [16, 16]),
                    ("onesrow_f", [1, 128]), ("onescol_f", [128, 1]),
                    ("cvec", [128, 8])]:
        consts_f[nm] = nc.dram_tensor(nm, shp, F32, kind="ExternalInput")
    consts_b = {}
    for nm, shp in [("identb", [128, 128]), ("maskSnb", [128, 128]),
                    ("maskIb", [128, 128]), ("onescol_b", [128, 1])]:
        consts_b[nm] = nc.dram_tensor(nm, shp, BF16, kind="ExternalInput")

    sc_qkv = nc.dram_tensor("sc_qkv", [N_QKV * 128, D], BF16)
    sc_z = nc.dram_tensor("sc_z", [N_Z * 128, D], BF16)
    sc_wo = nc.dram_tensor("sc_wo", [N_WO * 128, D], BF16)
    sc_g = nc.dram_tensor("sc_g", [N_GU * 128, D], BF16)
    sc_u = nc.dram_tensor("sc_u", [N_GU * 128, D], BF16)
    sc_d = nc.dram_tensor("sc_d", [N_DN * 128, INTER], BF16)

    build_program._outs = {"y0": yout[0], "y1": yout[1]}
    dbg_t = {nm: nc.dram_tensor("dbg_" + nm, DBG_SHAPES[nm][0],
                              F32 if DBG_SHAPES[nm][1] == "f32" else BF16,
                              kind="ExternalOutput")
             for nm in dbg}
    build_program._dbg = dbg_t

    em = Em()
    em.sem("ssm", "spl0", "spl1", "spd0", "spd1", "swd0", "swd1", "swd2",
           "swd3", "sxd", "sxdb", "sod", "sodb", "sdbg", "spe", "sdve", "sact",
           "spool")

    with ExitStack() as stk:
        e = stk.enter_context
        xbuf = e(nc.sbuf_tensor("xbuf", [128, NCI * TT], F32))
        lnbuf = e(nc.sbuf_tensor("lnbuf", [128, NCI * TT], BF16))
        mixbuf = e(nc.sbuf_tensor("mixbuf", [128, N_QKV * MW], BF16))
        qkvbuf = e(nc.sbuf_tensor("qkvbuf", [128, N_QKV * TT], BF16))
        zgwbuf = e(nc.sbuf_tensor("zgwbuf", [128, NCI * TT], BF16))
        state = e(nc.sbuf_tensor("state", [128, HV * 128], F32))
        statebf = e(nc.sbuf_tensor("statebf", [128, HV * 128], BF16))
        WHALF = 4096
        wstream = e(nc.sbuf_tensor("wstream", [128, 2 * WHALF], BF16))
        sqtmp = e(nc.sbuf_tensor("sqtmp", [128, 2 * TT], BF16))
        tmpA = e(nc.sbuf_tensor("tmpA", [128, TT], F32))
        convacc = e(nc.sbuf_tensor("convacc", [128, TT], F32))
        rows = e(nc.sbuf_tensor("rows", [1, 1536], F32))
        tailbuf = e(nc.sbuf_tensor("tailbuf", [128, N_QKV * 3], BF16))
        zero128 = e(nc.sbuf_tensor("zero128", [128, 128], BF16))
        gfm = e(nc.sbuf_tensor("gfm", [16, TT], F32))
        lbfm = e(nc.sbuf_tensor("lbfm", [16, TT], F32))
        befm = e(nc.sbuf_tensor("befm", [16, TT], F32))
        abw = e(nc.sbuf_tensor("abw", [128, NCI * 32], BF16))
        NIP = 3
        scnames = ["G", "P2t", "P4t", "P8t", "RW0",
                   "RW1", "Y0", "Y1", "Khat", "Mb"]
        scn = {nm: e(nc.sbuf_tensor("sc_" + nm, [128, NIP * 256], BF16)) for nm in scnames}
        scn["E"] = scn["Y1"]
        scn["Vtm"] = scn["RW0"]
        scf = {nm: e(nc.sbuf_tensor("sf_" + nm, [128, NIP * 256], F32))
               for nm in ["TO", "O"]}
        scf["CN"] = scf["TO"]
        scf["B0"] = e(nc.sbuf_tensor("sf_B0", [128, NIP * 128], F32))
        scp = {nm: e(nc.sbuf_tensor("sp_" + nm, [128, NIP * 128],
                                    F32 if nm != "Ktm" else BF16))
               for nm in ["SkkMf", "SkqMf", "Ktm"]}

        def sfsl2(nm, ipp):
            return scp[nm][:, ipp * 128:(ipp + 1) * 128]
        sfsl2b = sfsl2
        ctile = e(nc.sbuf_tensor("ctile", [128, 2 * 8 * 16], F32))
        cs127 = e(nc.sbuf_tensor("cs127", [16, 2], F32))
        bc16 = e(nc.sbuf_tensor("bc16", [16, 2], F32))
        diagbc = e(nc.sbuf_tensor("diagbc", [16, 2 * 16], F32))
        varcol = e(nc.sbuf_tensor("varcol", [128, 2 * NIP], F32))
        bcb = e(nc.sbuf_tensor("bcb", [128, 2 * NIP], F32))
        rstdcol = e(nc.sbuf_tensor("rstdcol", [128, 2 * NIP], F32))
        small16 = e(nc.sbuf_tensor("small16", [16, 4], F32))
        smalls = {nm: e(nc.sbuf_tensor("sb_" + nm, list(t.shape), F32))
                  for nm, t in smalls_in.items()}
        cf = {nm: e(nc.sbuf_tensor("sb_" + nm, list(t.shape), F32))
              for nm, t in consts_f.items()}
        cb = {nm: e(nc.sbuf_tensor("sb_" + nm, list(t.shape), BF16))
              for nm, t in consts_b.items()}

        pb = [e(nc.psum_tensor(f"pb{i}", [128, 512], F32)) for i in range(6)]
        pxb = [e(nc.psum_tensor(f"pxb{i}", [128, 1024], BF16)) for i in range(2)]

        sems = {n: e(nc.semaphore(n)) for n in
                ["ssm", "spl0", "spl1", "spd0", "spd1", "swd0", "swd1", "swd2",
                 "swd3", "sxd", "sxdb", "sod", "sodb", "sdbg", "spe", "sdve",
                 "sact", "spool"]}
        block = e(nc.Block())

        fb = Banks([t[:, 0:512] for t in pb])
        bb = Banks([t[:, 0:256] for t in pxb])

        def mm(out, lhsT, rhs, start=True, stop=True):
            return lambda t: t.matmul(out, lhsT, rhs, start=start, stop=stop)

        def trans(out, in_, ident):
            return lambda t: t.transpose(out, in_, ident)

        def act(out, in_, func, bias=None, scale=None, accum=None):
            kw = {}
            if bias is not None:
                kw["bias"] = bias
            if scale is not None:
                kw["scale"] = scale
            if accum is not None:
                kw["accum_out"] = accum
            return lambda s: s.activation(out, in_, func, **kw)

        def tt(out, a, b, op):
            return lambda v: v.tensor_tensor(out, a, b, op=op)

        def ts(out, a, s1, op0, s2=None, op1=None):
            if op1 is None:
                return lambda v: v.tensor_scalar(out, a, scalar1=s1, scalar2=None, op0=op0)
            return lambda v: v.tensor_scalar(out, a, scalar1=s1, scalar2=s2, op0=op0, op1=op1)

        def stt(out, in0, s, in1, op0, op1):
            return lambda v: v.scalar_tensor_tensor(out, in0, s, in1, op0=op0, op1=op1)

        def cp(out, in_):
            return lambda v: v.tensor_copy(out, in_)

        def dma(out, in_):
            return lambda q: q.dma_start(out, in_)

        def xsl(ci):
            return xbuf[:, ci * TT:(ci + 1) * TT]

        def lnsl(ci, c0=0, w=TT):
            return lnbuf[:, ci * TT + c0:ci * TT + c0 + w]

        def msl(co, off, w):
            return mixbuf[:, co * MW + off:co * MW + off + w]

        def mdata(co, c0=0, w=TT):
            return msl(co, MD + c0, w)

        def qsl(co, c0=0, w=TT):
            return qkvbuf[:, co * TT + c0:co * TT + c0 + w]

        def zsl(ci, c0=0, w=TT):
            return zgwbuf[:, ci * TT + c0:ci * TT + c0 + w]

        def ssl(h):
            return state[:, h * 128:(h + 1) * 128]

        def sbsl(h):
            return statebf[:, h * 128:(h + 1) * 128]

        def scsl(nm, par):
            return scn[nm][:, par * 128:(par + 1) * 128]

        def sfsl(nm, par):
            return scf[nm][:, par * 128:(par + 1) * 128]

        def ct(par, slot):
            base = par * 8 * 16 + slot * 16
            return ctile[:, base:base + 16]

        K_ = lambda *a: tuple(a)

        def pxsl(bk, s):
            return bk["ap"][:, s * 128:(s + 1) * 128]

        for nm, t in smalls_in.items():
            em.op("sync", dma(smalls[nm][:], t.ap()), inc="ssm", inc_n=16)
        for nm, t in consts_f.items():
            em.op("sync", dma(cf[nm][:], t.ap()), inc="ssm", inc_n=16)
        for nm, t in consts_b.items():
            em.op("sync", dma(cb[nm][:], t.ap()), inc="ssm", inc_n=16)
        tok_smalls = ("ssm", em.cnt["ssm"])

        em.op("dve", lambda v: v.memset(zero128[:], 0.0), inc="sdve")
        t_e = em.op("act", act(small16[:, 0:1], smalls["alog_s"][:, 0:1], AF.Exp),
                    waits=[tok_smalls], inc="sact")
        em.op("dve", ts(small16[:, 0:1], small16[:, 0:1], -1.0, OP.mult),
              waits=[t_e], inc="sdve")
        em.op("dve", tt(small16[:, 1:2], smalls["ba_s"][:, 0:1],
                        smalls["dtb_s"][:, 0:1], OP.add),
              waits=[tok_smalls], inc="sdve")
        em.op("dve", ts(small16[:, 2:3], smalls["bb_s"][:, 0:1], -1.0, OP.mult),
              waits=[tok_smalls], inc="sdve")
        tok_small_done = em.op("dve", lambda v: v.memset(tailbuf[:], 0.0), inc="sdve")
        negea = small16[:, 0:1]
        dtbeff = small16[:, 1:2]
        negbb = small16[:, 2:3]

        pre_tok = {}
        pre_cast = [tok_small_done, tok_small_done]
        pre_store = [None, None]

        def prelude_weight(name, wt, sc, n_co, IN, kctr=[0]):
            pieces = [(i * 1024, min(1024, IN - i * 1024)) for i in range((IN + 1023) // 1024)]
            for co in range(n_co):
                for (r0, rl) in pieces:
                    par = kctr[0] % 2
                    kctr[0] += 1
                    pin = state[:, par * 1024: par * 1024 + rl]
                    pout = statebf[:, par * 1024: par * 1024 + rl]
                    src = wt.ap()[r0:r0 + rl, co * 128:(co + 1) * 128] \
                        .rearrange("(c p) q -> p c q", p=128)
                    t_ld = em.op("sync", dma(pin.rearrange("p (c q) -> p c q", q=128), src),
                                 waits=[pre_cast[par]], inc="spl%d" % par, inc_n=16)
                    t_c = em.op("act", act(pout, pin, AF.Copy),
                                waits=[t_ld, pre_store[par]], inc="sact")
                    pre_cast[par] = t_c
                    t_cd = em.op("act", lambda s: s.drain(), inc="sact")
                    t_st = em.op("pool", dma(sc.ap()[co * 128:(co + 1) * 128, r0:r0 + rl], pout),
                                 waits=[t_cd], inc="spd%d" % par, inc_n=16)
                    pre_store[par] = t_st
            pre_tok[name] = em.op("pool", None,
                                  waits=[("spd0", em.cnt["spd0"]),
                                         ("spd1", em.cnt["spd1"])],
                                  inc="spool")


        tok_state_cast = [tok_small_done] * HV
        tok_state_new = [tok_small_done] * HV

        wslot_tok = [None, None, None, None]
        wld_tok = [None, None, None, None]
        wslot_i = [0]
        wpair_i = [0]

        def wload1(sc_ap, waits_extra=()):
            s = wslot_i[0] % 4
            wslot_i[0] += 1
            tok = em.op("sync",
                        dma(wstream[:, s * 2048:(s + 1) * 2048], sc_ap),
                        waits=[wslot_tok[s]] + list(waits_extra),
                        inc="swd%d" % s, inc_n=16)
            wld_tok[s] = tok
            return s

        def wsl1(s, off, w):
            return wstream[:, s * 2048 + off: s * 2048 + off + w]

        def pe_drained(fn, waits=(), reads=()):
            em.op("pe", fn, waits=waits)
            tok = em.op("pe", lambda t: t.drain(), inc="spe")
            for k in reads:
                em.rd(k, tok)
            return tok

        def emit_ln(kind, get, key_of, tile_toks, out_writer):
            bS = fb.get()
            bQ = fb.get()
            onesc = cf["onescol_f"][:] if kind == "f32" else cb["onescol_b"][:]
            grp_base = None
            for ci in range(NCI):
                w0 = [bS["rtok"], tile_toks[ci]] if ci == 0 else [tile_toks[ci]]
                em.op("pe", mm(bS["ap"][0:1, :], onesc, get(ci),
                               start=(ci == 0), stop=(ci == NCI - 1)),
                      waits=w0, inc="spe" if ci == NCI - 1 else None)
            tokS = ("spe", em.cnt["spe"])
            for ci in range(NCI):
                em.rd(key_of(ci), tokS)
            for ci in range(NCI):
                par = ci % 2
                sqv = sqtmp[:, par * TT:(par + 1) * TT]
                t2 = em.op("act", act(sqv, get(ci), AF.Square),
                           waits=[tile_toks[ci]], inc="sact",
                           reads=[key_of(ci)], writes=[K_("sq", par)])
                w1 = [bQ["rtok"], t2] if ci == 0 else [t2]
                t3 = em.op("pe", mm(bQ["ap"][0:1, :], cb["onescol_b"][:], sqv,
                                    start=(ci == 0), stop=(ci == NCI - 1)),
                           waits=w1, inc="spe", reads=[K_("sq", par)])
            tokQ = ("spe", em.cnt["spe"])
            m_r, v_r, w_r = rows[0:1, 0:512], rows[0:1, 512:1024], rows[0:1, 1024:1536]
            t4 = em.op("dve", ts(m_r, bS["ap"][0:1, :], 1.0 / D, OP.mult),
                       waits=[tokS], inc="sdve", writes=[K_("rows", 0)])
            bS["rtok"] = t4
            t5 = em.op("dve", ts(v_r, bQ["ap"][0:1, :], 1.0 / D, OP.mult),
                       waits=[tokQ], inc="sdve", writes=[K_("rows", 1)])
            bQ["rtok"] = t5
            t6 = em.op("dve", tt(w_r, m_r, m_r, OP.mult), inc="sdve",
                       reads=[K_("rows", 0)], writes=[K_("rows", 2)])
            t7 = em.op("dve", tt(v_r, v_r, w_r, OP.subtract), inc="sdve",
                       reads=[K_("rows", 2)], writes=[K_("rows", 1)])
            t8 = em.op("act", act(w_r, v_r, AF.Ln, bias=cf["cvec"][0:1, 0:1]), waits=[t7], inc="sact",
                       reads=[K_("rows", 1)], writes=[K_("rows", 2)])
            t9 = em.op("act", act(v_r, w_r, AF.Exp, scale=-0.5), inc="sact",
                       reads=[K_("rows", 2)], writes=[K_("rows", 1)])
            t10 = em.op("dve", tt(w_r, m_r, v_r, OP.mult), waits=[t9], inc="sdve",
                        reads=[K_("rows", 0), K_("rows", 1)], writes=[K_("rows", 2)])
            bR = fb.get()
            bM = fb.get()
            tR = em.op("pe", mm(bR["ap"], cf["onesrow_f"][:], v_r),
                       waits=[bR["rtok"], t9], inc="spe", reads=[K_("rows", 1)])
            tM = em.op("pe", mm(bM["ap"], cf["onesrow_f"][:], w_r),
                       waits=[bM["rtok"], t10], inc="spe", reads=[K_("rows", 2)])
            out_toks = [out_writer(ci, bR["ap"], bM["ap"], tR, tM) for ci in range(NCI)]
            bR["rtok"] = out_toks[-1]
            bM["rtok"] = out_toks[-1]
            return out_toks

        def dump(name, src_ap, waits, keys=(), dst=None):
            if name in dbg and tile_now[0] == _DBG_TILE:
                t_d1 = em.op("dve", lambda v: v.drain(), waits=waits, inc="sdve")
                t_d2 = em.op("act", lambda s: s.drain(), waits=waits, inc="sact")
                em.op("pool", dma(dst if dst is not None else dbg_t[name].ap(), src_ap),
                      waits=[t_d1, t_d2], inc="sdbg", inc_n=16)
                t_f = em.op("pool", None, waits=[("sdbg", em.cnt["sdbg"])], inc="spool")
                for k in keys:
                    em.rd(k, t_f)

        tile_now = [0]

        for ci in range(NCI):
            src0 = xin[0].ap()[ci * 128:(ci + 1) * 128, 0:TT]
            em.op("act", dma(xsl(ci), src0), inc="sxd", inc_n=16,
                  writes=[K_("x", ci)])
        t_xf0 = em.op("act", None, waits=[("sxd", em.cnt["sxd"]), tok_smalls],
                      inc="sact")
        for ci in range(NCI):
            em.war[K_("x", ci)]["w"] = t_xf0
        tok_xload0 = [t_xf0] * NCI

        def ln1_writer0(ci, apR, apM, tR, tM, _toks=tok_xload0):
            em.op("dve", tt(tmpA[:], xsl(ci), apR, OP.mult),
                  waits=[tR, _toks[ci]], inc="sdve",
                  reads=[K_("x", ci)], writes=[K_("tmpA",)])
            return em.op("dve", tt(lnsl(ci), tmpA[:], apM, OP.subtract),
                         waits=[tM], inc="sdve",
                         reads=[K_("tmpA",)], writes=[K_("ln", ci)])
        tok_ln1_t0 = emit_ln("f32", xsl, lambda ci: K_("x", ci), tok_xload0,
                             ln1_writer0)

        prelude_weight("qkv", wqkvT, sc_qkv, N_QKV, D)
        prelude_weight("z", wzT, sc_z, N_Z, D)
        t_ld = em.op("sync", dma(state[:, 0:512].rearrange("p (c q) -> p c q", q=32),
                                 wabT.ap().rearrange("(c p) q -> p c q", p=128)),
                     waits=[pre_cast[0]], inc="spl0", inc_n=16)
        t_ab = em.op("act", act(abw[:], state[:, 0:512], AF.Copy),
                     waits=[t_ld, pre_store[0]], inc="sact")
        pre_cast[0] = t_ab
        pre_tok["ab"] = t_ab
        tok_prelude_head = em.op("pool", None,
                                 waits=[("spd0", em.cnt["spd0"]),
                                        ("spd1", em.cnt["spd1"])],
                                 inc="spool")

        def prelude_tail_gen():
            pins = [(tmpA[:, 0:512], sqtmp[:, 0:TT], K_("tmpA",), K_("sq", 0)),
                    (convacc[:, 0:512], sqtmp[:, TT:2 * TT], K_("ca0",),
                     K_("sq", 1))]
            lstore = [None, None]
            plan = []
            done_after = {}
            for name, wt, sc, n_co, IN in [("wo", woutT, sc_wo, N_WO, D),
                                           ("g", wgT, sc_g, N_GU, D),
                                           ("u", wuT, sc_u, N_GU, D),
                                           ("d", wdT, sc_d, N_DN, INTER)]:
                for co in range(n_co):
                    for r0 in range(0, IN, 512):
                        plan.append((wt, sc, co, r0))
                done_after[len(plan) - 1] = name
            for i, (wt, sc, co, r0) in enumerate(plan):
                par = i % 2
                pin, bout, kpin, kbout = pins[par]
                src = wt.ap()[r0:r0 + 512, co * 128:(co + 1) * 128] \
                    .rearrange("(c p) q -> p c q", p=128)
                t_ld2 = em.op("sync", dma(pin.rearrange("p (c q) -> p c q", q=128), src),
                              inc="spl%d" % par, inc_n=16, writes=[kpin])
                if par == 0:
                    t_c2 = em.op("act", act(bout, pin, AF.Copy),
                                 waits=[t_ld2, lstore[par]], inc="sact",
                                 reads=[kpin], writes=[kbout])
                    st_eng = "act"
                else:
                    t_c2 = em.op("dve", cp(bout, pin),
                                 waits=[t_ld2, lstore[par]], inc="sdve",
                                 reads=[kpin], writes=[kbout])
                    st_eng = "sync"
                t_st2 = em.op(st_eng, dma(sc.ap()[co * 128:(co + 1) * 128,
                                                  r0:r0 + 512], bout),
                              waits=[t_c2], inc="spd%d" % par, inc_n=16)
                em.rd(kbout, t_st2)
                lstore[par] = t_st2
                if i in done_after:
                    pre_tok[done_after[i]] = em.op(
                        "pool", None,
                        waits=[("spd0", em.cnt["spd0"]),
                               ("spd1", em.cnt["spd1"])],
                        inc="spool")
                if i % 2 == 1:
                    yield

        tok_g = tok_lb = tok_be = tok_small_done
        pending_x = None

        for tile in range(min(NTILE, _MAXTILE)):
            tile_now[0] = tile
            b = tile // 4
            sc4 = tile % 4
            tok0 = sc4 * TT
            seq_start = (sc4 == 0)

            if tile == 0:
                tok_ln1 = tok_ln1_t0
            else:
                tok_xload = pending_x

                def ln1_writer(ci, apR, apM, tR, tM, _toks=tok_xload):
                    em.op("dve", tt(tmpA[:], xsl(ci), apR, OP.mult),
                          waits=[tR, _toks[ci]], inc="sdve",
                          reads=[K_("x", ci)], writes=[K_("tmpA",)])
                    return em.op("dve", tt(lnsl(ci), tmpA[:], apM, OP.subtract),
                                 waits=[tM], inc="sdve",
                                 reads=[K_("tmpA",)], writes=[K_("ln", ci)])
                tok_ln1 = emit_ln("f32", xsl, lambda ci: K_("x", ci), tok_xload,
                                  ln1_writer)
            dump("ln1", lnbuf[:], [tok_ln1[-1]], keys=[K_("ln", i) for i in range(NCI)])

            tok_conv = []
            for co in range(N_QKV):
                par = wload1(sc_qkv.ap()[co * 128:(co + 1) * 128, :],
                             waits_extra=[pre_tok["qkv"]])
                bk = fb.get()
                for ci in range(NCI):
                    w = [bk["rtok"], wld_tok[par], tok_ln1[ci]] if ci == 0 \
                        else [tok_ln1[ci]]
                    em.op("pe", mm(bk["ap"], wsl1(par, ci * 128, 128), lnsl(ci),
                                   start=(ci == 0), stop=(ci == NCI - 1)),
                          waits=w, inc="spe" if ci == NCI - 1 else None)
                tok_mm = ("spe", em.cnt["spe"])
                wslot_tok[par] = tok_mm
                for ci in range(NCI):
                    em.rd(K_("ln", ci), tok_mm)
                t_ev = em.op("act", act(mdata(co), bk["ap"], AF.Identity,
                                        bias=smalls["bqkv_s"][:, co:co + 1]),
                             waits=[tok_mm], inc="sact", writes=[K_("mixd", co)])
                bk["rtok"] = t_ev
                if seq_start:
                    t_h = em.op("pool", lambda v, _co=co: v.memset(msl(_co, 1, 3), 0.0),
                                inc="spool", writes=[K_("mixh", co)])
                else:
                    t_h = em.op("pool", cp(msl(co, 1, 3), tailbuf[:, co * 3:co * 3 + 3]),
                                inc="spool", writes=[K_("mixh", co)], reads=[K_("tail", co)])
                em.op("pool", cp(tailbuf[:, co * 3:co * 3 + 3], msl(co, 513, 3)),
                      waits=[t_ev], inc="spool", reads=[K_("mixd", co)],
                      writes=[K_("tail", co)])
                a0 = convacc[:, 0:TT]
                a1 = tmpA[:, 0:TT]
                em.op("dve", ts(a0, msl(co, 1, TT),
                                smalls["convw_s"][:, co * 4:co * 4 + 1], OP.mult),
                      waits=[t_ev, t_h], inc="sdve",
                      reads=[K_("mixd", co), K_("mixh", co)], writes=[K_("ca0",)])
                em.op("dve", stt(a1, msl(co, 2, TT), smalls["convw_s"][:, co * 4 + 1:co * 4 + 2],
                                 a0, OP.mult, OP.add), inc="sdve",
                      reads=[K_("mixd", co), K_("mixh", co), K_("ca0",)], writes=[K_("tmpA",)])
                em.op("dve", stt(a0, msl(co, 3, TT), smalls["convw_s"][:, co * 4 + 2:co * 4 + 3],
                                 a1, OP.mult, OP.add), inc="sdve",
                      reads=[K_("mixd", co), K_("tmpA",)], writes=[K_("ca0",)])
                t_c4 = em.op("dve", stt(a1, msl(co, 4, TT),
                                        smalls["convw_s"][:, co * 4 + 3:co * 4 + 4],
                                        a0, OP.mult, OP.add), inc="sdve",
                             reads=[K_("mixd", co), K_("ca0",)], writes=[K_("tmpA",)])
                t_s = em.op("act", act(qsl(co), a1, AF.Silu,
                                       bias=smalls["convb_s"][:, co:co + 1]),
                            waits=[t_c4], inc="sact",
                            reads=[K_("tmpA",)], writes=[K_("qkv", co)])
                tok_conv.append(t_s)
            dump("mixed", qkvbuf[:], [tok_conv[-1]], keys=[K_("qkv", i) for i in range(N_QKV)])

            tok_qn = []
            for j in range(16):
                is_q = j < 8
                par = j % 2
                sqv = sqtmp[:, par * TT:(par + 1) * TT]
                t_sq = em.op("act", act(sqv, qsl(j), AF.Square),
                             waits=[tok_conv[j]], inc="sact",
                             reads=[K_("qkv", j)], writes=[K_("sq", par)])
                bk = fb.get()
                t_mm = em.op("pe", mm(bk["ap"][0:1, :], cb["onescol_b"][:], sqv),
                             waits=[bk["rtok"], t_sq], inc="spe", reads=[K_("sq", par)])
                t_l = em.op("act", act(rows[0:1, 0:512], bk["ap"][0:1, :], AF.Ln,
                                       bias=cf["cvec"][0:1, 1:2]),
                            waits=[t_mm], inc="sact", writes=[K_("rows", 0)])
                bk["rtok"] = t_l
                bias = cf["cvec"][0:1, 3:4] if is_q else None
                t_r = em.op("act", act(rows[0:1, 512:1024], rows[0:1, 0:512], AF.Exp,
                                       scale=-0.5, bias=bias), inc="sact",
                            reads=[K_("rows", 0)], writes=[K_("rows", 1)])
                bk2 = fb.get()
                t_b = em.op("pe", mm(bk2["ap"], cf["onesrow_f"][:], rows[0:1, 512:1024]),
                            waits=[bk2["rtok"], t_r], inc="spe", reads=[K_("rows", 1)])
                t_ap = em.op("dve", tt(mdata(j), qsl(j), bk2["ap"], OP.mult),
                             waits=[t_b], inc="sdve",
                             reads=[K_("qkv", j)], writes=[K_("mixd", j)])
                bk2["rtok"] = t_ap
                tok_qn.append(t_ap)
            if "qn" in dbg and tile == _DBG_TILE:
                for j in range(16):
                    dump("qn", mdata(j), [tok_qn[j]], keys=[K_("mixd", j)],
                         dst=dbg_t["qn"].ap()[:, j * TT:(j + 1) * TT])

            tok_zgw = []
            for co in range(N_Z):
                par = wload1(sc_z.ap()[co * 128:(co + 1) * 128, :],
                             waits_extra=[pre_tok["z"]])
                bk = fb.get()
                for ci in range(NCI):
                    w = [bk["rtok"], wld_tok[par], tok_ln1[ci]] if ci == 0 else [tok_ln1[ci]]
                    em.op("pe", mm(bk["ap"], wsl1(par, ci * 128, 128), lnsl(ci),
                                   start=(ci == 0), stop=(ci == NCI - 1)),
                          waits=w, inc="spe" if ci == NCI - 1 else None)
                tok_mm = ("spe", em.cnt["spe"])
                wslot_tok[par] = tok_mm
                for ci in range(NCI):
                    em.rd(K_("ln", ci), tok_mm)
                t_s = em.op("act", act(tmpA[:], bk["ap"], AF.Silu,
                                       bias=smalls["bz_s"][:, co:co + 1]),
                            waits=[tok_mm], inc="sact", writes=[K_("tmpA",)])
                bk["rtok"] = t_s
                t_g_ = em.op("dve", ts(zsl(co), tmpA[:], smalls["normw_s"][:, 0:1], OP.mult),
                             waits=[t_s], inc="sdve",
                             reads=[K_("tmpA",)], writes=[K_("zg", co)])
                tok_zgw.append(t_g_)

            if "zgw" in dbg and tile == _DBG_TILE:
                dump("zgw", zgwbuf[:], [tok_zgw[-1]], keys=[K_("zg", i) for i in range(NCI)])
            bkA2 = fb.get()
            for ci in range(NCI):
                w = [bkA2["rtok"], pre_tok["ab"], tok_ln1[ci]] if ci == 0 else [tok_ln1[ci]]
                em.op("pe", mm(bkA2["ap"][0:16, :], abw[:, ci * 32:ci * 32 + 16], lnsl(ci),
                               start=(ci == 0), stop=(ci == NCI - 1)),
                      waits=w, inc="spe" if ci == NCI - 1 else None)
            tok_a2 = ("spe", em.cnt["spe"])
            bkB2 = fb.get()
            for ci in range(NCI):
                w = [bkB2["rtok"]] if ci == 0 else []
                em.op("pe", mm(bkB2["ap"][0:16, :], abw[:, ci * 32 + 16:ci * 32 + 32], lnsl(ci),
                               start=(ci == 0), stop=(ci == NCI - 1)),
                      waits=w, inc="spe" if ci == NCI - 1 else None)
            tok_ab = ("spe", em.cnt["spe"])
            for ci in range(NCI):
                em.rd(K_("ln", ci), tok_ab)
            psa = bkA2["ap"][0:16, :]
            psb = bkB2["ap"][0:16, :]
            sA = tmpA[0:16, :]
            sB = convacc[0:16, 0:TT]
            em.op("act", act(sA, psa, AF.Abs, bias=dtbeff),
                  waits=[tok_a2], inc="sact", writes=[K_("tmpA",)])
            em.op("act", act(sB, sA, AF.Exp, scale=-1.0), inc="sact",
                  reads=[K_("tmpA",)], writes=[K_("ca0",)])
            em.op("act", act(sA, sB, AF.Ln, bias=cf["cvec"][0:16, 2:3]), inc="sact",
                  reads=[K_("ca0",)], writes=[K_("tmpA",)])
            t_g1 = em.op("act", act(sB, psa, AF.Relu, bias=dtbeff), inc="sact",
                         writes=[K_("ca0",)])
            t_g2 = em.op("dve", tt(gfm[:], sA, sB, OP.add),
                         waits=[t_g1], inc="sdve",
                         reads=[K_("tmpA",), K_("ca0",)], writes=[K_("gfm",)])
            tok_g = em.op("dve", ts(gfm[:], gfm[:], negea, OP.mult), inc="sdve",
                          writes=[K_("gfm",)])
            bkA2["rtok"] = t_g1
            em.op("act", act(sA, psb, AF.Abs, bias=smalls["bb_s"][:, 0:1]),
                  waits=[tok_ab, t_g2], inc="sact", writes=[K_("tmpA",)])
            em.op("act", act(sB, sA, AF.Exp, scale=-1.0), inc="sact",
                  reads=[K_("tmpA",)], writes=[K_("ca0",)])
            em.op("act", act(sA, sB, AF.Ln, bias=cf["cvec"][0:16, 2:3]), inc="sact",
                  reads=[K_("ca0",)], writes=[K_("tmpA",)])
            t_b1 = em.op("act", act(sB, psb, AF.Relu, scale=-1.0, bias=negbb),
                         inc="sact", writes=[K_("ca0",)])
            t_b2 = em.op("dve", tt(lbfm[:], sA, sB, OP.add),
                         waits=[t_b1], inc="sdve",
                         reads=[K_("tmpA",), K_("ca0",)], writes=[K_("lbfm",)])
            tok_lb = em.op("dve", ts(lbfm[:], lbfm[:], -1.0, OP.mult), inc="sdve",
                           writes=[K_("lbfm",)])
            tok_be = em.op("act", act(befm[:], psb, AF.Sigmoid,
                                      bias=smalls["bb_s"][:, 0:1]),
                           waits=[tok_ab], inc="sact", writes=[K_("befm",)])
            bkB2["rtok"] = tok_be
            if "gclb" in dbg and tile == _DBG_TILE:
                dump("gclb", gfm[:], [tok_g], keys=[K_("gfm",)], dst=dbg_t["gclb"].ap()[:, 0:TT])
                dump("gclb", lbfm[:], [tok_lb], keys=[K_("lbfm",)], dst=dbg_t["gclb"].ap()[:, TT:2 * TT])
                dump("gclb", befm[:], [tok_be], keys=[K_("befm",)], dst=dbg_t["gclb"].ap()[:, 2 * TT:3 * TT])

            if seq_start:
                w_pre = [tok_prelude_head, pre_cast[0], pre_cast[1]] \
                    if tile == 0 else []
                t_ms1 = em.op("dve", lambda v: v.memset(state[:], 0.0),
                              waits=w_pre + [tok_state_cast[HV - 1]] + tok_state_new,
                              inc="sdve", writes=[K_("state",)])
                t_ms2 = em.op("act", act(statebf[:], state[:], AF.Copy),
                              waits=[t_ms1], inc="sact",
                              reads=[K_("state",)], writes=[K_("statebf",)])
                tok_state_cast = [t_ms2] * HV
                tok_state_new = [t_ms1] * HV

            prepres = {}
            pairtok = {}

            def prep_gen(c):
                par = c % 2
                c0 = c * 128
                bkp = fb.get()
                t_tr = pe_drained(trans(bkp["ap"][:, 0:16], gfm[:, c0:c0 + 128],
                                        cf["identf"][0:16, 0:16]),
                                  waits=[bkp["rtok"], tok_g], reads=[K_("gfm",)])
                t_ev = em.op("dve", cp(ct(par, 0), bkp["ap"][:, 0:16]), waits=[t_tr],
                             inc="sdve", writes=[K_("ct", par, 0)])
                bkp["rtok"] = t_ev
                yield
                bkc = fb.get()
                t_cs = pe_drained(mm(bkc["ap"][:, 0:16], cf["maskUf"][:], ct(par, 0)),
                                  waits=[bkc["rtok"], t_ev], reads=[K_("ct", par, 0)])
                t_cse = em.op("dve", cp(ct(par, 1), bkc["ap"][:, 0:16]), waits=[t_cs],
                              inc="sdve", writes=[K_("ct", par, 1)])
                bkc["rtok"] = t_cse
                yield
                bkl = fb.get()
                t_tl = pe_drained(trans(bkl["ap"][:, 0:16], lbfm[:, c0:c0 + 128],
                                        cf["identf"][0:16, 0:16]),
                                  waits=[bkl["rtok"], tok_lb], reads=[K_("lbfm",)])
                t_le = em.op("dve", cp(ct(par, 2), bkl["ap"][:, 0:16]), waits=[t_tl],
                             inc="sdve", writes=[K_("ct", par, 2)])
                bkl["rtok"] = t_le
                t_phi = em.op("dve", tt(ct(par, 3), ct(par, 1), ct(par, 2), OP.add),
                              waits=[t_cse, t_le], inc="sdve",
                              reads=[K_("ct", par, 1), K_("ct", par, 2)],
                              writes=[K_("ct", par, 3)])
                yield
                bkb = fb.get()
                t_tb = pe_drained(trans(bkb["ap"][:, 0:16], befm[:, c0:c0 + 128],
                                        cf["identf"][0:16, 0:16]),
                                  waits=[bkb["rtok"], tok_be], reads=[K_("befm",)])
                t_bev = em.op("dve", cp(ct(par, 4), bkb["ap"][:, 0:16]), waits=[t_tb],
                              inc="sdve", writes=[K_("ct", par, 4)])
                bkb["rtok"] = t_bev
                t_bc = em.op("act", act(ct(par, 5), ct(par, 1), AF.Exp), waits=[t_cse],
                             inc="sact", reads=[K_("ct", par, 1)], writes=[K_("ct", par, 5)])
                t_nb = em.op("dve", ts(ct(par, 6), ct(par, 5), -1.0, OP.mult),
                             waits=[t_bc], inc="sdve",
                             reads=[K_("ct", par, 5)], writes=[K_("ct", par, 6)])
                yield
                bkt = fb.get()
                t_ctr0 = pe_drained(trans(bkt["ap"][0:16, 0:128], ct(par, 1), cf["identf"][:]),
                                    waits=[bkt["rtok"], t_cse], reads=[K_("ct", par, 1)])
                t_c127 = em.op("dve", cp(cs127[:, par:par + 1], bkt["ap"][0:16, 127:128]),
                               waits=[t_ctr0], inc="sdve", writes=[K_("cs127", par)])
                bkt["rtok"] = t_c127
                t_bc16 = em.op("act", act(bc16[:, par:par + 1], cs127[:, par:par + 1], AF.Exp),
                               waits=[t_c127], inc="sact",
                               reads=[K_("cs127", par)], writes=[K_("bc16", par)])
                t_diag = em.op("dve", ts(diagbc[:, par * 16:(par + 1) * 16], cf["ident16f"][:],
                                         bc16[:, par:par + 1], OP.mult),
                               waits=[t_bc16], inc="sdve",
                               reads=[K_("bc16", par)], writes=[K_("diag", par)])
                yield
                bkr = fb.get()
                t_br = pe_drained(mm(bkr["ap"][:, 0:16], cf["ones16f"][:],
                                     diagbc[:, par * 16:(par + 1) * 16]),
                                  waits=[bkr["rtok"], t_diag], reads=[K_("diag", par)])
                t_bre = em.op("dve", cp(ct(par, 7), bkr["ap"][:, 0:16]), waits=[t_br],
                              inc="sdve", writes=[K_("ct", par, 7)])
                bkr["rtok"] = t_bre
                if "sct" in dbg and tile == _DBG_TILE and c == 0:
                    dump("sct", ctile[:, 0:128], [t_bre, t_phi, t_nb, t_bev],
                         keys=[K_("ct", 0, s) for s in range(8)])
                if "cs0" in dbg and tile == _DBG_TILE and c == 0:
                    dump("cs0", ct(par, 1), [t_cse], keys=[K_("ct", par, 1)])
                prepres[c] = dict(par=par, t_phi=t_phi, t_cse=t_cse, t_nb=t_nb,
                                  t_bev=t_bev, t_bc=t_bc, t_bre=t_bre)

            def pair_gen(c, kp):
                while c not in prepres:
                    yield
                gp = c * HK + kp
                while gp >= NIP and (gp - NIP) not in inst_done:
                    yield
                P = prepres[c]
                par = P["par"]
                c0 = c * 128
                t_phi, t_cse, t_nb = P["t_phi"], P["t_cse"], P["t_nb"]
                t_bev, t_bc, t_bre = P["t_bev"], P["t_bc"], P["t_bre"]
                h0 = 2 * kp
                ip = gp % NIP
                hs = [h0, h0 + 1]

                def ccol(slot, hh, _p=par):
                    return ct(_p, slot)[:, hh:hh + 1]

                def pr(nm, sub=None, w=128):
                    base = ip * 256 + (0 if sub is None else sub * 128)
                    return scn[nm][:, base:base + (256 if sub is None else w)]

                def prf(nm, sub=None, w=128):
                    base = ip * 256 + (0 if sub is None else sub * 128)
                    return scf[nm][:, base:base + (256 if sub is None else w)]

                KALIAS = {"E": "Y1", "Vtm": "RW0", "CN": "TO"}
                KP = lambda nm: K_(KALIAS.get(nm, nm), ip)
                Qf = mdata(kp, c0, 128)
                Kf = mdata(8 + kp, c0, 128)
                bk1 = fb.get()
                t1 = em.op("pe", mm(bk1["ap"][:, 0:128], Kf, Kf),
                           waits=[bk1["rtok"], tok_qn[8 + kp]], inc="spe",
                           reads=[K_("mixd", 8 + kp)])
                tS1 = em.op("dve", tt(sfsl2("SkkMf", ip), bk1["ap"][:, 0:128],
                                      cb["maskSnb"][:], OP.mult),
                            waits=[t1], inc="sdve", writes=[K_("SkkM", ip)])
                bk1["rtok"] = tS1
                yield
                bk2 = fb.get()
                t2 = em.op("pe", mm(bk2["ap"][:, 0:128], Kf, Qf),
                           waits=[bk2["rtok"], tok_qn[kp], tok_qn[8 + kp]], inc="spe",
                           reads=[K_("mixd", kp), K_("mixd", 8 + kp)])
                tS2 = em.op("dve", tt(sfsl2("SkqMf", ip), bk2["ap"][:, 0:128],
                                      cb["maskIb"][:], OP.mult),
                            waits=[t2], inc="sdve", writes=[K_("SkqM", ip)])
                bk2["rtok"] = tS2
                yield
                bkT = bb.get()
                t3 = em.op("pe", trans(pxsl(bkT, 0), Kf, cb["identb"][:]),
                           waits=[bkT["rtok"], tok_qn[8 + kp]], inc="spe",
                           reads=[K_("mixd", 8 + kp)])
                tKtm = em.op("act", act(sfsl2b("Ktm", ip), pxsl(bkT, 0), AF.Copy),
                             waits=[t3], inc="sact", writes=[K_("Ktm", ip)])
                bkT["rtok"] = tKtm
                yield
                tG = []
                tMb = []
                tKh = []
                for (cslot, Ename, tcol) in [(3, "Es", t_phi), (1, "Ei", t_cse)]:
                    bkB = fb.get()
                    tb0a = em.op("act", act(scf["B0"][:, ip * 128:(ip + 1) * 128],
                                            zero128[:], AF.Identity,
                                            bias=ccol(cslot, hs[0])),
                                 waits=[tcol], inc="sact",
                                 reads=[K_("ct", par, cslot)], writes=[K_("B0", ip)])
                    tt1 = em.op("pe", trans(bkB["ap"][:, 0:128],
                                            scf["B0"][:, ip * 128:(ip + 1) * 128],
                                            cf["identf"][:]),
                                waits=[bkB["rtok"], tb0a], inc="spe", reads=[K_("B0", ip)])
                    tb0b = em.op("act", act(scf["B0"][:, ip * 128:(ip + 1) * 128],
                                            zero128[:], AF.Identity,
                                            bias=ccol(cslot, hs[1])),
                                 waits=[tcol], inc="sact",
                                 reads=[K_("ct", par, cslot)], writes=[K_("B0", ip)])
                    tt2 = em.op("pe", trans(bkB["ap"][:, 128:256],
                                            scf["B0"][:, ip * 128:(ip + 1) * 128],
                                            cf["identf"][:]),
                                waits=[tb0b], inc="spe", reads=[K_("B0", ip)])
                    te1 = em.op("dve", ts(prf("TO", 0), bkB["ap"][:, 0:128],
                                          ccol(1, hs[0]), OP.subtract, 0.0, OP.min),
                                waits=[tt2, t_cse], inc="sdve",
                                reads=[K_("ct", par, 1)], writes=[KP("TO")])
                    te2 = em.op("dve", ts(prf("TO", 1), bkB["ap"][:, 128:256],
                                          ccol(1, hs[1]), OP.subtract, 0.0, OP.min),
                                waits=[tt2], inc="sdve",
                                reads=[K_("ct", par, 1)], writes=[KP("TO")])
                    bkB["rtok"] = te2
                    tE = em.op("act", act(pr("E"), prf("TO"), AF.Exp),
                               waits=[te1, te2], inc="sact",
                               reads=[KP("TO")], writes=[KP("E")])
                    yield
                    if Ename == "Es":
                        for s in (0, 1):
                            tG.append(em.op("pool", tt(pr("G", s), sfsl2("SkkMf", ip),
                                                       pr("E", s), OP.mult),
                                            waits=[tS1, tE], inc="spool",
                                            reads=[K_("SkkM", ip), KP("E")],
                                            writes=[KP("G")]))
                    else:
                        tEi = tE
                        for s in (0, 1):
                            tMb.append(em.op("pool", tt(pr("Mb", s), sfsl2("SkqMf", ip),
                                                        pr("E", s), OP.mult),
                                             waits=[tS2, tE], inc="spool",
                                             reads=[K_("SkqM", ip), KP("E")],
                                             writes=[KP("Mb")]))
                            tb = em.op("dve", cp(bcb[:, 2 * ip + s:2 * ip + s + 1],
                                                 pr("E", s)[:, 127:128]),
                                       waits=[tE], inc="sdve",
                                       reads=[KP("E")], writes=[K_("bcb", ip, s)])
                            tKh.append(em.op("pool", ts(pr("Khat", s), sfsl2b("Ktm", ip),
                                                        bcb[:, 2 * ip + s:2 * ip + s + 1],
                                                        OP.mult),
                                             waits=[tKtm, tb], inc="spool",
                                             reads=[K_("Ktm", ip), K_("bcb", ip, s)],
                                             writes=[KP("Khat")]))
                yield
                bkV = bb.get()
                tv1 = em.op("pe", trans(pxsl(bkV, 0), qsl(16 + hs[0], c0, 128),
                                        cb["identb"][:]),
                            waits=[bkV["rtok"], tok_conv[16 + hs[0]]], inc="spe",
                            reads=[K_("qkv", 16 + hs[0])])
                tv2 = em.op("pe", trans(pxsl(bkV, 1), qsl(16 + hs[1], c0, 128),
                                        cb["identb"][:]),
                            waits=[tok_conv[16 + hs[1]]], inc="spe",
                            reads=[K_("qkv", 16 + hs[1])])
                tVe = em.op("act", act(pr("Vtm"), bkV["ap"][:, 0:256], AF.Copy),
                            waits=[tv2], inc="sact", writes=[KP("Vtm")])
                bkV["rtok"] = tVe
                yield
                bkK = fb.get()
                tk1 = em.op("pe", mm(bkK["ap"][:, 0:128], Kf, sbsl(hs[0])),
                            waits=[bkK["rtok"], tok_state_cast[hs[0]], tok_qn[8 + kp]],
                            inc="spe", reads=[K_("mixd", 8 + kp), K_("statebf", hs[0])])
                tk2 = em.op("pe", mm(bkK["ap"][:, 128:256], Kf, sbsl(hs[1])),
                            waits=[tok_state_cast[hs[1]]], inc="spe",
                            reads=[K_("statebf", hs[1])])
                tr1 = em.op("dve", stt(prf("TO", 0), bkK["ap"][:, 0:128], ccol(6, hs[0]),
                                       pr("Vtm", 0), OP.mult, OP.add),
                            waits=[tk2, t_nb, tVe], inc="sdve",
                            reads=[K_("ct", par, 6), KP("Vtm")], writes=[KP("TO")])
                tr2 = em.op("dve", stt(prf("TO", 1), bkK["ap"][:, 128:256], ccol(6, hs[1]),
                                       pr("Vtm", 1), OP.mult, OP.add),
                            inc="sdve", reads=[K_("ct", par, 6), KP("Vtm")],
                            writes=[KP("TO")])
                bkK["rtok"] = tr2
                ty1 = em.op("dve", ts(pr("Y0", 0), prf("TO", 0), ccol(4, hs[0]), OP.mult),
                            waits=[t_bev], inc="sdve",
                            reads=[KP("TO"), K_("ct", par, 4)], writes=[KP("Y0")])
                t_y = em.op("dve", ts(pr("Y0", 1), prf("TO", 1), ccol(4, hs[1]), OP.mult),
                            inc="sdve", reads=[KP("TO"), K_("ct", par, 4)],
                            writes=[KP("Y0")])
                yield
                bkq = fb.get()
                tq1 = em.op("pe", mm(bkq["ap"][:, 0:128], Qf, sbsl(hs[0])),
                            waits=[bkq["rtok"], tok_state_cast[hs[0]], tok_qn[kp]],
                            inc="spe", reads=[K_("mixd", kp), K_("statebf", hs[0])])
                tq2 = em.op("pe", mm(bkq["ap"][:, 128:256], Qf, sbsl(hs[1])),
                            waits=[tok_state_cast[hs[1]]], inc="spe",
                            reads=[K_("statebf", hs[1])])
                tt1_ = em.op("dve", ts(prf("TO", 0), bkq["ap"][:, 0:128], ccol(5, hs[0]),
                                       OP.mult),
                             waits=[tq2, t_bc], inc="sdve",
                             reads=[K_("ct", par, 5)], writes=[KP("TO")])
                tt2_ = em.op("dve", ts(prf("TO", 1), bkq["ap"][:, 128:256], ccol(5, hs[1]),
                                       OP.mult),
                             inc="sdve", reads=[K_("ct", par, 5)], writes=[KP("TO")])
                bkq["rtok"] = tt2_
                yield
                bkA = bb.get()
                ta1 = em.op("pe", trans(pxsl(bkA, 0), pr("G", 0),
                                        cb["identb"][:]),
                            waits=[bkA["rtok"], tG[0]], inc="spe", reads=[KP("G")])
                ta2 = em.op("pe", trans(pxsl(bkA, 1), pr("G", 1), cb["identb"][:]),
                            waits=[tG[1]], inc="spe", reads=[KP("G")])
                tAe = em.op("act", act(pr("RW1"), bkA["ap"][:, 0:256], AF.Copy),
                            waits=[ta2], inc="sact", writes=[KP("RW1")])
                bkA["rtok"] = tAe
                yield
                ycur, ynext = "Y0", "Y1"
                apT0, apT1, apTok, apKey = pr("G", 0), pr("G", 1), tG[1], KP("G")
                rwT0, rwT1, rwTok, rwKey = pr("RW1", 0), pr("RW1", 1), tAe, KP("RW1")
                rot = [("P2t", "RW0"), ("P4t", "P8t")]
                for lv in range(7):
                    bks = fb.get()
                    em.op("pe", mm(bks["ap"][:, 0:128], apT0, pr(ycur, 0),
                                   start=True, stop=False),
                          waits=[bks["rtok"], apTok, t_y], inc="spe",
                          reads=[apKey, KP(ycur)])
                    em.op("pe", mm(bks["ap"][:, 0:128], cb["identb"][:], pr(ycur, 0),
                                   start=False, stop=True), inc="spe", reads=[KP(ycur)])
                    em.op("pe", mm(bks["ap"][:, 128:256], apT1, pr(ycur, 1),
                                   start=True, stop=False), inc="spe",
                          reads=[apKey, KP(ycur)])
                    ts2 = em.op("pe", mm(bks["ap"][:, 128:256], cb["identb"][:],
                                         pr(ycur, 1), start=False, stop=True),
                                inc="spe", reads=[KP(ycur)])
                    eng_y = "dve" if lv % 2 == 0 else "act"
                    t_y = em.op(eng_y,
                                (cp(pr(ynext), bks["ap"][:, 0:256]) if eng_y == "dve"
                                 else act(pr(ynext), bks["ap"][:, 0:256], AF.Copy)),
                                waits=[ts2], inc="s" + eng_y, writes=[KP(ynext)])
                    bks["rtok"] = t_y
                    ycur, ynext = ynext, ycur
                    if lv < 6:
                        ptn, prn = rot[lv % 2]
                        bkp2 = fb.get()
                        em.op("pe", mm(bkp2["ap"][:, 0:128], rwT0, apT0),
                              waits=[bkp2["rtok"], rwTok, apTok], inc="spe",
                              reads=[rwKey, apKey])
                        tb2 = em.op("pe", mm(bkp2["ap"][:, 128:256], rwT1, apT1),
                                    inc="spe", reads=[rwKey, apKey])
                        eng_t = "act" if lv % 2 == 0 else "dve"
                        tev1 = em.op(eng_t,
                                     (act(pr(ptn), bkp2["ap"][:, 0:256], AF.Copy)
                                      if eng_t == "act"
                                      else cp(pr(ptn), bkp2["ap"][:, 0:256])),
                                     waits=[tb2], inc="s" + eng_t, writes=[KP(ptn)])
                        bkp2["rtok"] = tev1
                        if lv < 5:
                            bkp3 = fb.get()
                            em.op("pe", mm(bkp3["ap"][:, 0:128], apT0, rwT0),
                                  waits=[bkp3["rtok"], rwTok, apTok], inc="spe",
                                  reads=[rwKey, apKey])
                            tb4 = em.op("pe", mm(bkp3["ap"][:, 128:256], apT1, rwT1),
                                        inc="spe", reads=[rwKey, apKey])
                            eng_r = "dve" if lv % 2 == 0 else "act"
                            tev2 = em.op(eng_r,
                                         (cp(pr(prn), bkp3["ap"][:, 0:256])
                                          if eng_r == "dve"
                                          else act(pr(prn), bkp3["ap"][:, 0:256],
                                                   AF.Copy)),
                                         waits=[tb4], inc="s" + eng_r,
                                         writes=[KP(prn)])
                            bkp3["rtok"] = tev2
                            rwT0, rwT1, rwTok, rwKey = (pr(prn, 0), pr(prn, 1),
                                                        tev2, KP(prn))
                        apT0, apT1, apTok, apKey = (pr(ptn, 0), pr(ptn, 1),
                                                    tev1, KP(ptn))
                    yield
                Ukey = KP(ycur)
                t_U = t_y
                bkm = fb.get()
                tm1 = em.op("pe", mm(bkm["ap"][:, 0:128], pr("Mb", 0), pr(ycur, 0)),
                            waits=[bkm["rtok"], tMb[0], t_U], inc="spe",
                            reads=[KP("Mb"), Ukey])
                tm2 = em.op("pe", mm(bkm["ap"][:, 128:256], pr("Mb", 1), pr(ycur, 1)),
                            waits=[tMb[1]], inc="spe", reads=[KP("Mb"), Ukey])
                t_O = em.op("dve", tt(prf("O"), prf("TO"), bkm["ap"][:, 0:256], OP.add),
                            waits=[tm2, tt2_], inc="sdve",
                            reads=[KP("TO")], writes=[KP("O")])
                bkm["rtok"] = t_O
                yield
                bksn = fb.get()
                tn1 = em.op("pe", mm(bksn["ap"][:, 0:128], pr("Khat", 0), pr(ycur, 0)),
                            waits=[bksn["rtok"], tKh[0], t_U], inc="spe",
                            reads=[KP("Khat"), Ukey])
                tn2 = em.op("pe", mm(bksn["ap"][:, 128:256], pr("Khat", 1), pr(ycur, 1)),
                            waits=[tKh[1]], inc="spe", reads=[KP("Khat"), Ukey])
                ts1_ = em.op("dve", stt(ssl(hs[0]), ssl(hs[0]), ccol(7, hs[0]),
                                        bksn["ap"][:, 0:128], OP.mult, OP.add),
                             waits=[tn2, t_bre, tok_state_new[hs[0]],
                                    tok_state_cast[hs[0]]],
                             inc="sdve",
                             reads=[K_("ct", par, 7)], writes=[K_("stateh", hs[0])])
                ts2_ = em.op("dve", stt(ssl(hs[1]), ssl(hs[1]), ccol(7, hs[1]),
                                        bksn["ap"][:, 128:256], OP.mult, OP.add),
                             waits=[tok_state_new[hs[1]], tok_state_cast[hs[1]]],
                             inc="sdve",
                             reads=[K_("ct", par, 7)], writes=[K_("stateh", hs[1])])
                bksn["rtok"] = ts2_
                tok_state_new[hs[0]] = ts1_
                tok_state_new[hs[1]] = ts2_
                tcast = em.op("act", act(statebf[:, h0 * 128:(h0 + 2) * 128],
                                         state[:, h0 * 128:(h0 + 2) * 128], AF.Copy),
                              waits=[ts1_, ts2_], inc="sact",
                              reads=[K_("stateh", hs[0]), K_("stateh", hs[1])],
                              writes=[K_("statebf", hs[0]), K_("statebf", hs[1])])
                tok_state_cast[hs[0]] = tcast
                tok_state_cast[hs[1]] = tcast
                yield
                t_osq0 = em.op("act", act(prf("TO"), prf("O"), AF.Square),
                               waits=[t_O, tt2_], inc="sact",
                               reads=[KP("O")], writes=[KP("TO")])
                tvr = []
                for s in (0, 1):
                    tvr.append(em.op("dve",
                                     lambda v, _o=prf("TO", s),
                                     _d=varcol[:, 2 * ip + s:2 * ip + s + 1]:
                                     v.reduce_sum(_d, _o, axis=mybir.AxisListType.X),
                                     waits=[t_osq0], inc="sdve",
                                     reads=[KP("TO")], writes=[K_("var", ip, s)]))
                em.op("act", act(rstdcol[:, 2 * ip:2 * ip + 2],
                                 varcol[:, 2 * ip:2 * ip + 2], AF.Ln,
                                 scale=1.0 / 128.0, bias=cf["cvec"][:, 1:2]),
                      waits=tvr, inc="sact",
                      reads=[K_("var", ip, 0), K_("var", ip, 1)],
                      writes=[K_("rstd", ip)])
                t_rs = em.op("act", act(rstdcol[:, 2 * ip:2 * ip + 2],
                                        rstdcol[:, 2 * ip:2 * ip + 2],
                                        AF.Exp, scale=-0.5), inc="sact",
                             writes=[K_("rstd", ip)])
                tc1 = em.op("dve", ts(prf("CN", 0), prf("O", 0),
                                      rstdcol[:, 2 * ip:2 * ip + 1], OP.mult),
                            waits=[t_rs], inc="sdve",
                            reads=[KP("O"), K_("rstd", ip)], writes=[KP("CN")])
                tc2 = em.op("dve", ts(prf("CN", 1), prf("O", 1),
                                      rstdcol[:, 2 * ip + 1:2 * ip + 2], OP.mult),
                            inc="sdve", reads=[KP("O"), K_("rstd", ip)],
                            writes=[KP("CN")])
                yield
                bkct = fb.get()
                tct1 = em.op("pe", trans(bkct["ap"][:, 0:128], prf("CN", 0),
                                         cf["identf"][:]),
                             waits=[bkct["rtok"], tc1, tc2], inc="spe", reads=[KP("CN")])
                tct2 = em.op("pe", trans(bkct["ap"][:, 128:256], prf("CN", 1),
                                         cf["identf"][:]), inc="spe", reads=[KP("CN")])
                tg1 = em.op("dve", tt(lnsl(hs[0], c0, 128), bkct["ap"][:, 0:128],
                                      zsl(hs[0], c0, 128), OP.mult),
                            waits=[tct2, tok_zgw[hs[0]]], inc="sdve",
                            reads=[K_("zg", hs[0])], writes=[K_("ln", hs[0])])
                tg2 = em.op("dve", tt(lnsl(hs[1], c0, 128), bkct["ap"][:, 128:256],
                                      zsl(hs[1], c0, 128), OP.mult),
                            waits=[tok_zgw[hs[1]]], inc="sdve",
                            reads=[K_("zg", hs[1])], writes=[K_("ln", hs[1])])
                bkct["rtok"] = tg2
                inst_done.add(gp)

            inst_done = set()
            gens = deque()
            if tile == 0:
                gens.append(prelude_tail_gen())
            for c in range(NCH):
                gens.append(prep_gen(c))
                for kp in range(HK):
                    gens.append(pair_gen(c, kp))
            active = deque()
            while gens or active:
                while len(active) < _SCAN_DEPTH and gens:
                    active.append(gens.popleft())
                g0 = active.popleft()
                try:
                    next(g0)
                    active.append(g0)
                except StopIteration:
                    pass
            tok_core = ("sdve", em.cnt["sdve"])
            dump("core", lnbuf[:], [tok_core], keys=[K_("ln", i) for i in range(NCI)])

            tok_attn = []
            for co in range(N_WO):
                par = wload1(sc_wo.ap()[co * 128:(co + 1) * 128, :],
                             waits_extra=[pre_tok["wo"]])
                bk = fb.get()
                for ci in range(NCI):
                    w = [bk["rtok"], wld_tok[par], tok_core] if ci == 0 else []
                    em.op("pe", mm(bk["ap"], wsl1(par, ci * 128, 128), lnsl(ci),
                                   start=(ci == 0), stop=(ci == NCI - 1)),
                          waits=w, inc="spe" if ci == NCI - 1 else None)
                tok_mm = ("spe", em.cnt["spe"])
                wslot_tok[par] = tok_mm
                for ci in range(NCI):
                    em.rd(K_("ln", ci), tok_mm)
                t_ev = em.op("act", act(qsl(co), bk["ap"], AF.Identity,
                                        bias=smalls["bout_s"][:, co:co + 1]),
                             waits=[tok_mm], inc="sact", writes=[K_("qkv", co)])
                bk["rtok"] = t_ev
                tok_attn.append(t_ev)
            dump("attn", qkvbuf[:, 0:NCI * TT], [tok_attn[-1]], keys=[K_("qkv", i) for i in range(NCI)])

            def ln2_writer(ci, apR, apM, tR, tM, _ta=tok_attn):
                em.op("dve", tt(tmpA[:], qsl(ci), apR, OP.mult),
                      waits=[tR, _ta[ci]], inc="sdve",
                      reads=[K_("qkv", ci)], writes=[K_("tmpA",)])
                em.op("dve", tt(tmpA[:], tmpA[:], apM, OP.subtract), waits=[tM], inc="sdve",
                      writes=[K_("tmpA",)])
                return em.op("dve", tt(xsl(ci), xsl(ci), tmpA[:], OP.add), inc="sdve",
                             reads=[K_("tmpA",)], writes=[K_("x", ci)])
            tok_x2 = emit_ln("bf16", qsl, lambda ci: K_("qkv", ci), tok_attn, ln2_writer)
            dump("x2", xbuf[:], [tok_x2[-1]], keys=[K_("x", i) for i in range(NCI)])

            def ln3_writer(ci, apR, apM, tR, tM, _tx=tok_x2):
                em.op("dve", tt(tmpA[:], xsl(ci), apR, OP.mult),
                      waits=[tR, _tx[ci]], inc="sdve",
                      reads=[K_("x", ci)], writes=[K_("tmpA",)])
                return em.op("dve", tt(lnsl(ci), tmpA[:], apM, OP.subtract),
                             waits=[tM], inc="sdve",
                             reads=[K_("tmpA",)], writes=[K_("ln", ci)])
            tok_ln3 = emit_ln("f32", xsl, lambda ci: K_("x", ci), tok_x2, ln3_writer)
            dump("ln3", lnbuf[:], [tok_ln3[-1]], keys=[K_("ln", i) for i in range(NCI)])

            def hkey(j):
                return K_("qkv", j) if j < 32 else K_("mixd", j - 32)

            def hslot(j, c0=0, w=TT):
                return qsl(j, c0, w) if j < 32 else mdata(j - 32, c0, w)

            tok_h = []
            for co in range(N_GU):
                p2 = 2 * (wpair_i[0] % 2)
                wpair_i[0] += 1
                t_gld = em.op("sync",
                              dma(wstream[:, p2 * 2048:(p2 + 1) * 2048],
                                  sc_g.ap()[co * 128:(co + 1) * 128, :]),
                              waits=[wslot_tok[p2], pre_tok["g"], pre_tok["u"]],
                              inc="swd%d" % p2, inc_n=16)
                t_uld = em.op("sync",
                              dma(wstream[:, (p2 + 1) * 2048:(p2 + 2) * 2048],
                                  sc_u.ap()[co * 128:(co + 1) * 128, :]),
                              waits=[wslot_tok[p2 + 1]],
                              inc="swd%d" % (p2 + 1), inc_n=16)
                wld_tok[p2], wld_tok[p2 + 1] = t_gld, t_uld
                bkG = fb.get()
                for ci in range(NCI):
                    w = [bkG["rtok"], t_gld, tok_ln3[ci]] if ci == 0 else [tok_ln3[ci]]
                    em.op("pe", mm(bkG["ap"], wsl1(p2, ci * 128, 128), lnsl(ci),
                                   start=(ci == 0), stop=(ci == NCI - 1)),
                          waits=w, inc="spe" if ci == NCI - 1 else None)
                tok_mmg = ("spe", em.cnt["spe"])
                bkU = fb.get()
                for ci in range(NCI):
                    w = [bkU["rtok"], t_uld] if ci == 0 else []
                    em.op("pe", mm(bkU["ap"], wsl1(p2 + 1, ci * 128, 128), lnsl(ci),
                                   start=(ci == 0), stop=(ci == NCI - 1)),
                          waits=w, inc="spe" if ci == NCI - 1 else None)
                tok_mmu = ("spe", em.cnt["spe"])
                wslot_tok[p2] = wslot_tok[p2 + 1] = tok_mmu
                for ci in range(NCI):
                    em.rd(K_("ln", ci), tok_mmu)
                t_sg = em.op("act", act(tmpA[:], bkG["ap"], AF.Silu,
                                        scale=smalls["gs_s"][:, co:co + 1]),
                             waits=[tok_mmg], inc="sact", writes=[K_("tmpA",)])
                bkG["rtok"] = t_sg
                spar = co % 2
                t_su_ = em.op("act", act(sqtmp[:, spar * TT:(spar + 1) * TT], bkU["ap"],
                                         AF.Identity, scale=smalls["us_s"][:, co:co + 1]),
                              waits=[tok_mmu], inc="sact", writes=[K_("sq", spar)])
                bkU["rtok"] = t_su_
                t_h = em.op("dve", tt(hslot(co), tmpA[:],
                                      sqtmp[:, spar * TT:(spar + 1) * TT], OP.mult),
                            waits=[t_sg, t_su_], inc="sdve",
                            reads=[K_("tmpA",), K_("sq", spar)], writes=[hkey(co)])
                tok_h.append(t_h)
            if "hbuf" in dbg and tile == _DBG_TILE:
                dump("hbuf", qkvbuf[:, 0:8 * TT], [tok_h[7]],
                     keys=[hkey(j) for j in range(8)])

            tok_ffn = []
            for co in range(N_DN):
                p2 = 2 * (wpair_i[0] % 2)
                wpair_i[0] += 1
                base = p2 * 2048
                ld1 = em.op("sync",
                            dma(wstream[:, base:base + 32 * 128],
                                sc_d.ap()[co * 128:(co + 1) * 128, 0:32 * 128]),
                            waits=[wslot_tok[p2], wslot_tok[p2 + 1], pre_tok["d"]],
                            inc="swd%d" % p2, inc_n=16)
                wld_tok[p2] = ld1
                ld2holder = []
                for ci in range(N_GU):
                    w = []
                    if ci == 0:
                        bk = fb.get()
                        w = [bk["rtok"], ld1] + tok_h
                    elif ci == 32:
                        w = [ld2holder[0]]
                    em.op("pe", mm(bk["ap"], wsl1(p2, (ci if ci < 32 else ci - 32) * 128, 128),
                                   hslot(ci),
                                   start=(ci == 0), stop=(ci == N_GU - 1)),
                          waits=w, inc="spe" if ci in (11, N_GU - 1) else None)
                    if ci == 11:
                        t_mm11 = ("spe", em.cnt["spe"])
                        ld2 = em.op("sync",
                                    dma(wstream[:, base:base + 12 * 128],
                                        sc_d.ap()[co * 128:(co + 1) * 128, 32 * 128:44 * 128]),
                                    waits=[t_mm11], inc="swd%d" % p2, inc_n=16)
                        wld_tok[p2] = ld2
                        ld2holder.append(ld2)
                tok_mm = ("spe", em.cnt["spe"])
                wslot_tok[p2] = wslot_tok[p2 + 1] = tok_mm
                for ci in range(N_GU):
                    em.rd(hkey(ci), tok_mm)
                t_ev = em.op("act", act(mdata(12 + co), bk["ap"], AF.Identity,
                                        scale=smalls["ds_s"][:, co:co + 1]),
                             waits=[tok_mm], inc="sact", writes=[K_("mixd", 12 + co)])
                bk["rtok"] = t_ev
                tok_ffn.append(t_ev)

            def ln4_writer(ci, apR, apM, tR, tM, _tf=tok_ffn):
                em.op("dve", tt(tmpA[:], mdata(12 + ci), apR, OP.mult),
                      waits=[tR, _tf[ci]], inc="sdve",
                      reads=[K_("mixd", 12 + ci)], writes=[K_("tmpA",)])
                em.op("dve", tt(tmpA[:], tmpA[:], apM, OP.subtract), waits=[tM],
                      inc="sdve", writes=[K_("tmpA",)])
                return em.op("dve", tt(xsl(ci), xsl(ci), tmpA[:], OP.add), inc="sdve",
                             reads=[K_("tmpA",)], writes=[K_("x", ci)])

            tok_res4 = emit_ln("bf16", lambda ci: mdata(12 + ci),
                               lambda ci: K_("mixd", 12 + ci), tok_ffn, ln4_writer)
            halves = [(0, "sod"), (8, "sodb")]
            half_store_tok = []
            for (c0h, s_st) in halves:
                t_fl = em.op("dve", lambda v: v.drain(),
                             waits=[tok_res4[c0h + 7]], inc="sdve")
                for ci in range(c0h, c0h + 8):
                    em.op("act",
                          dma(yout[b].ap()[ci * 128:(ci + 1) * 128,
                                           tok0:tok0 + TT], xsl(ci)),
                          waits=[t_fl], inc=s_st, inc_n=16)
                    em.rd(K_("x", ci), (s_st, em.cnt[s_st]))
                half_store_tok.append((s_st, em.cnt[s_st]))
            nxt = tile + 1
            if nxt < min(NTILE, _MAXTILE):
                bn = nxt // 4
                tok0n = (nxt % 4) * TT
                pending_x = []
                for hi, (c0h, s_ld) in enumerate([(0, "sxd"), (8, "sxdb")]):
                    for ci in range(c0h, c0h + 8):
                        w = [half_store_tok[hi]] if ci == c0h else []
                        em.op("pool",
                              dma(xsl(ci), xin[bn].ap()[ci * 128:(ci + 1) * 128,
                                                        tok0n:tok0n + TT]),
                              waits=w, inc=s_ld, inc_n=16, writes=[K_("x", ci)])
                    t_xfh = em.op("pool", None, waits=[(s_ld, em.cnt[s_ld])],
                                  inc="spool")
                    for ci in range(c0h, c0h + 8):
                        em.war[K_("x", ci)]["w"] = t_xfh
                    pending_x += [t_xfh] * 8
            else:
                t_of = em.op("act", None,
                             waits=[half_store_tok[0], half_store_tok[1]],
                             inc="sact")
                for ci in range(NCI):
                    em.rd(K_("x", ci), t_of)

        em.op("pool", None, waits=[("sod", em.cnt["sod"]), ("sodb", em.cnt["sodb"]),
                                   ("sxd", em.cnt["sxd"]), ("sxdb", em.cnt["sxdb"]),
                                   ("swd0", em.cnt["swd0"]), ("swd1", em.cnt["swd1"]),
                                   ("swd2", em.cnt["swd2"]), ("swd3", em.cnt["swd3"]),
                                   ("sdbg", em.cnt["sdbg"]),
                                   ("spd0", em.cnt["spd0"]), ("spd1", em.cnt["spd1"]),
                                   ("spl0", em.cnt["spl0"]), ("spl1", em.cnt["spl1"])])
        em.emit(block, sems)

    return nc


def _host_stage(inputs):
    f32 = np.float32
    x = inputs["L0_attn_hidden_states"]
    common = {}
    common["wqkvT"] = np.ascontiguousarray(np.asarray(inputs["L0_attn_Wqkv"]).T.astype(f32))
    common["wzT"] = np.ascontiguousarray(np.asarray(inputs["L0_attn_Wz"]).T.astype(f32))
    wab = np.zeros((D, 32), f32)
    wab[:, 0:16] = np.asarray(inputs["L0_attn_Wa"]).T
    wab[:, 16:32] = np.asarray(inputs["L0_attn_Wb"]).T
    common["wabT"] = wab
    common["woutT"] = np.ascontiguousarray(np.asarray(inputs["L0_attn_Wout"]).T.astype(f32))
    common["wgT"] = np.ascontiguousarray(np.asarray(inputs["L0_ffn_gate_w"]).T.astype(f32))
    common["wuT"] = np.ascontiguousarray(np.asarray(inputs["L0_ffn_up_w"]).T.astype(f32))
    common["wdT"] = np.ascontiguousarray(np.asarray(inputs["L0_ffn_down_w"]).T.astype(f32))
    common["bqkv_s"] = np.ascontiguousarray(
        np.asarray(inputs["L0_attn_bqkv"]).reshape(32, 128).T.astype(f32))
    common["bz_s"] = np.ascontiguousarray(
        np.asarray(inputs["L0_attn_bz"]).reshape(16, 128).T.astype(f32))
    common["bout_s"] = np.ascontiguousarray(
        np.asarray(inputs["L0_attn_bout"]).reshape(16, 128).T.astype(f32))
    common["convw_s"] = np.ascontiguousarray(
        np.asarray(inputs["L0_attn_conv_w"]).reshape(32, 128, 4)
        .transpose(1, 0, 2).reshape(128, 128).astype(f32))
    common["convb_s"] = np.ascontiguousarray(
        np.asarray(inputs["L0_attn_conv_b"]).reshape(32, 128).T.astype(f32))
    common["gs_s"] = np.ascontiguousarray(
        np.asarray(inputs["L0_ffn_gate_s"]).reshape(44, 128).T.astype(f32))
    common["us_s"] = np.ascontiguousarray(
        np.asarray(inputs["L0_ffn_up_s"]).reshape(44, 128).T.astype(f32))
    common["ds_s"] = np.ascontiguousarray(
        np.asarray(inputs["L0_ffn_down_s"]).reshape(16, 128).T.astype(f32))
    common["normw_s"] = np.ascontiguousarray(np.asarray(inputs["L0_attn_norm_w"]).astype(f32)[:, None])
    common["ba_s"] = np.asarray(inputs["L0_attn_ba"]).astype(f32)[:, None]
    common["bb_s"] = np.asarray(inputs["L0_attn_bb"]).astype(f32)[:, None]
    common["dtb_s"] = np.ascontiguousarray(np.asarray(inputs["L0_attn_dt_bias"]).astype(f32)[:, None])
    common["alog_s"] = np.ascontiguousarray(np.asarray(inputs["L0_attn_A_log"]).astype(f32)[:, None])
    common["identf"] = np.eye(128, dtype=f32)
    common["maskUf"] = np.triu(np.ones((128, 128), f32))
    common["ones16f"] = np.ones((16, 128), f32)
    common["ident16f"] = np.eye(16, dtype=f32)
    common["onesrow_f"] = np.ones((1, 128), f32)
    common["onescol_f"] = np.ones((128, 1), f32)
    cvec = np.zeros((128, 8), f32)
    cvec[:, 0] = LNEPS
    cvec[:, 1] = 1e-6
    cvec[:, 2] = 1.0
    cvec[:, 3] = -0.5 * np.log(128.0)
    common["cvec"] = cvec
    common["identb"] = np.eye(128).astype(BF)
    common["maskSnb"] = (-np.triu(np.ones((128, 128), f32), 1)).astype(BF)
    common["maskIb"] = np.triu(np.ones((128, 128), f32)).astype(BF)
    common["onescol_b"] = np.ones((128, 1), f32).astype(BF)
    in_maps = []
    for core in range(8):
        m = dict(common)
        for bb_ in range(2):
            m[f"x{bb_}"] = np.ascontiguousarray(np.asarray(x[core * 2 + bb_]).T.astype(f32))
        in_maps.append(m)
    return in_maps


def kernel(**inputs):
    in_maps = _host_stage(inputs)
    nc = build_program(dbg=_DBG)
    res = run_bass_kernel_spmd(nc, in_maps, core_ids=list(range(8)))
    out = np.empty((16, 2048, D), np.float32)
    for core in range(8):
        for bb_ in range(2):
            out[core * 2 + bb_] = res.results[core][f"y{bb_}"].T
    return out
